# version 02
# baseline (speedup 1.0000x reference)
import numpy as np
import ml_dtypes
from contextlib import ExitStack

import concourse.bass as bass
import concourse.mybir as mybir
from concourse.bass_utils import run_bass_kernel_spmd

F32 = mybir.dt.float32
BF16 = mybir.dt.bfloat16
I8 = mybir.dt.int8
U8 = mybir.dt.uint8
AF = mybir.ActivationFunctionType
ALU = mybir.AluOpType
AXX = mybir.AxisListType.X

NCORES = 8
L = 2048
H = 2048
NH = 64
INTER = 4096
SEQ_PER_CORE = 2
SPAN = 512
NSPAN = SEQ_PER_CORE * L // SPAN
KC = H // 128
OCH = 68
CLAMP = 10.0

DBG = {}


def _f32(x):
    return np.ascontiguousarray(np.asarray(x, dtype=np.float32))


def prep_host(inputs):
    g = {}
    w_in = _f32(inputs["L0_attn_in_proj_w"])
    w_in_p = np.zeros((OCH * 128, H), np.float32)
    w_in_p[:8512] = w_in
    g["w_in"] = np.ascontiguousarray(
        w_in_p.T.reshape(KC, 128, 34, 256).transpose(2, 0, 1, 3))
    b_in = np.zeros((OCH * 128,), np.float32)
    b_in[:8512] = _f32(inputs["L0_attn_in_proj_b"])
    g["b_in"] = np.ascontiguousarray(b_in.reshape(OCH, 128).T)
    g["convw"] = np.ascontiguousarray(
        _f32(inputs["L0_attn_conv_w"]).reshape(34, 128, 4).transpose(1, 0, 2).reshape(128, 136))
    g["dtb"] = _f32(inputs["L0_attn_dt_bias"]).reshape(64, 1)
    g["alog"] = _f32(inputs["L0_attn_A_log"]).reshape(64, 1)
    g["d_row"] = _f32(inputs["L0_attn_D"]).reshape(1, 64)
    g["nw"] = np.ascontiguousarray(_f32(inputs["L0_attn_norm_w"]).reshape(32, 128).T)
    g["w_out"] = np.ascontiguousarray(
        _f32(inputs["L0_attn_out_proj_w"]).T.reshape(32, 128, 16, 128).transpose(2, 0, 1, 3))
    g["b_out"] = np.ascontiguousarray(_f32(inputs["L0_attn_out_proj_b"]).reshape(16, 128).T)
    g["w_g"] = np.ascontiguousarray(
        np.asarray(inputs["L0_ffn_w_gate"]).reshape(KC, 128, 32, 256).transpose(2, 0, 1, 3))
    g["w_u"] = np.ascontiguousarray(
        np.asarray(inputs["L0_ffn_w_up"]).reshape(KC, 128, 32, 256).transpose(2, 0, 1, 3))
    g["w_d"] = np.ascontiguousarray(
        np.asarray(inputs["L0_ffn_w_down"]).reshape(4, 16, 128, 8, 256).transpose(3, 0, 1, 2, 4))
    g["s_g"] = np.ascontiguousarray(_f32(inputs["L0_ffn_s_gate"]).reshape(64, 128).T)
    g["s_u"] = np.ascontiguousarray(_f32(inputs["L0_ffn_s_up"]).reshape(64, 128).T)
    g["s_d"] = np.ascontiguousarray(_f32(inputs["L0_ffn_s_down"]).reshape(16, 128).T)
    g["ident_b"] = np.eye(128, dtype=ml_dtypes.bfloat16)
    g["ident_f"] = np.eye(128, dtype=np.float32)
    g["onesr"] = np.ones((1, 128), np.float32)
    g["onescol"] = np.ones((128, 1), np.float32)
    s = np.arange(128)[:, None]
    t = np.arange(256)[None, :]
    g["mask0"] = (t >= s).astype(np.float32)
    cv = np.zeros((128, 2), np.float32)
    cv[:, 0] = 1e-5
    cv[:, 1] = 1.0
    g["cvals"] = cv
    return g


def build_program():
    nc = bass.Bass("TRN2", target_bir_lowering=False, debug=False)

    dx = nc.dram_tensor("x", [SEQ_PER_CORE * L, H], F32, kind="ExternalInput").ap()
    dout = nc.dram_tensor("out", [SEQ_PER_CORE * L, H], F32, kind="ExternalOutput").ap()
    dr = {}
    for n, shape, dt in [
        ("w_in", [34, KC, 128, 256], F32), ("b_in", [128, OCH], F32),
        ("convw", [128, 136], F32), ("dtb", [64, 1], F32), ("alog", [64, 1], F32),
        ("d_row", [1, 64], F32), ("nw", [128, 32], F32),
        ("w_out", [16, 32, 128, 128], F32), ("b_out", [128, 16], F32),
        ("w_g", [32, KC, 128, 256], I8), ("w_u", [32, KC, 128, 256], I8),
        ("w_d", [8, 4, 16, 128, 256], I8),
        ("s_g", [128, 64], F32), ("s_u", [128, 64], F32), ("s_d", [128, 16], F32),
        ("ident_b", [128, 128], BF16), ("ident_f", [128, 128], F32),
        ("onesr", [1, 128], F32), ("onescol", [128, 1], F32),
        ("mask0", [128, 256], F32), ("cvals", [128, 2], F32),
    ]:
        dr[n] = nc.dram_tensor(n, shape, dt, kind="ExternalInput").ap()
    ddbg = {n: nc.dram_tensor("dbg_" + n, list(s), F32, kind="ExternalOutput").ap()
            for n, s in DBG.items()}

    ctx = ExitStack()
    sb = lambda n, s, d: ctx.enter_context(nc.sbuf_tensor(n, s, d))
    psb = lambda n, s, d: ctx.enter_context(nc.psum_tensor(n, s, d))

    C = {}
    for n, shape, dt in [
        ("identb", [128, 128], BF16), ("identf", [128, 128], F32),
        ("onesr", [1, 128], F32), ("onescol", [128, 1], F32),
        ("mask0", [128, 256], F32), ("b_in", [128, OCH], F32),
        ("convw", [128, 136], F32), ("dtb", [64, 1], F32), ("alog", [64, 1], F32),
        ("d_row", [1, 64], F32), ("nw", [128, 32], F32), ("bout", [128, 16], F32),
        ("sg", [128, 64], F32), ("su", [128, 64], F32), ("sd", [128, 16], F32),
        ("acol", [64, 1], F32), ("bias66", [64, 1], F32),
        ("DB", [128, 64], F32), ("cvals", [128, 2], F32),
    ]:
        C[n] = sb("c_" + n, shape, dt)

    xn_sb = sb("xn", [128, KC * SPAN], BF16)
    gate_sb = sb("gate", [128, 32 * SPAN], BF16)
    xtm_sb = sb("xtm", [128, 4 * INTER], BF16)
    out_sb = sb("outtm", [128, 4 * H], F32)
    rbf_sb = sb("rbf", [128, NH * 64], BF16)
    arena = sb("arena", [128, 32768], U8)
    arf = arena[:].bitcast(F32)
    arb = arena[:].bitcast(BF16)
    lnconv = sb("lnconv", [128, 6208], U8)
    lcf = lnconv[:].bitcast(F32)
    ssb = sb("ssb", [128, 2 * 384], BF16)
    ybuf = sb("ybuf", [128, 2 * 256], BF16)
    xpb = sb("xpb", [128, 4 * 64], BF16)
    bc_sb = sb("bcfm", [128, 2 * SPAN], BF16)
    btm_sb = sb("btm", [128, 4 * 128], BF16)
    dtraw_sb = sb("dtraw", [64, SPAN], F32)
    halo_sb = sb("halo", [128, 34 * 3], F32)
    ffp_sb = sb("ffp", [128, 2 * 256], F32)
    tsil_sb = sb("tsil", [128, 2 * 256], F32)
    lnsm = sb("lnsm", [128, 16], F32)
    wsA = sb("wsA", [128, 16384], U8)
    wsB = sb("wsB", [128, 16384], U8)
    wsAb = wsA[:].bitcast(BF16)
    wsBb = wsB[:].bitcast(BF16)

    A_GM0, A_GM1 = 0, 256
    A_ARG, A_E = 384, 1152
    A_G2, A_RSTD = 1920, 2432
    A_DT = 2944
    A_CATM = 4480
    A_TOFF = 5056
    A_CAR = 6144

    rstdrow_sb = arf[:, 5120:5376]
    etrow_sb = arf[:, 5376:5440]
    etcol_sb = arf[0:64, 5440:5441]

    def fuse_t(i):
        return arb[:, i * 256:(i + 1) * 256]

    print("sbuf bytes remaining:", nc.sbuf_bytes_remaining)

    pb = [psb(f"pb{i}", [128, 512], F32) for i in range(6)]
    pb6 = psb("pb6", [128, 1024], BF16)
    pb7 = psb("pb7", [128, 512], F32)

    sems, cnt = {}, {}
    for name in ["ldx", "ldc", "ldcr0", "ldcr1", "st", "dbg",
                 "win0", "win1", "wout0", "wout1", "wgu0", "wgu1", "wd0", "wd1",
                 "pe", "dve", "act"]:
        sems[name] = ctx.enter_context(nc.semaphore(name))
        cnt[name] = 0

    ops = []

    SELF_CH = {"vector": "dve", "scalar": "act"}

    def emit(en, waits, fn, inc=None, inc_by=1):
        if en in SELF_CH and cnt[SELF_CH[en]] > 0:
            waits = list(waits) + [(SELF_CH[en], cnt[SELF_CH[en]])]
        if cnt["dbg"] > 0:
            waits = list(waits) + [("dbg", cnt["dbg"])]
        for c, v in waits:
            assert v <= cnt[c], f"wait on future event {c}:{v} > {cnt[c]}"
        if inc is not None:
            cnt[inc] += inc_by
        w = [(sems[c], v) for c, v in waits if v > 0]
        s_inc = (sems[inc], inc_by) if inc else None
        def f(eng, w=w, s_inc=s_inc, fn=fn):
            for sh, v in w:
                eng.wait_ge(sh, v)
            instr = fn(eng)
            if s_inc is not None:
                instr.then_inc(s_inc[0], s_inc[1])
        ops.append((en, f))

    def now(chn):
        return cnt[chn]

    class Ring:
        def __init__(self, ld0, ld1, depth=2):
            self.ld = [ld0, ld1]
            self.depth = depth
            self.nload = 0
            self.freed_pe = []
        def load(self, fn):
            k = self.nload
            waits = []
            j = k - self.depth
            if j >= 0:
                assert j < len(self.freed_pe), "block consumed before reuse"
                waits.append(("pe", self.freed_pe[j]))
            slot = k % self.depth
            emit("gpsimd", waits, lambda e, slot=slot: fn(e, slot),
                 inc=self.ld[slot], inc_by=16)
            self.nload += 1
            return k
        def use_wait(self, k):
            return (self.ld[k % 2], (k // 2 + 1) * 16)
        def free(self, pe_val):
            self.freed_pe.append(pe_val)

    r_win = Ring("win0", "win1")
    r_wout = Ring("wout0", "wout1")
    r_wgu = Ring("wgu0", "wgu1")
    r_wd = Ring("wd0", "wd1")

    class TBank:
        def __init__(self, nmax):
            self.last_drain = 0
            self.nmax = nmax

    tb6 = TBank(8)
    tb7 = TBank(7)

    def transpose_batch(bank, items, extra_pe_waits=()):
        assert 0 < len(items) <= bank.nmax
        waits = list(extra_pe_waits)
        if bank.last_drain > 0:
            waits.append(("dve", bank.last_drain))
        def fill(e, items=items):
            last = None
            for slot, inap, idap, outap, evf in items:
                last = e.transpose(slot, inap, idap)
            return last
        emit("tensor", waits, fill, inc="pe")
        fill_pe = now("pe")
        def drain(e, items=items, fill_pe=fill_pe):
            e.wait_ge(sems["pe"], fill_pe)
            last = None
            for slot, inap, idap, outap, evf in items:
                if evf is None:
                    last = e.tensor_copy(outap, slot)
                else:
                    last = evf(e, slot)
            return last
        emit("vector", [], drain, inc="dve")
        bank.last_drain = now("dve")
        return fill_pe, bank.last_drain

    def dbg_tap(name, src_ap):
        if name in DBG:
            emit("gpsimd", [("pe", now("pe")), ("dve", now("dve")), ("act", now("act"))],
                 lambda e, s=src_ap, d=ddbg[name]: e.dma_start(out=d, in_=s),
                 inc="dbg", inc_by=16)

    const_pairs = [
        (C["identb"], dr["ident_b"]), (C["identf"], dr["ident_f"]),
        (C["onesr"], dr["onesr"]), (C["onescol"], dr["onescol"]),
        (C["mask0"], dr["mask0"]), (C["b_in"], dr["b_in"]),
        (C["convw"], dr["convw"]), (C["dtb"], dr["dtb"]), (C["alog"], dr["alog"]),
        (C["d_row"], dr["d_row"]), (C["nw"], dr["nw"]), (C["bout"], dr["b_out"]),
        (C["sg"], dr["s_g"]), (C["su"], dr["s_u"]), (C["sd"], dr["s_d"]),
        (C["cvals"], dr["cvals"]),
    ]
    for dst, src in const_pairs:
        emit("sync", [], lambda e, d=dst, s=src: e.dma_start(out=d[:], in_=s),
             inc="ldc", inc_by=16)
    NCONST = now("ldc")
    emit("vector", [], lambda e: e.memset(rbf_sb[:], 0.0), inc="dve")
    emit("vector", [], lambda e: e.memset(halo_sb[:], 0.0), inc="dve")
    emit("scalar", [("ldc", NCONST)],
         lambda e: e.activation(C["acol"][:], C["alog"][:], AF.Exp), inc="act")
    emit("vector", [("act", now("act"))],
         lambda e: e.tensor_scalar(C["acol"][:], C["acol"][:], -1.0, None, ALU.mult),
         inc="dve")
    emit("vector", [("ldc", NCONST)],
         lambda e: e.tensor_tensor(C["bias66"][:], C["b_in"][0:64, 66:67], C["dtb"][:],
                                   ALU.add), inc="dve")
    emit("tensor", [("ldc", NCONST)],
         lambda e: e.matmul(pb7[:, 0:64], C["onesr"][:], C["d_row"][:],
                            start=True, stop=True), inc="pe")
    emit("vector", [("pe", now("pe"))],
         lambda e: e.tensor_copy(C["DB"][:], pb7[:, 0:64]), inc="dve")
    tb7.last_drain = now("dve")
    INIT_DVE = now("dve")
    INIT_ACT = now("act")

    rupd_ev = [INIT_DVE] * NH
    car_k = 0
    car_free_pe = []
    S = {"outproj_pe": 0, "ssm_pe": 0, "ffn_pe": 0, "arena_dve": 0,
         "pb01_dve": 0, "pb23_act": 0, "pb23_dve": 0, "pb4_dve": 0, "pb5_dve": 0,
         "hb2": (0, 0), "hb3": (0, 0)}

    for span in range(NSPAN):
        row0 = (span // (L // SPAN)) * L + (span % (L // SPAN)) * SPAN
        seq_start = (span % (L // SPAN)) == 0

        st_before = now("st")
        for tq in range(4):
            emit("sync", [("st", st_before)],
                 lambda e, tq=tq, row0=row0: e.dma_start(
                     out=out_sb[:, tq * H:(tq + 1) * H],
                     in_=dx[row0 + tq * 128: row0 + (tq + 1) * 128, :]),
                 inc="ldx", inc_by=16)
        ldx_now = now("ldx")

        xn_guard = [("pe", S["ffn_pe"])] if span else []
        xn_act_ev = []
        xn_fill_pe = []
        for tq in range(4):
            xt = out_sb[:, tq * H:(tq + 1) * H]
            cb = (tq % 2) * 8
            w0 = [("ldx", ldx_now), ("dve", INIT_DVE)]
            if tq >= 2:
                w0.append(("act", xn_act_ev[tq - 2]))
            emit("vector", w0,
                 lambda e, xt=xt, cb=cb: e.reduce_sum(lnsm[:, cb:cb + 1], xt, axis=AXX),
                 inc="dve")
            sq_w = [("ldx", ldx_now), ("act", INIT_ACT)]
            if span:
                sq_w.append(("pe", S["ssm_pe"]))
                sq_w.append(("dve", S["arena_dve"]))
            emit("scalar", sq_w,
                 lambda e, xt=xt, cb=cb: e.activation(arf[:, 0:2048], xt, AF.Square,
                                                      accum_out=lnsm[:, cb + 1:cb + 2]),
                 inc="act")
            emit("vector", [("act", now("act"))],
                 lambda e, cb=cb: e.tensor_scalar(lnsm[:, cb + 2:cb + 3],
                                                  lnsm[:, cb:cb + 1], 1.0 / H, None,
                                                  ALU.mult), inc="dve")
            emit("vector", [],
                 lambda e, cb=cb: e.tensor_tensor(lnsm[:, cb + 3:cb + 4],
                                                  lnsm[:, cb + 2:cb + 3],
                                                  lnsm[:, cb + 2:cb + 3], ALU.mult),
                 inc="dve")
            emit("vector", [],
                 lambda e, cb=cb: e.scalar_tensor_tensor(
                     lnsm[:, cb + 4:cb + 5], lnsm[:, cb + 1:cb + 2], 1.0 / H,
                     lnsm[:, cb + 3:cb + 4], ALU.mult, ALU.subtract), inc="dve")
            emit("scalar", [("dve", now("dve"))],
                 lambda e, cb=cb: e.activation(lnsm[:, cb + 5:cb + 6],
                                               lnsm[:, cb + 4:cb + 5], AF.Ln,
                                               bias=C["cvals"][:, 0:1]), inc="act")
            emit("scalar", [],
                 lambda e, cb=cb: e.activation(lnsm[:, cb + 5:cb + 6],
                                               lnsm[:, cb + 5:cb + 6], AF.Exp,
                                               scale=-0.5), inc="act")
            emit("vector", [("act", now("act"))],
                 lambda e, cb=cb: e.tensor_tensor(lnsm[:, cb + 6:cb + 7],
                                                  lnsm[:, cb + 2:cb + 3],
                                                  lnsm[:, cb + 5:cb + 6], ALU.mult),
                 inc="dve")
            emit("vector", [],
                 lambda e, cb=cb: e.tensor_scalar(lnsm[:, cb + 6:cb + 7],
                                                  lnsm[:, cb + 6:cb + 7], -1.0, None,
                                                  ALU.mult), inc="dve")
            slot = tq % 2
            xnw = [("dve", now("dve"))]
            if tq >= 2:
                xnw.append(("pe", xn_fill_pe[tq - 2]))
            emit("scalar", xnw,
                 lambda e, xt=xt, slot=slot, cb=cb: e.activation(
                     arb[:, 4096 + slot * 2048: 4096 + (slot + 1) * 2048], xt, AF.Identity,
                     bias=lnsm[:, cb + 6:cb + 7], scale=lnsm[:, cb + 5:cb + 6]),
                 inc="act")
            xn_act_ev.append(now("act"))
            xa = now("act")
            fp_last = 0
            for half in range(2):
                items = []
                for j in range(8):
                    kc = half * 8 + j
                    items.append((
                        pb6[:, j * 128:(j + 1) * 128],
                        arb[:, 4096 + slot * 2048 + kc * 128: 4096 + slot * 2048 + (kc + 1) * 128],
                        C["identb"][:],
                        xn_sb[:, kc * SPAN + tq * 128: kc * SPAN + (tq + 1) * 128],
                        None))
                fp, _ = transpose_batch(tb6, items,
                                        extra_pe_waits=[("act", xa)] + xn_guard)
                fp_last = fp
            xn_fill_pe.append(fp_last)
        ln_done_dve = now("dve")
        if span == 0:
            dbg_tap("xn", xn_sb[:])

        gate_guard = [("pe", S["outproj_pe"])] if span else []
        pb01_guard = [("dve", S["pb01_dve"])] if span else []
        conv_dve_ev = [0] * 34
        silu_act_ev = [0] * 34
        xtr_fill_pe = [0] * 34
        bank_evac = {0: 0, 1: 0}
        dtraw_act = 0
        for og in range(34):
            k = r_win.load(lambda e, slot, og=og: e.dma_start(
                out=wsAb[:, slot * 4096: slot * 4096 + 4096],
                in_=dr["w_in"][og].rearrange("kc p o -> p kc o")))
            for oc in range(2):
                o = og * 2 + oc
                if o == 67:
                    continue
                bank = o % 2
                pw = [r_win.use_wait(k), ("dve", ln_done_dve)]
                if bank_evac[bank]:
                    pw.append(("act", bank_evac[bank]))
                else:
                    pw += pb01_guard
                bufbase = (k % 2) * 4096
                def mms(e, bufbase=bufbase, oc=oc, bank=bank):
                    last = None
                    for cl in range(2):
                        for kc in range(KC):
                            last = e.matmul(
                                pb[bank][:, cl * 256:(cl + 1) * 256],
                                wsAb[:, bufbase + kc * 256 + oc * 128:
                                     bufbase + kc * 256 + oc * 128 + 128],
                                xn_sb[:, kc * SPAN + cl * 256: kc * SPAN + (cl + 1) * 256],
                                start=(cl == 0 and kc == 0),
                                stop=(cl == 1 and kc == KC - 1),
                                skip_group_check=True)
                    return last
                emit("tensor", pw, mms, inc="pe")
                if oc == 1 or o == 66:
                    r_win.free(now("pe"))
                mmw = now("pe")
                if o < 32:
                    fc = o
                    emit("scalar", [("pe", mmw)] + gate_guard,
                         lambda e, fc=fc, bank=bank: e.activation(
                             gate_sb[:, fc * SPAN:(fc + 1) * SPAN], pb[bank][:],
                             AF.Silu, bias=C["b_in"][:, fc:fc + 1]), inc="act")
                elif o < 66:
                    fc = o - 32
                    ring = fc % 3
                    hbct = lcf[:, ring * 516:(ring + 1) * 516]
                    gw = [("pe", mmw)]
                    if fc >= 3:
                        gw.append(("dve", conv_dve_ev[fc - 3]))
                    emit("scalar", gw,
                         lambda e, hbct=hbct, bank=bank, o=o: e.activation(
                             hbct[:, 3:515], pb[bank][:], AF.Identity,
                             bias=C["b_in"][:, o:o + 1]), inc="act")
                    hev = now("act")
                    def halo_fill(e, hbct=hbct, fc=fc, seq_start=seq_start):
                        if seq_start:
                            return e.memset(hbct[:, 0:3], 0.0)
                        return e.tensor_copy(hbct[:, 0:3], halo_sb[:, fc * 3:fc * 3 + 3])
                    emit("vector", [("act", hev)], halo_fill, inc="dve")
                    acc = arf[:, 4096 + (fc % 2) * 512: 4096 + (fc % 2) * 512 + 512]
                    accw = [("act", silu_act_ev[fc - 2])] if fc >= 2 else []
                    emit("vector", accw,
                         lambda e, hbct=hbct, acc=acc, fc=fc: e.tensor_scalar(
                             acc, hbct[:, 0:512], C["convw"][:, fc * 4:fc * 4 + 1],
                             None, ALU.mult), inc="dve")
                    for j in range(1, 4):
                        emit("vector", [],
                             lambda e, hbct=hbct, acc=acc, fc=fc, j=j: e.scalar_tensor_tensor(
                                 acc, hbct[:, j:512 + j],
                                 C["convw"][:, fc * 4 + j:fc * 4 + j + 1],
                                 acc, ALU.mult, ALU.add), inc="dve")
                    emit("vector", [],
                         lambda e, hbct=hbct, fc=fc: e.tensor_copy(
                             halo_sb[:, fc * 3:fc * 3 + 3], hbct[:, 512:515]), inc="dve")
                    conv_dve_ev[fc] = now("dve")
                    if fc < 32:
                        dst = arb[:, 10240 + (fc % 2) * 512: 10240 + (fc % 2) * 512 + 512]
                    elif fc == 32:
                        dst = bc_sb[:, 0:512]
                    else:
                        dst = bc_sb[:, 512:1024]
                    sw = [("dve", now("dve"))]
                    if fc >= 2 and fc < 34 and fc - 2 < 32:
                        sw.append(("pe", xtr_fill_pe[fc - 2]))
                    emit("scalar", sw,
                         lambda e, acc=acc, dst=dst: e.activation(dst, acc, AF.Silu),
                         inc="act")
                    silu_act_ev[fc] = now("act")
                    if fc < 32:
                        items = []
                        for tt in range(4):
                            items.append((
                                pb6[:, tt * 128:(tt + 1) * 128],
                                arb[:, 10240 + (fc % 2) * 512 + tt * 128:
                                    10240 + (fc % 2) * 512 + (tt + 1) * 128],
                                C["identb"][:],
                                xtm_sb[:, tt * INTER + fc * 128: tt * INTER + (fc + 1) * 128],
                                None))
                        xg = [("act", silu_act_ev[fc])]
                        if span:
                            xg.append(("pe", S["ssm_pe"]))
                        fp, _ = transpose_batch(tb6, items, extra_pe_waits=xg)
                        xtr_fill_pe[fc] = fp
                    elif fc == 32:
                        items = []
                        for tt in range(4):
                            items.append((
                                pb6[:, tt * 128:(tt + 1) * 128],
                                bc_sb[:, tt * 128:(tt + 1) * 128],
                                C["identb"][:],
                                btm_sb[:, tt * 128:(tt + 1) * 128],
                                None))
                        bg = [("act", silu_act_ev[32])]
                        if span:
                            bg.append(("pe", S["ssm_pe"]))
                        transpose_batch(tb6, items, extra_pe_waits=bg)
                else:
                    emit("scalar", [("pe", mmw)],
                         lambda e, bank=bank: e.activation(
                             dtraw_sb[:, :], pb[bank][0:64, :], AF.Identity,
                             bias=C["bias66"][:]), inc="act")
                    dtraw_act = now("act")
                bank_evac[bank] = now("act")
        if span == 0:
            dbg_tap("gate", gate_sb[:])
            dbg_tap("xtm", xtm_sb[:])
            dbg_tap("bcfm", bc_sb[:])
            dbg_tap("dtraw", dtraw_sb[:])

        gu_bank_act = {2: S["pb23_act"], 3: S["pb23_act"], 4: 0, 5: 0}
        gu_bank_dve = {2: S["pb23_dve"], 3: S["pb23_dve"],
                       4: S["pb4_dve"], 5: S["pb5_dve"]}
        out_add_ev = [0, 0, 0, 0]
        outadd_guard_act = xn_act_ev[3]
        fuse_ev_slot = [0, 0]
        for cl in range(2):
            for ig in range(32):
                bg_, bu_ = (2, 3) if ig % 2 == 0 else (4, 5)
                kg = r_wgu.load(lambda e, slot, ig=ig: e.dma_start(
                    out=wsBb[:, slot * 4096: slot * 4096 + 4096],
                    in_=dr["w_g"][ig].rearrange("kc p o -> p kc o")))
                gw = [r_wgu.use_wait(kg)]
                if gu_bank_act[bg_]:
                    gw.append(("act", gu_bank_act[bg_]))
                if gu_bank_dve[bg_]:
                    gw.append(("dve", gu_bank_dve[bg_]))
                gbuf = (kg % 2) * 4096
                def gmms(e, gbuf=gbuf, cl=cl, bg_=bg_):
                    last = None
                    for isub in range(2):
                        for kc in range(KC):
                            last = e.matmul(
                                pb[bg_][:, isub * 256:(isub + 1) * 256],
                                wsBb[:, gbuf + kc * 256 + isub * 128:
                                     gbuf + kc * 256 + isub * 128 + 128],
                                xn_sb[:, kc * SPAN + cl * 256: kc * SPAN + (cl + 1) * 256],
                                start=(isub == 0 and kc == 0), stop=(isub == 1 and kc == KC - 1),
                                skip_group_check=True)
                    return last
                emit("tensor", gw, gmms, inc="pe")
                g_pe = now("pe")
                r_wgu.free(g_pe)
                ku = r_wgu.load(lambda e, slot, ig=ig: e.dma_start(
                    out=wsBb[:, slot * 4096: slot * 4096 + 4096],
                    in_=dr["w_u"][ig].rearrange("kc p o -> p kc o")))
                uw = [r_wgu.use_wait(ku)]
                if gu_bank_act[bu_]:
                    uw.append(("act", gu_bank_act[bu_]))
                if gu_bank_dve[bu_]:
                    uw.append(("dve", gu_bank_dve[bu_]))
                ubuf = (ku % 2) * 4096
                def umms(e, ubuf=ubuf, cl=cl, bu_=bu_):
                    last = None
                    for isub in range(2):
                        for kc in range(KC):
                            last = e.matmul(
                                pb[bu_][:, isub * 256:(isub + 1) * 256],
                                wsBb[:, ubuf + kc * 256 + isub * 128:
                                     ubuf + kc * 256 + isub * 128 + 128],
                                xn_sb[:, kc * SPAN + cl * 256: kc * SPAN + (cl + 1) * 256],
                                start=(isub == 0 and kc == 0), stop=(isub == 1 and kc == KC - 1),
                                skip_group_check=True)
                    return last
                emit("tensor", uw, umms, inc="pe")
                u_pe = now("pe")
                r_wgu.free(u_pe)
                for isub in range(2):
                    i = ig * 2 + isub
                    ts_slot = isub
                    tsw = [("pe", g_pe)]
                    if fuse_ev_slot[ts_slot]:
                        tsw.append(("dve", fuse_ev_slot[ts_slot]))
                    emit("scalar", tsw,
                         lambda e, bg_=bg_, isub=isub, i=i, ts_slot=ts_slot: e.activation(
                             tsil_sb[:, ts_slot * 256:(ts_slot + 1) * 256],
                             pb[bg_][:, isub * 256:(isub + 1) * 256],
                             AF.Silu, scale=C["sg"][:, i:i + 1]), inc="act")
                    emit("vector", [("act", now("act")), ("pe", u_pe)],
                         lambda e, bu_=bu_, isub=isub, i=i, ts_slot=ts_slot: e.scalar_tensor_tensor(
                             fuse_t(i), pb[bu_][:, isub * 256:(isub + 1) * 256],
                             C["su"][:, i:i + 1],
                             tsil_sb[:, ts_slot * 256:(ts_slot + 1) * 256],
                             ALU.mult, ALU.mult), inc="dve")
                    fuse_ev_slot[ts_slot] = now("dve")
                gu_bank_act[bg_] = now("act")
                gu_bank_dve[bu_] = now("dve")
            last_fuse_dve = now("dve")
            if span == 0 and cl == 0:
                dbg_tap("fuse", arb[:, 0:16384])
            dn_bank_dve = {2: 0, 3: 0}
            for fop in range(8):
                bank = 2 + fop % 2
                first_waits = [("dve", last_fuse_dve)]
                if dn_bank_dve[bank]:
                    first_waits.append(("dve", dn_bank_dve[bank]))
                else:
                    first_waits.append(("act", gu_bank_act[bank]))
                    first_waits.append(("dve", gu_bank_dve[bank]))
                first = True
                for ihq in range(4):
                    kd = r_wd.load(lambda e, slot, fop=fop, ihq=ihq: e.dma_start(
                        out=wsAb[:, slot * 4096: slot * 4096 + 4096],
                        in_=dr["w_d"][fop, ihq].rearrange("ic p o -> p ic o")))
                    dbuf = (kd % 2) * 4096
                    w = (list(first_waits) if first else []) + [r_wd.use_wait(kd)]
                    def dmms(e, dbuf=dbuf, bank=bank, first=first, ihq=ihq):
                        last = None
                        for fh in range(2):
                            for ic in range(16):
                                last = e.matmul(
                                    pb[bank][:, fh * 256:(fh + 1) * 256],
                                    wsAb[:, dbuf + ic * 256 + fh * 128:
                                         dbuf + ic * 256 + fh * 128 + 128],
                                    fuse_t(ihq * 16 + ic),
                                    start=(first and fh == 0 and ic == 0),
                                    stop=(ihq == 3 and fh == 1 and ic == 15),
                                    skip_group_check=True)
                        return last
                    emit("tensor", w, dmms, inc="pe")
                    r_wd.free(now("pe"))
                    first = False
                d_pe = now("pe")
                for fh in range(2):
                    fo = fop * 2 + fh
                    pslot = fo % 2
                    emit("vector", [("pe", d_pe)],
                         lambda e, bank=bank, fh=fh, fo=fo, pslot=pslot: e.tensor_scalar(
                             ffp_sb[:, pslot * 256:(pslot + 1) * 256],
                             pb[bank][:, fh * 256:(fh + 1) * 256],
                             C["sd"][:, fo:fo + 1], None, ALU.mult), inc="dve")
                    pev = now("dve")
                    items = []
                    for q in range(2):
                        tq = cl * 2 + q
                        def evf(e, slot, tq=tq, fo=fo):
                            return e.tensor_tensor(
                                out_sb[:, tq * H + fo * 128: tq * H + (fo + 1) * 128],
                                out_sb[:, tq * H + fo * 128: tq * H + (fo + 1) * 128],
                                slot, ALU.add)
                        items.append((pb7[:, q * 128:(q + 1) * 128],
                                      ffp_sb[:, pslot * 256 + q * 128:
                                             pslot * 256 + (q + 1) * 128],
                                      C["identf"][:], None, evf))
                    _, dd = transpose_batch(
                        tb7, items,
                        extra_pe_waits=[("dve", pev), ("act", outadd_guard_act),
                                        ("ldx", ldx_now)])
                    for q in range(2):
                        out_add_ev[cl * 2 + q] = dd
                dn_bank_dve[bank] = now("dve")
            gu_bank_dve[2] = max(gu_bank_dve[2], dn_bank_dve[2])
            gu_bank_dve[3] = max(gu_bank_dve[3], dn_bank_dve[3])
            S["hb2"] = (0, dn_bank_dve[2])
            S["hb3"] = (0, dn_bank_dve[3])
        S["ffn_pe"] = now("pe")

        for cl in range(2):
            DT = A_DT
            dt_fm = arf[0:64, DT:DT + 256]
            ca_fm = arf[0:64, DT + 256:DT + 512]
            lndt = arf[0:64, DT + 512:DT + 768]
            scr1 = arf[0:64, DT + 768:DT + 1024]
            scr2 = arf[0:64, DT + 1024:DT + 1280]
            scr3 = arf[0:64, DT + 1280:DT + 1536]
            dts = dtraw_sb[:, cl * 256:(cl + 1) * 256]
            dtg = [("act", dtraw_act), ("pe", now("pe"))]
            emit("scalar", dtg,
                 lambda e, dts=dts, scr1=scr1: e.activation(scr1, dts, AF.Abs),
                 inc="act")
            emit("scalar", [],
                 lambda e, scr1=scr1, scr2=scr2: e.activation(scr2, scr1, AF.Exp,
                                                              scale=-1.0), inc="act")
            emit("scalar", [],
                 lambda e, scr2=scr2, scr3=scr3: e.activation(scr3, scr2, AF.Ln,
                                                              bias=C["cvals"][0:64, 1:2]), inc="act")
            emit("vector", [("act", now("act"))],
                 lambda e, dts=dts, scr2=scr2: e.tensor_scalar(
                     scr2, dts, 0.0, None, ALU.max), inc="dve")
            emit("vector", [],
                 lambda e, dt_fm=dt_fm, scr3=scr3, scr2=scr2: e.tensor_tensor(
                     dt_fm, scr3, scr2, ALU.add), inc="dve")
            emit("scalar", [("dve", now("dve"))],
                 lambda e, lndt=lndt, dt_fm=dt_fm: e.activation(lndt, dt_fm, AF.Ln),
                 inc="act")
            emit("vector", [],
                 lambda e, scr1=scr1, dt_fm=dt_fm: e.tensor_scalar(
                     scr1, dt_fm, C["acol"][:], None, ALU.mult), inc="dve")
            emit("vector", [],
                 lambda e, ca_fm=ca_fm, scr1=scr1: e.tensor_tensor_scan(
                     ca_fm, scr1, scr1, 0.0, ALU.add, ALU.bypass),
                 inc="dve")
            ca_dve = now("dve")
            emit("vector", [("act", now("act"))],
                 lambda e, scr2=scr2, lndt=lndt, ca_fm=ca_fm: e.tensor_tensor(
                     scr2, lndt, ca_fm, ALU.subtract), inc="dve")
            emit("scalar", [("dve", now("dve"))],
                 lambda e, scr3=scr3, ca_fm=ca_fm: e.activation(
                     scr3, ca_fm, AF.Exp, scale=-1.0, bias=ca_fm[:, 255:256]),
                 inc="act")
            emit("vector", [("act", now("act"))],
                 lambda e, scr3=scr3, dt_fm=dt_fm: e.tensor_tensor(
                     scr3, scr3, dt_fm, ALU.mult), inc="dve")
            emit("scalar", [],
                 lambda e, ca_fm=ca_fm: e.activation(etcol_sb[:], ca_fm[:, 255:256],
                                                     AF.Exp), inc="act")
            sm_dve = now("dve")
            sm_act = now("act")
            CATM = A_CATM
            items = []
            for q in range(2):
                items.append((pb7[:, q * 64:(q + 1) * 64],
                              ca_fm[:, q * 128:(q + 1) * 128], C["identf"][0:64, 0:64],
                              arf[:, CATM + q * 64:CATM + (q + 1) * 64], None))
            for q in range(2):
                items.append((pb7[:, 128 + q * 64:128 + (q + 1) * 64],
                              scr2[:, q * 128:(q + 1) * 128], C["identf"][0:64, 0:64],
                              arf[:, CATM + 128 + q * 64:CATM + 128 + (q + 1) * 64], None))
            for q in range(2):
                items.append((pb7[:, 256 + q * 64:256 + (q + 1) * 64],
                              scr3[:, q * 128:(q + 1) * 128], C["identf"][0:64, 0:64],
                              arf[:, CATM + 256 + q * 64:CATM + 256 + (q + 1) * 64], None))
            items.append((pb7[0:1, 384:448], etcol_sb[:], C["identf"][0:64, 0:64],
                          etrow_sb[0:1, :], None))
            transpose_batch(tb7, items, extra_pe_waits=[("dve", sm_dve), ("act", sm_act)])
            catm_dve = tb7.last_drain
            emit("scalar", [("dve", catm_dve)],
                 lambda e: e.activation(arf[:, CATM + 384:CATM + 512],
                                        arf[:, CATM:CATM + 128], AF.Exp), inc="act")
            eca_act = now("act")
            emit("tensor", [("dve", catm_dve)],
                 lambda e: e.matmul(pb7[:, 448:512], C["onesr"][:], etrow_sb[0:1, :],
                                    start=True, stop=True), inc="pe")
            emit("vector", [("pe", now("pe"))],
                 lambda e: e.tensor_copy(arf[:, CATM + 512:CATM + 576], pb7[:, 448:512]),
                 inc="dve")
            etb_dve = now("dve")

            gmw = [("act", silu_act_ev[33]), ("act", now("act")), ("dve", now("dve"))]
            def gmm(e, cl=cl):
                e.matmul(pb[4][:, 0:256], bc_sb[:, cl * 256:cl * 256 + 128],
                         bc_sb[:, 512 + cl * 256:512 + (cl + 1) * 256],
                         start=True, stop=False, skip_group_check=True)
                return e.matmul(pb[4][:, 256:512],
                                bc_sb[:, cl * 256 + 128:cl * 256 + 256],
                                bc_sb[:, 512 + cl * 256:512 + (cl + 1) * 256],
                                start=False, stop=True, skip_group_check=True)
            emit("tensor", gmw, gmm, inc="pe")
            emit("vector", [("pe", now("pe"))],
                 lambda e: e.tensor_tensor(arf[:, A_GM0:A_GM0 + 256], pb[4][:, 0:256],
                                           C["mask0"][:], ALU.mult), inc="dve")
            emit("vector", [],
                 lambda e: e.tensor_tensor(arf[:, A_GM1:A_GM1 + 128], pb[4][:, 384:512],
                                           C["mask0"][:, 0:128], ALU.mult), inc="dve")
            gm_dve = now("dve")
            if span == 0 and cl == 0:
                dbg_tap("gm", arf[:, 0:384])

            if seq_start and cl == 0:
                emit("vector", [("pe", S["ssm_pe"])] if span else [],
                     lambda e: e.memset(rbf_sb[:], 0.0), inc="dve")
                for h in range(NH):
                    rupd_ev[h] = now("dve")

            argE_ev = [0] * NH
            sx_ev = [0] * NH
            toff_ev = [0] * NH
            yr_ev = [0] * NH
            ymm_ev = [0] * NH
            for h in range(NH):
                r = h % 2
                g4 = h // 4
                if h % 4 == 0:
                    cw = [("dve", ca_dve)]
                    if car_k >= 2:
                        cw.append(("pe", car_free_pe[car_k - 2]))
                    emit("sync", cw,
                         lambda e, g4=g4: e.dma_start(
                             out=arf[0:1, A_CAR + (g4 % 2) * 1024:
                                     A_CAR + (g4 % 2) * 1024 + 1024],
                             in_=ca_fm[g4 * 4:(g4 + 1) * 4, :]),
                         inc="ldcr%d" % (car_k % 2), inc_by=16)
                    car_k += 1
                cabbank = pb[4] if h % 2 == 0 else pb7
                cw = [("ldcr%d" % ((car_k - 1) % 2), ((car_k - 1) // 2 + 1) * 16)]
                if h >= 2:
                    cw.append(("dve", argE_ev[h - 2]))
                elif h == 0:
                    cw.append(("dve", gm_dve))
                else:
                    cw.append(("dve", etb_dve))
                emit("tensor", cw,
                     lambda e, cabbank=cabbank, g4=g4, h=h: e.matmul(
                         cabbank[:, 0:256], C["onesr"][:],
                         arf[0:1, A_CAR + (g4 % 2) * 1024 + (h % 4) * 256:
                             A_CAR + (g4 % 2) * 1024 + (h % 4) * 256 + 256],
                         start=True, stop=True, skip_group_check=True), inc="pe")
                if h % 4 == 3:
                    car_free_pe.append(now("pe"))
                cab_pe = now("pe")
                bcol0 = arf[:, A_CATM + 128 + h:A_CATM + 128 + h + 1]
                bcol1 = arf[:, A_CATM + 192 + h:A_CATM + 192 + h + 1]
                emit("vector", [("pe", cab_pe)],
                     lambda e, cabbank=cabbank, r=r, bcol0=bcol0: e.tensor_scalar(
                         arf[:, A_ARG + r * 384:A_ARG + r * 384 + 256],
                         cabbank[:, 0:256], bcol0, CLAMP, ALU.add, ALU.min), inc="dve")
                emit("vector", [],
                     lambda e, cabbank=cabbank, r=r, bcol1=bcol1: e.tensor_scalar(
                         arf[:, A_ARG + r * 384 + 256:A_ARG + r * 384 + 384],
                         cabbank[:, 128:256], bcol1, CLAMP, ALU.add, ALU.min),
                     inc="dve")
                argE_ev[h] = now("dve")
                ew = [("dve", argE_ev[h])]
                if h == 0:
                    ew.append(("act", eca_act))
                emit("scalar", ew,
                     lambda e, r=r: e.activation(
                         arf[:, A_E + r * 384:A_E + r * 384 + 256],
                         arf[:, A_ARG + r * 384:A_ARG + r * 384 + 256], AF.Exp),
                     inc="act")
                emit("scalar", [],
                     lambda e, r=r: e.activation(
                         arf[:, A_E + r * 384 + 256:A_E + r * 384 + 384],
                         arf[:, A_ARG + r * 384 + 256:A_ARG + r * 384 + 384], AF.Exp),
                     inc="act")
                e_act = now("act")
                sw = [("act", e_act)]
                if h >= 2:
                    sw.append(("pe", ymm_ev[h - 2]))
                emit("vector", sw,
                     lambda e, r=r: e.tensor_tensor(
                         ssb[:, r * 384:r * 384 + 256], arf[:, A_GM0:A_GM0 + 256],
                         arf[:, A_E + r * 384:A_E + r * 384 + 256], ALU.mult),
                     inc="dve")
                emit("vector", [],
                     lambda e, r=r: e.tensor_tensor(
                         ssb[:, r * 384 + 256:r * 384 + 384], arf[:, A_GM1:A_GM1 + 128],
                         arf[:, A_E + r * 384 + 256:A_E + r * 384 + 384], ALU.mult),
                     inc="dve")
                for q in range(2):
                    tt = 2 * cl + q
                    wcol = arf[:, A_CATM + 256 + q * 64 + h:A_CATM + 256 + q * 64 + h + 1]
                    emit("vector", [],
                         lambda e, r=r, q=q, tt=tt, wcol=wcol, h=h: e.tensor_scalar(
                             xpb[:, (r * 2 + q) * 64:(r * 2 + q) * 64 + 64],
                             xtm_sb[:, tt * INTER + h * 64: tt * INTER + h * 64 + 64],
                             wcol, None, ALU.mult), inc="dve")
                sx_ev[h] = now("dve")
                B = pb[2 + h % 2]
                yw = [("dve", sx_ev[h]), ("dve", rupd_ev[h])]
                if h >= 2:
                    yw.append(("act", toff_ev[h - 2]))
                    yw.append(("dve", yr_ev[h - 2]))
                else:
                    ga, gd = S["hb2"] if h % 2 == 0 else S["hb3"]
                    if ga:
                        yw.append(("act", ga))
                    if gd:
                        yw.append(("dve", gd))
                def ymms(e, B=B, r=r, cl=cl, h=h):
                    tt0, tt1 = 2 * cl, 2 * cl + 1
                    hc = h * 64
                    e.matmul(B[:, 0:64], ssb[:, r * 384:r * 384 + 128],
                             xtm_sb[:, tt0 * INTER + hc:tt0 * INTER + hc + 64],
                             start=True, stop=True, skip_group_check=True)
                    e.matmul(B[:, 64:128], ssb[:, r * 384 + 128:r * 384 + 256],
                             xtm_sb[:, tt0 * INTER + hc:tt0 * INTER + hc + 64],
                             start=False, stop=False, skip_group_check=True)
                    e.matmul(B[:, 64:128], ssb[:, r * 384 + 256:r * 384 + 384],
                             xtm_sb[:, tt1 * INTER + hc:tt1 * INTER + hc + 64],
                             start=False, stop=True, skip_group_check=True)
                    e.matmul(B[:, 128:192],
                             bc_sb[:, 512 + cl * 256:512 + cl * 256 + 128],
                             rbf_sb[:, hc:hc + 64],
                             start=False, stop=True, skip_group_check=True)
                    e.matmul(B[:, 192:256],
                             bc_sb[:, 512 + cl * 256 + 128:512 + cl * 256 + 256],
                             rbf_sb[:, hc:hc + 64],
                             start=False, stop=True, skip_group_check=True)
                    e.matmul(B[:, 256:320],
                             btm_sb[:, (2 * cl) * 128:(2 * cl) * 128 + 128],
                             xpb[:, (r * 2) * 64:(r * 2) * 64 + 64],
                             start=False, stop=False, skip_group_check=True)
                    return e.matmul(
                        B[:, 256:320],
                        btm_sb[:, (2 * cl + 1) * 128:(2 * cl + 1) * 128 + 128],
                        xpb[:, (r * 2 + 1) * 64:(r * 2 + 1) * 64 + 64],
                        start=False, stop=True, skip_group_check=True)
                emit("tensor", yw, ymms, inc="pe")
                ymm_ev[h] = now("pe")
                for q in range(2):
                    ecacol = arf[:, A_CATM + 384 + q * 64 + h:
                                 A_CATM + 384 + q * 64 + h + 1]
                    emit("scalar", [("pe", ymm_ev[h])] if q == 0 else [],
                         lambda e, B=B, q=q, r=r, ecacol=ecacol: e.activation(
                             arf[:, A_TOFF + (r * 2 + q) * 64:
                                 A_TOFF + (r * 2 + q) * 64 + 64],
                             B[:, 128 + q * 64:128 + (q + 1) * 64], AF.Identity,
                             scale=ecacol), inc="act")
                toff_ev[h] = now("act")
                p = h // 2
                pslot = p % 2
                for q in range(2):
                    tslot = arf[:, A_TOFF + (r * 2 + q) * 64:A_TOFF + (r * 2 + q) * 64 + 64]
                    emit("vector", [("act", toff_ev[h])] if q == 0 else [],
                         lambda e, B=B, q=q, tslot=tslot: e.tensor_tensor(
                             tslot, tslot, B[:, q * 64:(q + 1) * 64], ALU.add),
                         inc="dve")
                    tt = 2 * cl + q
                    emit("vector", [],
                         lambda e, tslot=tslot, tt=tt, h=h, pslot=pslot, q=q:
                         e.scalar_tensor_tensor(
                             ybuf[:, pslot * 256 + q * 128 + (h % 2) * 64:
                                  pslot * 256 + q * 128 + (h % 2) * 64 + 64],
                             xtm_sb[:, tt * INTER + h * 64: tt * INTER + h * 64 + 64],
                             C["DB"][:, h:h + 1], tslot, ALU.mult, ALU.add), inc="dve")
                emit("vector", [],
                     lambda e, B=B, h=h: e.scalar_tensor_tensor(
                         rbf_sb[:, h * 64:h * 64 + 64], rbf_sb[:, h * 64:h * 64 + 64],
                         arf[:, A_CATM + 512 + h:A_CATM + 512 + h + 1],
                         B[:, 256:320], ALU.mult, ALU.add), inc="dve")
                rupd_ev[h] = now("dve")
                yr_ev[h] = now("dve")
                if h % 2 == 1:
                    fc = p
                    yev = now("dve")
                    items = []
                    for q in range(2):
                        def evf(e, slot, fc=fc, cl=cl, q=q):
                            gsl = gate_sb[:, fc * SPAN + cl * 256 + q * 128:
                                          fc * SPAN + cl * 256 + (q + 1) * 128]
                            return e.scalar_tensor_tensor(
                                gsl, slot, C["nw"][:, fc:fc + 1], gsl,
                                ALU.mult, ALU.mult)
                        items.append((
                            pb6[:, q * 128:(q + 1) * 128],
                            ybuf[:, pslot * 256 + q * 128: pslot * 256 + (q + 1) * 128],
                            C["identb"][:], None, evf))
                    _, pd = transpose_batch(tb6, items, extra_pe_waits=[("dve", yev)])
                    emit("scalar", [("dve", pd)],
                         lambda e, fc=fc, cl=cl: e.activation(
                             arf[:, A_G2 + (fc % 2) * 256:A_G2 + (fc % 2) * 256 + 256],
                             gate_sb[:, fc * SPAN + cl * 256: fc * SPAN + (cl + 1) * 256],
                             AF.Square), inc="act")
                    sqe = now("act")
                    rmsw = [("act", sqe)]
                    if fc < 2 and span + cl > 0:
                        rmsw.append(("dve", S["pb5_dve"]))
                    emit("tensor", rmsw,
                         lambda e, fc=fc: e.matmul(
                             pb[5][0:1, 0:256], C["onescol"][:],
                             arf[:, A_G2 + (fc % 2) * 256:A_G2 + (fc % 2) * 256 + 256],
                             start=(fc == 0), stop=(fc == 31),
                             skip_group_check=True), inc="pe")
            emit("vector", [("pe", now("pe"))],
                 lambda e: e.tensor_scalar(rstdrow_sb[0:1, :], pb[5][0:1, 0:256],
                                           1.0 / INTER, None, ALU.mult), inc="dve")
            S["pb5_dve"] = now("dve")
            emit("scalar", [("dve", now("dve"))],
                 lambda e: e.activation(rstdrow_sb[0:1, :], rstdrow_sb[0:1, :],
                                        AF.Ln, bias=C["cvals"][0:1, 0:1]), inc="act")
            emit("scalar", [],
                 lambda e: e.activation(rstdrow_sb[0:1, :], rstdrow_sb[0:1, :],
                                        AF.Exp, scale=-0.5), inc="act")
            emit("tensor", [("act", now("act")), ("dve", argE_ev[NH - 1])],
                 lambda e: e.matmul(pb7[:, 0:256], C["onesr"][:], rstdrow_sb[0:1, :],
                                    start=True, stop=True, skip_group_check=True),
                 inc="pe")
            emit("vector", [("pe", now("pe"))],
                 lambda e, cl=cl: e.tensor_copy(
                     arf[:, A_RSTD + cl * 256:A_RSTD + (cl + 1) * 256], pb7[:, 0:256]),
                 inc="dve")
            tb7.last_drain = now("dve")
            S["pb23_act"] = toff_ev[NH - 1]
            S["pb23_dve"] = yr_ev[NH - 1]
            S["pb4_dve"] = argE_ev[NH - 2]
            S["hb2"] = (toff_ev[NH - 2], yr_ev[NH - 2])
            S["hb3"] = (toff_ev[NH - 1], yr_ev[NH - 1])
        S["ssm_pe"] = now("pe")
        rstd_dve = now("dve")
        S["arena_dve"] = rstd_dve
        if span == 0:
            dbg_tap("g", gate_sb[:])
            dbg_tap("rstd", arf[:, A_RSTD:A_RSTD + 512])

        op_bank_dve = {0: 0, 1: 0}
        for og in range(16):
            k = r_wout.load(lambda e, slot, og=og: e.dma_start(
                out=wsBb[:, slot * 4096: slot * 4096 + 4096],
                in_=dr["w_out"][og].rearrange("kc p o -> p kc o")))
            bank = og % 2
            pw = [r_wout.use_wait(k), ("dve", rstd_dve), ("act", bank_evac[bank])]
            if op_bank_dve[bank]:
                pw.append(("dve", op_bank_dve[bank]))
            obuf = (k % 2) * 4096
            def omms(e, obuf=obuf, bank=bank):
                last = None
                for cl in range(2):
                    for kc in range(32):
                        last = e.matmul(
                            pb[bank][:, cl * 256:(cl + 1) * 256],
                            wsBb[:, obuf + kc * 128: obuf + (kc + 1) * 128],
                            gate_sb[:, kc * SPAN + cl * 256: kc * SPAN + (cl + 1) * 256],
                            start=(cl == 0 and kc == 0), stop=(cl == 1 and kc == 31),
                            skip_group_check=True)
                return last
            emit("tensor", pw, omms, inc="pe")
            o_pe = now("pe")
            r_wout.free(o_pe)
            for cl in range(2):
                pslot = cl
                emit("vector", [("pe", o_pe)],
                     lambda e, bank=bank, cl=cl, pslot=pslot: e.tensor_tensor(
                         ffp_sb[:, pslot * 256:(pslot + 1) * 256],
                         pb[bank][:, cl * 256:(cl + 1) * 256],
                         arf[:, A_RSTD + cl * 256:A_RSTD + (cl + 1) * 256], ALU.mult),
                     inc="dve")
                emit("vector", [],
                     lambda e, og=og, pslot=pslot: e.tensor_scalar(
                         ffp_sb[:, pslot * 256:(pslot + 1) * 256],
                         ffp_sb[:, pslot * 256:(pslot + 1) * 256],
                         C["bout"][:, og:og + 1], None, ALU.add), inc="dve")
                pev = now("dve")
                items = []
                for q in range(2):
                    tq = cl * 2 + q
                    def evf(e, slot, tq=tq, og=og):
                        return e.tensor_tensor(
                            out_sb[:, tq * H + og * 128: tq * H + (og + 1) * 128],
                            out_sb[:, tq * H + og * 128: tq * H + (og + 1) * 128],
                            slot, ALU.add)
                    items.append((pb7[:, q * 128:(q + 1) * 128],
                                  ffp_sb[:, pslot * 256 + q * 128:
                                         pslot * 256 + (q + 1) * 128],
                                  C["identf"][:], None, evf))
                _, dd = transpose_batch(tb7, items, extra_pe_waits=[("dve", pev)])
                out_add_ev[cl * 2] = dd
                out_add_ev[cl * 2 + 1] = dd
            op_bank_dve[bank] = now("dve")
        S["outproj_pe"] = now("pe")
        S["pb01_dve"] = max(op_bank_dve[0], op_bank_dve[1])
        if span == 0:
            dbg_tap("outsb", out_sb[:])

        for tq in range(4):
            emit("sync", [("dve", out_add_ev[tq])],
                 lambda e, tq=tq, row0=row0: e.dma_start(
                     out=dout[row0 + tq * 128: row0 + (tq + 1) * 128, :],
                     in_=out_sb[:, tq * H:(tq + 1) * H]), inc="st", inc_by=16)

    total_st = now("st")
    total_dbg = now("dbg")
    def fin(e):
        if total_dbg:
            e.wait_ge(sems["dbg"], total_dbg)
        e.wait_ge(sems["st"], total_st)
        return e.nop()
    emit("sync", [], fin)

    nops = {k: sum(1 for en, _ in ops if en == k)
            for k in ["tensor", "vector", "scalar", "gpsimd", "sync"]}
    print("op groups per engine:", nops)

    with nc.Block() as block:
        @block.tensor
        def _(e):
            for en, f in ops:
                if en == "tensor":
                    f(e)

        @block.vector
        def _(e):
            for en, f in ops:
                if en == "vector":
                    f(e)

        @block.scalar
        def _(e):
            for en, f in ops:
                if en == "scalar":
                    f(e)

        @block.gpsimd
        def _(e):
            for en, f in ops:
                if en == "gpsimd":
                    f(e)

        @block.sync
        def _(e):
            for en, f in ops:
                if en == "sync":
                    f(e)

    ctx.close()
    return nc


def kernel(**inputs):
    g = prep_host(inputs)
    x = _f32(inputs["L0_attn_hidden_states"])
    nc = build_program()
    in_maps = []
    for core in range(NCORES):
        m = dict(g)
        m["x"] = np.ascontiguousarray(
            x[core * SEQ_PER_CORE:(core + 1) * SEQ_PER_CORE].reshape(SEQ_PER_CORE * L, H))
        in_maps.append(m)
    res = run_bass_kernel_spmd(nc, in_maps, core_ids=list(range(NCORES)))
    out = np.stack([np.asarray(res.results[c]["out"]).reshape(SEQ_PER_CORE, L, H)
                    for c in range(NCORES)])
    return out.reshape(16, L, H)


if __name__ == "__main__":
    import reference as refmod
    ins = refmod.setup_inputs(0)
    o = kernel(**{k: np.asarray(v) for k, v in ins.items()})
    print(o.shape, o.dtype)


# version 03
# speedup vs baseline: 1.1854x; 1.1854x over previous
import numpy as np
import ml_dtypes
from contextlib import ExitStack

import concourse.bass as bass
import concourse.mybir as mybir
from concourse.bass_utils import run_bass_kernel_spmd

F32 = mybir.dt.float32
BF16 = mybir.dt.bfloat16
I8 = mybir.dt.int8
U8 = mybir.dt.uint8
AF = mybir.ActivationFunctionType
ALU = mybir.AluOpType
AXX = mybir.AxisListType.X

NCORES = 8
L = 2048
H = 2048
NH = 64
INTER = 4096
SEQ_PER_CORE = 2
SPAN = 512
NSPAN = SEQ_PER_CORE * L // SPAN
KC = H // 128
OCH = 68
CLAMP = 10.0

DBG = {}


def _f32(x):
    return np.ascontiguousarray(np.asarray(x, dtype=np.float32))


def prep_host(inputs):
    g = {}
    w_in = _f32(inputs["L0_attn_in_proj_w"])
    w_in_p = np.zeros((OCH * 128, H), np.float32)
    w_in_p[:8512] = w_in
    g["w_in"] = np.ascontiguousarray(
        w_in_p.T.reshape(KC, 128, 34, 256).transpose(2, 0, 1, 3))
    b_in = np.zeros((OCH * 128,), np.float32)
    b_in[:8512] = _f32(inputs["L0_attn_in_proj_b"])
    g["b_in"] = np.ascontiguousarray(b_in.reshape(OCH, 128).T)
    g["convw"] = np.ascontiguousarray(
        _f32(inputs["L0_attn_conv_w"]).reshape(34, 128, 4).transpose(1, 0, 2).reshape(128, 136))
    g["dtb"] = _f32(inputs["L0_attn_dt_bias"]).reshape(64, 1)
    g["alog"] = _f32(inputs["L0_attn_A_log"]).reshape(64, 1)
    g["d_row"] = _f32(inputs["L0_attn_D"]).reshape(1, 64)
    g["nw"] = np.ascontiguousarray(_f32(inputs["L0_attn_norm_w"]).reshape(32, 128).T)
    g["w_out"] = np.ascontiguousarray(
        _f32(inputs["L0_attn_out_proj_w"]).T.reshape(32, 128, 16, 128).transpose(2, 0, 1, 3))
    g["b_out"] = np.ascontiguousarray(_f32(inputs["L0_attn_out_proj_b"]).reshape(16, 128).T)
    g["w_g"] = np.ascontiguousarray(
        np.asarray(inputs["L0_ffn_w_gate"]).reshape(KC, 128, 32, 256).transpose(2, 0, 1, 3))
    g["w_u"] = np.ascontiguousarray(
        np.asarray(inputs["L0_ffn_w_up"]).reshape(KC, 128, 32, 256).transpose(2, 0, 1, 3))
    g["w_d"] = np.ascontiguousarray(
        np.asarray(inputs["L0_ffn_w_down"]).reshape(4, 16, 128, 8, 256).transpose(3, 0, 1, 2, 4))
    g["s_g"] = np.ascontiguousarray(_f32(inputs["L0_ffn_s_gate"]).reshape(64, 128).T)
    g["s_u"] = np.ascontiguousarray(_f32(inputs["L0_ffn_s_up"]).reshape(64, 128).T)
    g["s_d"] = np.ascontiguousarray(_f32(inputs["L0_ffn_s_down"]).reshape(16, 128).T)
    g["ident_b"] = np.eye(128, dtype=ml_dtypes.bfloat16)
    g["ident_f"] = np.eye(128, dtype=np.float32)
    g["onesr"] = np.ones((1, 128), np.float32)
    g["onescol"] = np.ones((128, 1), np.float32)
    s = np.arange(128)[:, None]
    t = np.arange(256)[None, :]
    g["mask0"] = (t >= s).astype(ml_dtypes.bfloat16)
    cv = np.zeros((128, 2), np.float32)
    cv[:, 0] = 1e-5
    cv[:, 1] = 1.0
    g["cvals"] = cv
    return g


def build_program():
    nc = bass.Bass("TRN2", target_bir_lowering=False, debug=False)

    dx = nc.dram_tensor("x", [SEQ_PER_CORE * L, H], F32, kind="ExternalInput").ap()
    dout = nc.dram_tensor("out", [SEQ_PER_CORE * L, H], F32, kind="ExternalOutput").ap()
    dr = {}
    for n, shape, dt in [
        ("w_in", [34, KC, 128, 256], F32), ("b_in", [128, OCH], F32),
        ("convw", [128, 136], F32), ("dtb", [64, 1], F32), ("alog", [64, 1], F32),
        ("d_row", [1, 64], F32), ("nw", [128, 32], F32),
        ("w_out", [16, 32, 128, 128], F32), ("b_out", [128, 16], F32),
        ("w_g", [32, KC, 128, 256], I8), ("w_u", [32, KC, 128, 256], I8),
        ("w_d", [8, 4, 16, 128, 256], I8),
        ("s_g", [128, 64], F32), ("s_u", [128, 64], F32), ("s_d", [128, 16], F32),
        ("ident_b", [128, 128], BF16), ("ident_f", [128, 128], F32),
        ("onesr", [1, 128], F32), ("onescol", [128, 1], F32),
        ("mask0", [128, 256], BF16), ("cvals", [128, 2], F32),
    ]:
        dr[n] = nc.dram_tensor(n, shape, dt, kind="ExternalInput").ap()
    d_w_in_b = nc.dram_tensor("w_in_b", [34, 128, 4096], BF16, kind="Internal").ap()
    d_w_out_b = nc.dram_tensor("w_out_b", [16, 128, 4096], BF16, kind="Internal").ap()
    ddbg = {n: nc.dram_tensor("dbg_" + n, list(s), F32, kind="ExternalOutput").ap()
            for n, s in DBG.items()}

    ctx = ExitStack()
    sb = lambda n, s, d: ctx.enter_context(nc.sbuf_tensor(n, s, d))
    psb = lambda n, s, d: ctx.enter_context(nc.psum_tensor(n, s, d))

    C = {}
    for n, shape, dt in [
        ("identb", [128, 128], BF16), ("identf", [128, 128], F32),
        ("onesr", [1, 128], F32), ("onescol", [128, 1], F32),
        ("mask0", [128, 256], BF16), ("b_in", [128, OCH], F32),
        ("convw", [128, 136], F32), ("dtb", [64, 1], F32), ("alog", [64, 1], F32),
        ("d_row", [1, 64], F32), ("nw", [128, 32], F32), ("bout", [128, 16], F32),
        ("sg", [128, 64], F32), ("su", [128, 64], F32), ("sd", [128, 16], F32),
        ("acol", [64, 1], F32), ("bias66", [64, 1], F32),
        ("DB", [128, 64], F32), ("cvals", [128, 2], F32),
    ]:
        C[n] = sb("c_" + n, shape, dt)

    xn_sb = sb("xn", [128, KC * SPAN], BF16)
    gate_sb = sb("gate", [128, 32 * SPAN], BF16)
    xtm_sb = sb("xtm", [128, 4 * INTER], BF16)
    out_sb = sb("outtm", [128, 4 * H], F32)
    rbf_sb = sb("rbf", [128, NH * 64], BF16)
    arena = sb("arena", [128, 32768], U8)
    arf = arena[:].bitcast(F32)
    arb = arena[:].bitcast(BF16)
    lnconv = sb("lnconv", [128, 6208], U8)
    lcf = lnconv[:].bitcast(F32)
    ssb = sb("ssb", [128, 2 * 384], BF16)
    ybuf = sb("ybuf", [128, 4 * 256], BF16)
    xpb = sb("xpb", [128, 4 * 64], BF16)
    bc_sb = sb("bcfm", [128, 2 * SPAN], BF16)
    btm_sb = sb("btm", [128, 4 * 128], BF16)
    dtraw_sb = sb("dtraw", [64, SPAN], F32)
    halo_sb = sb("halo", [128, 34 * 3], F32)
    ffp_sb = sb("ffp", [128, 1024], BF16)
    tsil_sb = sb("tsil", [128, 1024], BF16)
    lnsm = sb("lnsm", [128, 16], F32)
    wsA = sb("wsA", [128, 16384], U8)
    wsB = sb("wsB", [128, 16384], U8)
    wsAb = wsA[:].bitcast(BF16)
    wsBb = wsB[:].bitcast(BF16)

    A_GM0, A_GM1 = 0, 256
    A_ARG, A_E = 384, 1152
    A_G2, A_RSTD = 1920, 2432
    A_DT = 2944
    A_CATM = 4480
    A_TOFF = 5056
    A_RSR = 5568
    A_FFOUT = A_G2
    A_CAR = 6144

    rstdrow_sb = arf[:, A_RSR:A_RSR + 256]
    etrow_sb = arf[:, A_RSR + 256:A_RSR + 320]
    etcol_sb = arf[0:64, A_RSR + 320:A_RSR + 321]

    def fuse_t(i):
        return arb[:, i * 512:(i + 1) * 512]

    print("sbuf bytes remaining:", nc.sbuf_bytes_remaining)

    pb = [psb(f"pb{i}", [128, 512], F32) for i in range(6)]
    pb6 = psb("pb6", [128, 1024], BF16)
    pb7 = psb("pb7", [128, 512], F32)
    pbb = {"pb6": pb6[:],
           "pb0b": pb[0][:].bitcast(BF16), "pb1b": pb[1][:].bitcast(BF16),
           "pb2b": pb[2][:].bitcast(BF16), "pb3b": pb[3][:].bitcast(BF16)}

    sems, cnt = {}, {}
    for name in ["ldx", "ldc", "ldcr0", "ldcr1", "st", "dbg",
                 "win0", "win1", "wout0", "wout1", "wgu0", "wgu1", "wd0", "wd1",
                 "cvA0", "cvA1", "cvB0", "cvB1", "cvl0", "cvl1", "cvm0", "cvm1",
                 "pe", "dve", "act"]:
        sems[name] = ctx.enter_context(nc.semaphore(name))
        cnt[name] = 0

    ops = []
    SELF_CH = {"vector": "dve", "scalar": "act"}

    def emit(en, waits, fn, inc=None, inc_by=1):
        if en in SELF_CH and cnt[SELF_CH[en]] > 0:
            waits = list(waits) + [(SELF_CH[en], cnt[SELF_CH[en]])]
        if cnt["dbg"] > 0:
            waits = list(waits) + [("dbg", cnt["dbg"])]
        for c, v in waits:
            assert v <= cnt[c], f"wait on future event {c}:{v} > {cnt[c]}"
        if inc is not None:
            cnt[inc] += inc_by
        w = [(sems[c], v) for c, v in waits if v > 0]
        s_inc = (sems[inc], inc_by) if inc else None
        def f(eng, w=w, s_inc=s_inc, fn=fn):
            for sh, v in w:
                eng.wait_ge(sh, v)
            instr = fn(eng)
            if s_inc is not None:
                instr.then_inc(s_inc[0], s_inc[1])
        ops.append((en, f))

    def now(chn):
        return cnt[chn]

    class Ring:
        def __init__(self, ld0, ld1, depth=2):
            self.ld = [ld0, ld1]
            self.depth = depth
            self.loadrec = []
            self.freed = []
        def load(self, fn, engine="gpsimd", extra_waits=(), sem_override=None):
            k = len(self.loadrec)
            waits = list(extra_waits)
            j = k - self.depth
            if j >= 0:
                assert j < len(self.freed), "block consumed before reuse"
                waits.append(self.freed[j])
            slot = k % self.depth
            sname = (sem_override or self.ld)[slot]
            emit(engine, waits, lambda e, slot=slot: fn(e, slot),
                 inc=sname, inc_by=16)
            self.loadrec.append((sname, cnt[sname]))
            return k
        def use_wait(self, k):
            return self.loadrec[k]
        def free(self, ev):
            self.freed.append(ev)

    r_win = Ring("win0", "win1")
    r_wout = Ring("wout0", "wout1")
    r_wgu = Ring("wgu0", "wgu1")
    r_wd = Ring("wd0", "wd1")

    class TB:
        def __init__(self):
            self.state = {}
            self.cur = []
            self.i = 0
        def set_banks(self, keys, new_guards=None):
            for k in keys:
                if k not in self.state:
                    self.state[k] = {"last": 0, "guards": []}
                if new_guards and k in new_guards:
                    self.state[k]["guards"].extend(new_guards[k])
            self.cur = list(keys)
            self.i = 0
        def drain_of(self, key):
            return self.state[key]["last"] if key in self.state else 0
        def batch(self, entries, extra=()):
            key = self.cur[self.i]
            self.i = (self.i + 1) % len(self.cur)
            st = self.state[key]
            bank = pbb[key]
            assert len(entries) <= 8
            waits = list(extra) + st["guards"]
            st["guards"] = []
            if st["last"]:
                waits.append(("dve", st["last"]))
            def fill(e, entries=entries, bank=bank):
                last = None
                for j, (inap, idap, outap, evf) in enumerate(entries):
                    last = e.transpose(bank[:, j * 128:(j + 1) * 128], inap, idap)
                return last
            emit("tensor", waits, fill, inc="pe")
            fill_pe = now("pe")
            def drain(e, entries=entries, bank=bank, fill_pe=fill_pe):
                e.wait_ge(sems["pe"], fill_pe)
                last = None
                for j, (inap, idap, outap, evf) in enumerate(entries):
                    slot = bank[:, j * 128:(j + 1) * 128]
                    if evf is None:
                        last = e.tensor_copy(outap, slot)
                    else:
                        last = evf(e, slot)
                return last
            emit("vector", [], drain, inc="dve")
            st["last"] = now("dve")
            return fill_pe, st["last"]

    tbb = TB()

    def dbg_tap(name, src_ap):
        if name in DBG:
            emit("gpsimd", [("pe", now("pe")), ("dve", now("dve")), ("act", now("act"))],
                 lambda e, s=src_ap, d=ddbg[name]: e.dma_start(out=d, in_=s),
                 inc="dbg", inc_by=16)

    const_pairs = [
        (C["identb"], dr["ident_b"]), (C["identf"], dr["ident_f"]),
        (C["onesr"], dr["onesr"]), (C["onescol"], dr["onescol"]),
        (C["mask0"], dr["mask0"]), (C["b_in"], dr["b_in"]),
        (C["convw"], dr["convw"]), (C["dtb"], dr["dtb"]), (C["alog"], dr["alog"]),
        (C["d_row"], dr["d_row"]), (C["nw"], dr["nw"]), (C["bout"], dr["b_out"]),
        (C["sg"], dr["s_g"]), (C["su"], dr["s_u"]), (C["sd"], dr["s_d"]),
        (C["cvals"], dr["cvals"]),
    ]
    for dst, src in const_pairs:
        emit("sync", [], lambda e, d=dst, s=src: e.dma_start(out=d[:], in_=s),
             inc="ldc", inc_by=16)
    NCONST = now("ldc")
    emit("vector", [("ldc", NCONST)], lambda e: e.memset(rbf_sb[:], 0.0), inc="dve")
    emit("vector", [], lambda e: e.memset(halo_sb[:], 0.0), inc="dve")
    emit("scalar", [("ldc", NCONST)],
         lambda e: e.activation(C["acol"][:], C["alog"][:], AF.Exp), inc="act")
    emit("vector", [("act", now("act"))],
         lambda e: e.tensor_scalar(C["acol"][:], C["acol"][:], -1.0, None, ALU.mult),
         inc="dve")
    emit("vector", [("ldc", NCONST)],
         lambda e: e.tensor_tensor(C["bias66"][:], C["b_in"][0:64, 66:67], C["dtb"][:],
                                   ALU.add), inc="dve")
    emit("tensor", [("ldc", NCONST)],
         lambda e: e.matmul(pb7[:, 0:64], C["onesr"][:], C["d_row"][:],
                            start=True, stop=True), inc="pe")
    emit("vector", [("pe", now("pe"))],
         lambda e: e.tensor_copy(C["DB"][:], pb7[:, 0:64]), inc="dve")
    pb7_last = now("dve")
    INIT_DVE = now("dve")
    INIT_ACT = now("act")

    for og in range(34):
        k = r_win.load(lambda e, slot, og=og: e.dma_start(
            out=wsAb[:, slot * 4096: slot * 4096 + 4096],
            in_=dr["w_in"][og].rearrange("kc p o -> p kc o")),
            sem_override=("cvl0", "cvl1"))
        sname = "cvA%d" % (k % 2)
        emit("sync", [r_win.use_wait(k)],
             lambda e, og=og, k=k: e.dma_start(
                 out=d_w_in_b[og], in_=wsAb[:, (k % 2) * 4096: (k % 2) * 4096 + 4096]),
             inc=sname, inc_by=16)
        r_win.free((sname, cnt[sname]))
    for og in range(16):
        k = r_wout.load(lambda e, slot, og=og: e.dma_start(
            out=wsBb[:, slot * 4096: slot * 4096 + 4096],
            in_=dr["w_out"][og].rearrange("kc p o -> p kc o")),
            sem_override=("cvm0", "cvm1"))
        sname = "cvB%d" % (k % 2)
        emit("sync", [r_wout.use_wait(k)],
             lambda e, og=og, k=k: e.dma_start(
                 out=d_w_out_b[og], in_=wsBb[:, (k % 2) * 4096: (k % 2) * 4096 + 4096]),
             inc=sname, inc_by=16)
        r_wout.free((sname, cnt[sname]))

    rupd_ev = [INIT_DVE] * NH
    car_k = 0
    car_free_pe = []
    S = {"outproj_pe": 0, "ssm_pe": 0, "ffn_pe": 0, "arena_dve": 0,
         "pb01_dve": 0, "pb01_act": 0,
         "hb2": (0, 0), "hb3": (0, 0), "pb4_dve": 0, "pb5_dve": 0}

    for span in range(NSPAN):
        row0 = (span // (L // SPAN)) * L + (span % (L // SPAN)) * SPAN
        seq_start = (span % (L // SPAN)) == 0

        st_before = now("st")
        for tq in range(4):
            emit("sync", [("st", st_before)],
                 lambda e, tq=tq, row0=row0: e.dma_start(
                     out=out_sb[:, tq * H:(tq + 1) * H],
                     in_=dx[row0 + tq * 128: row0 + (tq + 1) * 128, :]),
                 inc="ldx", inc_by=16)
        ldx_now = now("ldx")

        tbb.set_banks(["pb6", "pb2b", "pb3b"],
                      new_guards=({"pb2b": [("act", S["hb2"][0]), ("dve", S["hb2"][1])],
                                   "pb3b": [("act", S["hb3"][0]), ("dve", S["hb3"][1])]}
                                  if span else None))
        xn_guard = [("pe", S["ffn_pe"])] if span else []
        xn_act_ev = []
        xn_fill_pe = []
        for tq in range(4):
            xt = out_sb[:, tq * H:(tq + 1) * H]
            cb = (tq % 2) * 8
            w0 = [("ldx", ldx_now), ("dve", INIT_DVE)]
            if tq >= 2:
                w0.append(("act", xn_act_ev[tq - 2]))
            emit("vector", w0,
                 lambda e, xt=xt, cb=cb: e.reduce_sum(lnsm[:, cb:cb + 1], xt, axis=AXX),
                 inc="dve")
            sq_w = [("ldx", ldx_now), ("act", INIT_ACT)]
            if span:
                sq_w.append(("pe", S["ssm_pe"]))
                sq_w.append(("dve", S["arena_dve"]))
            emit("scalar", sq_w,
                 lambda e, xt=xt, cb=cb: e.activation(arf[:, 0:2048], xt, AF.Square,
                                                      accum_out=lnsm[:, cb + 1:cb + 2]),
                 inc="act")
            emit("vector", [("act", now("act"))],
                 lambda e, cb=cb: e.tensor_scalar(lnsm[:, cb + 2:cb + 3],
                                                  lnsm[:, cb:cb + 1], 1.0 / H, None,
                                                  ALU.mult), inc="dve")
            emit("vector", [],
                 lambda e, cb=cb: e.tensor_tensor(lnsm[:, cb + 3:cb + 4],
                                                  lnsm[:, cb + 2:cb + 3],
                                                  lnsm[:, cb + 2:cb + 3], ALU.mult),
                 inc="dve")
            emit("vector", [],
                 lambda e, cb=cb: e.scalar_tensor_tensor(
                     lnsm[:, cb + 4:cb + 5], lnsm[:, cb + 1:cb + 2], 1.0 / H,
                     lnsm[:, cb + 3:cb + 4], ALU.mult, ALU.subtract), inc="dve")
            emit("scalar", [("dve", now("dve"))],
                 lambda e, cb=cb: e.activation(lnsm[:, cb + 5:cb + 6],
                                               lnsm[:, cb + 4:cb + 5], AF.Ln,
                                               bias=C["cvals"][:, 0:1]), inc="act")
            emit("scalar", [],
                 lambda e, cb=cb: e.activation(lnsm[:, cb + 5:cb + 6],
                                               lnsm[:, cb + 5:cb + 6], AF.Exp,
                                               scale=-0.5), inc="act")
            emit("vector", [("act", now("act"))],
                 lambda e, cb=cb: e.tensor_tensor(lnsm[:, cb + 6:cb + 7],
                                                  lnsm[:, cb + 2:cb + 3],
                                                  lnsm[:, cb + 5:cb + 6], ALU.mult),
                 inc="dve")
            emit("vector", [],
                 lambda e, cb=cb: e.tensor_scalar(lnsm[:, cb + 6:cb + 7],
                                                  lnsm[:, cb + 6:cb + 7], -1.0, None,
                                                  ALU.mult), inc="dve")
            slot = tq % 2
            xnw = [("dve", now("dve"))]
            if tq >= 2:
                xnw.append(("pe", xn_fill_pe[tq - 2]))
            emit("scalar", xnw,
                 lambda e, xt=xt, slot=slot, cb=cb: e.activation(
                     arb[:, 4096 + slot * 2048: 4096 + (slot + 1) * 2048], xt,
                     AF.Identity, bias=lnsm[:, cb + 6:cb + 7],
                     scale=lnsm[:, cb + 5:cb + 6]), inc="act")
            xn_act_ev.append(now("act"))
            xa = now("act")
            fp_last = 0
            for half in range(2):
                entries = []
                for j in range(8):
                    kc = half * 8 + j
                    entries.append((
                        arb[:, 4096 + slot * 2048 + kc * 128:
                            4096 + slot * 2048 + (kc + 1) * 128],
                        C["identb"][:],
                        xn_sb[:, kc * SPAN + tq * 128: kc * SPAN + (tq + 1) * 128],
                        None))
                fp, _ = tbb.batch(entries, extra=[("act", xa)] + xn_guard)
                fp_last = fp
            xn_fill_pe.append(fp_last)
        ln_done_dve = now("dve")
        if span == 0:
            dbg_tap("xn", xn_sb[:])

        gate_guard = [("pe", S["outproj_pe"])] if span else []
        conv_dve_ev = [0] * 34
        silu_act_ev = [0] * 34
        xtr_fill_pe = [0] * 34
        bank_evac = {0: 0, 1: 0}
        pb01_guard = ([("dve", S["pb01_dve"])] if span else [])
        dtraw_act = 0
        for og in range(34):
            cw = []
            if span == 0:
                cw.append(("cvA%d" % (og % 2), (og // 2 + 1) * 16))
            if og < 2:
                cw.extend(r_wd.freed[-2:])
            k = r_win.load(lambda e, slot, og=og: e.dma_start(
                out=wsAb[:, slot * 4096: slot * 4096 + 4096],
                in_=d_w_in_b[og]), engine="sync", extra_waits=cw)
            for oc in range(2):
                o = og * 2 + oc
                if o == 67:
                    continue
                bank = o % 2
                pw = [r_win.use_wait(k), ("dve", ln_done_dve)]
                if bank_evac[bank]:
                    pw.append(("act", bank_evac[bank]))
                else:
                    pw += pb01_guard
                bufbase = (k % 2) * 4096
                def mms(e, bufbase=bufbase, oc=oc, bank=bank):
                    last = None
                    for kc in range(KC):
                        last = e.matmul(
                            pb[bank][:],
                            wsAb[:, bufbase + kc * 256 + oc * 128:
                                 bufbase + kc * 256 + oc * 128 + 128],
                            xn_sb[:, kc * SPAN: kc * SPAN + 512],
                            start=(kc == 0), stop=(kc == KC - 1))
                    return last
                emit("tensor", pw, mms, inc="pe")
                if oc == 1 or o == 66:
                    r_win.free(("pe", now("pe")))
                mmw = now("pe")
                if o < 32:
                    fc = o
                    emit("scalar", [("pe", mmw)] + gate_guard,
                         lambda e, fc=fc, bank=bank: e.activation(
                             gate_sb[:, fc * SPAN:(fc + 1) * SPAN], pb[bank][:],
                             AF.Silu, bias=C["b_in"][:, fc:fc + 1]), inc="act")
                elif o < 66:
                    fc = o - 32
                    ring = fc % 3
                    hbct = lcf[:, ring * 516:(ring + 1) * 516]
                    gw = [("pe", mmw)]
                    if fc >= 3:
                        gw.append(("dve", conv_dve_ev[fc - 3]))
                    emit("scalar", gw,
                         lambda e, hbct=hbct, bank=bank, o=o: e.activation(
                             hbct[:, 3:515], pb[bank][:], AF.Identity,
                             bias=C["b_in"][:, o:o + 1]), inc="act")
                    hev = now("act")
                    def halo_fill(e, hbct=hbct, fc=fc, seq_start=seq_start):
                        if seq_start:
                            return e.memset(hbct[:, 0:3], 0.0)
                        return e.tensor_copy(hbct[:, 0:3], halo_sb[:, fc * 3:fc * 3 + 3])
                    emit("vector", [("act", hev)], halo_fill, inc="dve")
                    acc = arf[:, 4096 + (fc % 2) * 512: 4096 + (fc % 2) * 512 + 512]
                    accw = [("act", silu_act_ev[fc - 2])] if fc >= 2 else []
                    emit("vector", accw,
                         lambda e, hbct=hbct, acc=acc, fc=fc: e.tensor_scalar(
                             acc, hbct[:, 0:512], C["convw"][:, fc * 4:fc * 4 + 1],
                             None, ALU.mult), inc="dve")
                    for j in range(1, 4):
                        emit("vector", [],
                             lambda e, hbct=hbct, acc=acc, fc=fc, j=j: e.scalar_tensor_tensor(
                                 acc, hbct[:, j:512 + j],
                                 C["convw"][:, fc * 4 + j:fc * 4 + j + 1],
                                 acc, ALU.mult, ALU.add), inc="dve")
                    emit("vector", [],
                         lambda e, hbct=hbct, fc=fc: e.tensor_copy(
                             halo_sb[:, fc * 3:fc * 3 + 3], hbct[:, 512:515]), inc="dve")
                    conv_dve_ev[fc] = now("dve")
                    if fc < 32:
                        dst = arb[:, 10240 + (fc % 2) * 512: 10240 + (fc % 2) * 512 + 512]
                    elif fc == 32:
                        dst = bc_sb[:, 0:512]
                    else:
                        dst = bc_sb[:, 512:1024]
                    sw = [("dve", now("dve"))]
                    if 2 <= fc and fc - 2 < 32:
                        sw.append(("pe", xtr_fill_pe[fc - 2]))
                    emit("scalar", sw,
                         lambda e, acc=acc, dst=dst: e.activation(dst, acc, AF.Silu),
                         inc="act")
                    silu_act_ev[fc] = now("act")
                    if fc < 32:
                        entries = []
                        for tt in range(4):
                            entries.append((
                                arb[:, 10240 + (fc % 2) * 512 + tt * 128:
                                    10240 + (fc % 2) * 512 + (tt + 1) * 128],
                                C["identb"][:],
                                xtm_sb[:, tt * INTER + fc * 128: tt * INTER + (fc + 1) * 128],
                                None))
                        xg = [("act", silu_act_ev[fc])]
                        if span:
                            xg.append(("pe", S["ssm_pe"]))
                        fp, _ = tbb.batch(entries, extra=xg)
                        xtr_fill_pe[fc] = fp
                    elif fc == 32:
                        entries = []
                        for tt in range(4):
                            entries.append((
                                bc_sb[:, tt * 128:(tt + 1) * 128], C["identb"][:],
                                btm_sb[:, tt * 128:(tt + 1) * 128], None))
                        bg2 = [("act", silu_act_ev[32])]
                        if span:
                            bg2.append(("pe", S["ssm_pe"]))
                        tbb.batch(entries, extra=bg2)
                else:
                    emit("scalar", [("pe", mmw)],
                         lambda e, bank=bank: e.activation(
                             dtraw_sb[:, :], pb[bank][0:64, :], AF.Identity,
                             bias=C["bias66"][:]), inc="act")
                    dtraw_act = now("act")
                bank_evac[bank] = now("act")
        if span == 0:
            dbg_tap("gate", gate_sb[:])
            dbg_tap("xtm", xtm_sb[:])
            dbg_tap("bcfm", bc_sb[:])
            dbg_tap("dtraw", dtraw_sb[:])

        tbb.set_banks(["pb6", "pb0b", "pb1b"],
                      new_guards={"pb0b": [("act", bank_evac[0])] + pb01_guard,
                                  "pb1b": [("act", bank_evac[1])] + pb01_guard})
        gu_bank_guard = {2: [("dve", tbb.drain_of("pb2b")), ("act", S["hb2"][0]),
                             ("dve", S["hb2"][1])],
                         3: [("dve", tbb.drain_of("pb3b")), ("act", S["hb3"][0]),
                             ("dve", S["hb3"][1])],
                         4: [("dve", S["pb4_dve"])],
                         5: [("dve", S["pb5_dve"])]}
        gu_bank_act = {2: 0, 3: 0, 4: 0, 5: 0}
        gu_bank_dve = {2: 0, 3: 0, 4: 0, 5: 0}
        out_add_ev = [0, 0, 0, 0]
        outadd_guard_act = xn_act_ev[3]
        tsil_ev = [0, 0]
        dn_ev = {2: 0, 3: 0, 4: 0, 5: 0}
        for ih in range(2):
            for ig in range(16):
                igg = ih * 16 + ig
                xgw = r_wout.freed[-2:] if (ih == 0 and ig == 0) else ()
                kg = r_wgu.load(lambda e, slot, igg=igg: e.dma_start(
                    out=wsBb[:, slot * 4096: slot * 4096 + 4096],
                    in_=dr["w_g"][igg].rearrange("kc p o -> p kc o")),
                    extra_waits=xgw)
                gbuf = (kg % 2) * 4096
                for isub in range(2):
                    bg_ = 2 + isub
                    gw = [r_wgu.use_wait(kg)] + gu_bank_guard[bg_]
                    gu_bank_guard[bg_] = []
                    if gu_bank_act[bg_]:
                        gw.append(("act", gu_bank_act[bg_]))
                    if dn_ev[bg_]:
                        gw.append(("dve", dn_ev[bg_]))
                    def gmms(e, gbuf=gbuf, isub=isub, bg_=bg_):
                        last = None
                        for kc in range(KC):
                            last = e.matmul(
                                pb[bg_][:],
                                wsBb[:, gbuf + kc * 256 + isub * 128:
                                     gbuf + kc * 256 + isub * 128 + 128],
                                xn_sb[:, kc * SPAN: kc * SPAN + 512],
                                start=(kc == 0), stop=(kc == KC - 1))
                        return last
                    emit("tensor", gw, gmms, inc="pe")
                r_wgu.free(("pe", now("pe")))
                g_pe = now("pe")
                ku = r_wgu.load(lambda e, slot, igg=igg: e.dma_start(
                    out=wsBb[:, slot * 4096: slot * 4096 + 4096],
                    in_=dr["w_u"][igg].rearrange("kc p o -> p kc o")))
                ubuf = (ku % 2) * 4096
                for isub in range(2):
                    bu_ = 4 + isub
                    uw = [r_wgu.use_wait(ku)] + gu_bank_guard[bu_]
                    gu_bank_guard[bu_] = []
                    if gu_bank_dve[bu_]:
                        uw.append(("dve", gu_bank_dve[bu_]))
                    if dn_ev[bu_]:
                        uw.append(("dve", dn_ev[bu_]))
                    def umms(e, ubuf=ubuf, isub=isub, bu_=bu_):
                        last = None
                        for kc in range(KC):
                            last = e.matmul(
                                pb[bu_][:],
                                wsBb[:, ubuf + kc * 256 + isub * 128:
                                     ubuf + kc * 256 + isub * 128 + 128],
                                xn_sb[:, kc * SPAN: kc * SPAN + 512],
                                start=(kc == 0), stop=(kc == KC - 1))
                        return last
                    emit("tensor", uw, umms, inc="pe")
                r_wgu.free(("pe", now("pe")))
                u_pe = now("pe")
                for isub in range(2):
                    i_g = igg * 2 + isub
                    il = ig * 2 + isub
                    tw = [("pe", g_pe)]
                    if tsil_ev[isub]:
                        tw.append(("dve", tsil_ev[isub]))
                    emit("scalar", tw,
                         lambda e, isub=isub, i_g=i_g: e.activation(
                             tsil_sb[:, isub * 512:(isub + 1) * 512],
                             pb[2 + isub][:], AF.Silu,
                             scale=C["sg"][:, i_g:i_g + 1]), inc="act")
                    emit("vector", [("act", now("act")), ("pe", u_pe)],
                         lambda e, isub=isub, i_g=i_g, il=il: e.scalar_tensor_tensor(
                             fuse_t(il), pb[4 + isub][:], C["su"][:, i_g:i_g + 1],
                             tsil_sb[:, isub * 512:(isub + 1) * 512],
                             ALU.mult, ALU.mult), inc="dve")
                    tsil_ev[isub] = now("dve")
                    gu_bank_act[2 + isub] = now("act")
                    gu_bank_dve[4 + isub] = now("dve")
            last_fuse_dve = now("dve")
            if span == 0 and ih == 0:
                dbg_tap("fuse", arb[:, 0:16384])
            for fop in range(8):
                bA, bB = (2, 3) if fop % 2 == 0 else (4, 5)
                fw = [("dve", last_fuse_dve)]
                for b_ in (bA, bB):
                    if dn_ev[b_]:
                        fw.append(("dve", dn_ev[b_]))
                    else:
                        if gu_bank_act[b_]:
                            fw.append(("act", gu_bank_act[b_]))
                        if gu_bank_dve[b_]:
                            fw.append(("dve", gu_bank_dve[b_]))
                first = True
                for ihq2 in range(2):
                    ihq = ih * 2 + ihq2
                    xdw = (r_win.freed[-2:]
                           if (ih == 0 and fop == 0 and ihq2 == 0) else ())
                    kd = r_wd.load(lambda e, slot, fop=fop, ihq=ihq: e.dma_start(
                        out=wsAb[:, slot * 4096: slot * 4096 + 4096],
                        in_=dr["w_d"][fop, ihq].rearrange("ic p o -> p ic o")),
                        extra_waits=xdw)
                    dbuf = (kd % 2) * 4096
                    w = (list(fw) if first else []) + [r_wd.use_wait(kd)]
                    def dmms(e, dbuf=dbuf, bA=bA, bB=bB, first=first, ihq2=ihq2):
                        last = None
                        for fh, bk in ((0, bA), (1, bB)):
                            for ic in range(16):
                                last = e.matmul(
                                    pb[bk][:],
                                    wsAb[:, dbuf + ic * 256 + fh * 128:
                                         dbuf + ic * 256 + fh * 128 + 128],
                                    fuse_t(ihq2 * 16 + ic),
                                    start=(first and ic == 0),
                                    stop=(ihq2 == 1 and ic == 15))
                        return last
                    emit("tensor", w, dmms, inc="pe")
                    r_wd.free(("pe", now("pe")))
                    first = False
                d_pe = now("pe")
                for fh, bk in ((0, bA), (1, bB)):
                    fo = fop * 2 + fh
                    pslot = fo % 2
                    emit("vector", [("pe", d_pe)],
                         lambda e, bk=bk, fo=fo, pslot=pslot: e.tensor_scalar(
                             ffp_sb[:, pslot * 512:(pslot + 1) * 512],
                             pb[bk][:], C["sd"][:, fo:fo + 1], None, ALU.mult),
                         inc="dve")
                    pev = now("dve")
                    dn_ev[bk] = pev
                    entries = []
                    for tq in range(4):
                        def evf(e, slot, tq=tq, fo=fo):
                            return e.tensor_tensor(
                                out_sb[:, tq * H + fo * 128: tq * H + (fo + 1) * 128],
                                out_sb[:, tq * H + fo * 128: tq * H + (fo + 1) * 128],
                                slot, ALU.add)
                        entries.append((
                            ffp_sb[:, pslot * 512 + tq * 128: pslot * 512 + (tq + 1) * 128],
                            C["identb"][:], None, evf))
                    _, dd = tbb.batch(entries,
                                      extra=[("dve", pev), ("act", outadd_guard_act),
                                             ("ldx", ldx_now)])
                    for tq in range(4):
                        out_add_ev[tq] = dd
        S["ffn_pe"] = now("pe")
        ffn_dn_ev = dict(dn_ev)

        tbb.set_banks(["pb6"])
        for cl in range(2):
            DT = A_DT
            dt_fm = arf[0:64, DT:DT + 256]
            ca_fm = arf[0:64, DT + 256:DT + 512]
            lndt = arf[0:64, DT + 512:DT + 768]
            scr1 = arf[0:64, DT + 768:DT + 1024]
            scr2 = arf[0:64, DT + 1024:DT + 1280]
            scr3 = arf[0:64, DT + 1280:DT + 1536]
            dts = dtraw_sb[:, cl * 256:(cl + 1) * 256]
            dtg = [("act", dtraw_act), ("pe", now("pe"))]
            emit("scalar", dtg,
                 lambda e, dts=dts, scr1=scr1: e.activation(scr1, dts, AF.Abs),
                 inc="act")
            emit("scalar", [],
                 lambda e, scr1=scr1, scr2=scr2: e.activation(scr2, scr1, AF.Exp,
                                                              scale=-1.0), inc="act")
            emit("scalar", [],
                 lambda e, scr2=scr2, scr3=scr3: e.activation(
                     scr3, scr2, AF.Ln, bias=C["cvals"][0:64, 1:2]), inc="act")
            emit("vector", [("act", now("act"))],
                 lambda e, dts=dts, scr2=scr2: e.tensor_scalar(
                     scr2, dts, 0.0, None, ALU.max), inc="dve")
            emit("vector", [],
                 lambda e, dt_fm=dt_fm, scr3=scr3, scr2=scr2: e.tensor_tensor(
                     dt_fm, scr3, scr2, ALU.add), inc="dve")
            emit("scalar", [("dve", now("dve"))],
                 lambda e, lndt=lndt, dt_fm=dt_fm: e.activation(lndt, dt_fm, AF.Ln),
                 inc="act")
            emit("vector", [],
                 lambda e, scr1=scr1, dt_fm=dt_fm: e.tensor_scalar(
                     scr1, dt_fm, C["acol"][:], None, ALU.mult), inc="dve")
            emit("vector", [],
                 lambda e, ca_fm=ca_fm, scr1=scr1: e.tensor_tensor_scan(
                     ca_fm, scr1, scr1, 0.0, ALU.add, ALU.bypass), inc="dve")
            ca_dve = now("dve")
            emit("vector", [("act", now("act"))],
                 lambda e, scr2=scr2, lndt=lndt, ca_fm=ca_fm: e.tensor_tensor(
                     scr2, lndt, ca_fm, ALU.subtract), inc="dve")
            emit("scalar", [("dve", now("dve"))],
                 lambda e, scr3=scr3, ca_fm=ca_fm: e.activation(
                     scr3, ca_fm, AF.Exp, scale=-1.0, bias=ca_fm[:, 255:256]),
                 inc="act")
            emit("vector", [("act", now("act"))],
                 lambda e, scr3=scr3, dt_fm=dt_fm: e.tensor_tensor(
                     scr3, scr3, dt_fm, ALU.mult), inc="dve")
            emit("scalar", [],
                 lambda e, ca_fm=ca_fm: e.activation(etcol_sb[:], ca_fm[:, 255:256],
                                                     AF.Exp), inc="act")
            sm_dve = now("dve")
            sm_act = now("act")
            CATM = A_CATM
            tw = [("dve", sm_dve), ("act", sm_act)]
            if pb7_last:
                tw.append(("dve", pb7_last))
            def dtfill(e, ca_fm=ca_fm, scr2=scr2, scr3=scr3):
                for q in range(2):
                    e.transpose(pb7[:, q * 64:(q + 1) * 64],
                                ca_fm[:, q * 128:(q + 1) * 128], C["identf"][0:64, 0:64])
                for q in range(2):
                    e.transpose(pb7[:, 128 + q * 64:128 + (q + 1) * 64],
                                scr2[:, q * 128:(q + 1) * 128], C["identf"][0:64, 0:64])
                for q in range(2):
                    e.transpose(pb7[:, 256 + q * 64:256 + (q + 1) * 64],
                                scr3[:, q * 128:(q + 1) * 128], C["identf"][0:64, 0:64])
                return e.transpose(pb7[0:1, 384:448], etcol_sb[:],
                                   C["identf"][0:64, 0:64])
            emit("tensor", tw, dtfill, inc="pe")
            dtf_pe = now("pe")
            def dtdrain(e, dtf_pe=dtf_pe):
                e.wait_ge(sems["pe"], dtf_pe)
                e.tensor_copy(arf[:, A_CATM:A_CATM + 384], pb7[:, 0:384])
                return e.tensor_copy(etrow_sb[0:1, :], pb7[0:1, 384:448])
            emit("vector", [], dtdrain, inc="dve")
            catm_dve = now("dve")
            emit("scalar", [("dve", catm_dve)],
                 lambda e: e.activation(arf[:, A_CATM + 384:A_CATM + 512],
                                        arf[:, A_CATM:A_CATM + 128], AF.Exp), inc="act")
            eca_act = now("act")
            emit("tensor", [("dve", catm_dve)],
                 lambda e: e.matmul(pb7[:, 448:512], C["onesr"][:], etrow_sb[0:1, :],
                                    start=True, stop=True, skip_group_check=True),
                 inc="pe")
            emit("vector", [("pe", now("pe"))],
                 lambda e: e.tensor_copy(arf[:, A_CATM + 512:A_CATM + 576],
                                         pb7[:, 448:512]), inc="dve")
            etb_dve = now("dve")
            pb7_last = etb_dve

            gmw = [("act", silu_act_ev[33]), ("act", now("act")), ("dve", now("dve"))]
            def gmm(e, cl=cl):
                e.matmul(pb[4][:, 0:256], bc_sb[:, cl * 256:cl * 256 + 128],
                         bc_sb[:, 512 + cl * 256:512 + (cl + 1) * 256],
                         start=True, stop=False, skip_group_check=True)
                return e.matmul(pb[4][:, 256:512],
                                bc_sb[:, cl * 256 + 128:cl * 256 + 256],
                                bc_sb[:, 512 + cl * 256:512 + (cl + 1) * 256],
                                start=False, stop=True, skip_group_check=True)
            emit("tensor", gmw, gmm, inc="pe")
            emit("vector", [("pe", now("pe"))],
                 lambda e: e.tensor_tensor(arf[:, A_GM0:A_GM0 + 256], pb[4][:, 0:256],
                                           C["mask0"][:], ALU.mult), inc="dve")
            emit("vector", [],
                 lambda e: e.tensor_tensor(arf[:, A_GM1:A_GM1 + 128], pb[4][:, 384:512],
                                           C["mask0"][:, 0:128], ALU.mult), inc="dve")
            gm_dve = now("dve")
            if span == 0 and cl == 0:
                dbg_tap("gm", arf[:, 0:384])

            if seq_start and cl == 0:
                emit("vector", [("pe", S["ssm_pe"])] if span else [],
                     lambda e: e.memset(rbf_sb[:], 0.0), inc="dve")
                for h in range(NH):
                    rupd_ev[h] = now("dve")

            argE_ev = [0] * NH
            sx_ev = [0] * NH
            toff_ev = [0] * NH
            yr_ev = [0] * NH
            ymm_ev = [0] * NH
            ytr_entries = []
            ytr_fcs = []
            for h in range(NH):
                r = h % 2
                g4 = h // 4
                if h % 4 == 0:
                    cw = [("dve", ca_dve)]
                    if car_k >= 2:
                        cw.append(("pe", car_free_pe[car_k - 2]))
                    emit("gpsimd", cw,
                         lambda e, g4=g4, ca_fm=ca_fm: e.dma_start(
                             out=arf[0:1, A_CAR + (g4 % 2) * 1024:
                                     A_CAR + (g4 % 2) * 1024 + 1024],
                             in_=ca_fm[g4 * 4:(g4 + 1) * 4, :]),
                         inc="ldcr%d" % (car_k % 2), inc_by=16)
                    car_k += 1
                cabbank = pb[4] if h % 2 == 0 else pb7
                cw = [("ldcr%d" % ((car_k - 1) % 2), ((car_k - 1) // 2 + 1) * 16)]
                if h >= 2:
                    cw.append(("dve", argE_ev[h - 2]))
                elif h == 0:
                    cw.append(("dve", gm_dve))
                else:
                    cw.append(("dve", etb_dve))
                emit("tensor", cw,
                     lambda e, cabbank=cabbank, g4=g4, h=h: e.matmul(
                         cabbank[:, 0:256], C["onesr"][:],
                         arf[0:1, A_CAR + (g4 % 2) * 1024 + (h % 4) * 256:
                             A_CAR + (g4 % 2) * 1024 + (h % 4) * 256 + 256],
                         start=True, stop=True, skip_group_check=True), inc="pe")
                if h % 4 == 3:
                    car_free_pe.append(now("pe"))
                cab_pe = now("pe")
                bcol0 = arf[:, A_CATM + 128 + h:A_CATM + 128 + h + 1]
                bcol1 = arf[:, A_CATM + 192 + h:A_CATM + 192 + h + 1]
                emit("vector", [("pe", cab_pe)],
                     lambda e, cabbank=cabbank, r=r, bcol0=bcol0: e.tensor_scalar(
                         arf[:, A_ARG + r * 384:A_ARG + r * 384 + 256],
                         cabbank[:, 0:256], bcol0, CLAMP, ALU.add, ALU.min), inc="dve")
                emit("vector", [],
                     lambda e, cabbank=cabbank, r=r, bcol1=bcol1: e.tensor_scalar(
                         arf[:, A_ARG + r * 384 + 256:A_ARG + r * 384 + 384],
                         cabbank[:, 128:256], bcol1, CLAMP, ALU.add, ALU.min),
                     inc="dve")
                argE_ev[h] = now("dve")
                ew = [("dve", argE_ev[h])]
                if h == 0:
                    ew.append(("act", eca_act))
                emit("scalar", ew,
                     lambda e, r=r: e.activation(
                         arf[:, A_E + r * 384:A_E + r * 384 + 256],
                         arf[:, A_ARG + r * 384:A_ARG + r * 384 + 256], AF.Exp),
                     inc="act")
                emit("scalar", [],
                     lambda e, r=r: e.activation(
                         arf[:, A_E + r * 384 + 256:A_E + r * 384 + 384],
                         arf[:, A_ARG + r * 384 + 256:A_ARG + r * 384 + 384], AF.Exp),
                     inc="act")
                e_act = now("act")
                sw = [("act", e_act)]
                if h >= 2:
                    sw.append(("pe", ymm_ev[h - 2]))
                emit("vector", sw,
                     lambda e, r=r: e.tensor_tensor(
                         ssb[:, r * 384:r * 384 + 256], arf[:, A_GM0:A_GM0 + 256],
                         arf[:, A_E + r * 384:A_E + r * 384 + 256], ALU.mult),
                     inc="dve")
                emit("vector", [],
                     lambda e, r=r: e.tensor_tensor(
                         ssb[:, r * 384 + 256:r * 384 + 384], arf[:, A_GM1:A_GM1 + 128],
                         arf[:, A_E + r * 384 + 256:A_E + r * 384 + 384], ALU.mult),
                     inc="dve")
                for q in range(2):
                    tt = 2 * cl + q
                    wcol = arf[:, A_CATM + 256 + q * 64 + h:A_CATM + 256 + q * 64 + h + 1]
                    emit("vector", [],
                         lambda e, r=r, q=q, tt=tt, wcol=wcol, h=h: e.tensor_scalar(
                             xpb[:, (r * 2 + q) * 64:(r * 2 + q) * 64 + 64],
                             xtm_sb[:, tt * INTER + h * 64: tt * INTER + h * 64 + 64],
                             wcol, None, ALU.mult), inc="dve")
                sx_ev[h] = now("dve")
                B = pb[2 + h % 2]
                yw = [("dve", sx_ev[h]), ("dve", rupd_ev[h])]
                if h >= 2:
                    yw.append(("act", toff_ev[h - 2]))
                    yw.append(("dve", yr_ev[h - 2]))
                else:
                    ga, gd = S["hb2"] if h % 2 == 0 else S["hb3"]
                    if ga:
                        yw.append(("act", ga))
                    if gd:
                        yw.append(("dve", gd))
                    fe = ffn_dn_ev.get(2 + h % 2, 0)
                    if fe:
                        yw.append(("dve", fe))
                def ymms(e, B=B, r=r, cl=cl, h=h):
                    tt0, tt1 = 2 * cl, 2 * cl + 1
                    hc = h * 64
                    e.matmul(B[:, 0:64], ssb[:, r * 384:r * 384 + 128],
                             xtm_sb[:, tt0 * INTER + hc:tt0 * INTER + hc + 64],
                             start=True, stop=True, skip_group_check=True)
                    e.matmul(B[:, 64:128], ssb[:, r * 384 + 128:r * 384 + 256],
                             xtm_sb[:, tt0 * INTER + hc:tt0 * INTER + hc + 64],
                             start=False, stop=False, skip_group_check=True)
                    e.matmul(B[:, 64:128], ssb[:, r * 384 + 256:r * 384 + 384],
                             xtm_sb[:, tt1 * INTER + hc:tt1 * INTER + hc + 64],
                             start=False, stop=True, skip_group_check=True)
                    e.matmul(B[:, 128:192],
                             bc_sb[:, 512 + cl * 256:512 + cl * 256 + 128],
                             rbf_sb[:, hc:hc + 64],
                             start=False, stop=True, skip_group_check=True)
                    e.matmul(B[:, 192:256],
                             bc_sb[:, 512 + cl * 256 + 128:512 + cl * 256 + 256],
                             rbf_sb[:, hc:hc + 64],
                             start=False, stop=True, skip_group_check=True)
                    e.matmul(B[:, 256:320],
                             btm_sb[:, (2 * cl) * 128:(2 * cl) * 128 + 128],
                             xpb[:, (r * 2) * 64:(r * 2) * 64 + 64],
                             start=False, stop=False, skip_group_check=True)
                    return e.matmul(
                        B[:, 256:320],
                        btm_sb[:, (2 * cl + 1) * 128:(2 * cl + 1) * 128 + 128],
                        xpb[:, (r * 2 + 1) * 64:(r * 2 + 1) * 64 + 64],
                        start=False, stop=True, skip_group_check=True)
                emit("tensor", yw, ymms, inc="pe")
                ymm_ev[h] = now("pe")
                for q in range(2):
                    ecacol = arf[:, A_CATM + 384 + q * 64 + h:
                                 A_CATM + 384 + q * 64 + h + 1]
                    emit("scalar", [("pe", ymm_ev[h])] if q == 0 else [],
                         lambda e, B=B, q=q, r=r, ecacol=ecacol: e.activation(
                             arf[:, A_TOFF + (r * 2 + q) * 64:
                                 A_TOFF + (r * 2 + q) * 64 + 64],
                             B[:, 128 + q * 64:128 + (q + 1) * 64], AF.Identity,
                             scale=ecacol), inc="act")
                toff_ev[h] = now("act")
                p = h // 2
                pslot = p % 4
                for q in range(2):
                    tslot = arf[:, A_TOFF + (r * 2 + q) * 64:A_TOFF + (r * 2 + q) * 64 + 64]
                    emit("vector", [("act", toff_ev[h])] if q == 0 else [],
                         lambda e, B=B, q=q, tslot=tslot: e.tensor_tensor(
                             tslot, tslot, B[:, q * 64:(q + 1) * 64], ALU.add),
                         inc="dve")
                    tt = 2 * cl + q
                    emit("vector", [],
                         lambda e, tslot=tslot, tt=tt, h=h, pslot=pslot, q=q:
                         e.scalar_tensor_tensor(
                             ybuf[:, pslot * 256 + q * 128 + (h % 2) * 64:
                                  pslot * 256 + q * 128 + (h % 2) * 64 + 64],
                             xtm_sb[:, tt * INTER + h * 64: tt * INTER + h * 64 + 64],
                             C["DB"][:, h:h + 1], tslot, ALU.mult, ALU.add), inc="dve")
                emit("vector", [],
                     lambda e, B=B, h=h: e.scalar_tensor_tensor(
                         rbf_sb[:, h * 64:h * 64 + 64], rbf_sb[:, h * 64:h * 64 + 64],
                         arf[:, A_CATM + 512 + h:A_CATM + 512 + h + 1],
                         B[:, 256:320], ALU.mult, ALU.add), inc="dve")
                rupd_ev[h] = now("dve")
                yr_ev[h] = now("dve")
                if h % 2 == 1:
                    fc = p
                    for q in range(2):
                        def evf(e, slot, fc=fc, cl=cl, q=q):
                            gsl = gate_sb[:, fc * SPAN + cl * 256 + q * 128:
                                          fc * SPAN + cl * 256 + (q + 1) * 128]
                            return e.scalar_tensor_tensor(
                                gsl, slot, C["nw"][:, fc:fc + 1], gsl,
                                ALU.mult, ALU.mult)
                        ytr_entries.append((
                            ybuf[:, pslot * 256 + q * 128: pslot * 256 + (q + 1) * 128],
                            C["identb"][:], None, evf))
                    ytr_fcs.append(fc)
                    if len(ytr_fcs) == 2:
                        yev = now("dve")
                        _, pd = tbb.batch(ytr_entries, extra=[("dve", yev)])
                        for fcq in ytr_fcs:
                            emit("scalar", [("dve", pd)],
                                 lambda e, fc=fcq, cl=cl: e.activation(
                                     arf[:, A_G2 + (fc % 2) * 256:
                                         A_G2 + (fc % 2) * 256 + 256],
                                     gate_sb[:, fc * SPAN + cl * 256:
                                             fc * SPAN + (cl + 1) * 256],
                                     AF.Square), inc="act")
                            sqe = now("act")
                            rmsw = [("act", sqe)]
                            if fcq == 0:
                                rmsw.append(("dve", max(S["pb5_dve"],
                                                        ffn_dn_ev.get(5, 0),
                                                        gu_bank_dve.get(5, 0))))
                            emit("tensor", rmsw,
                                 lambda e, fc=fcq: e.matmul(
                                     pb[5][0:1, 0:256], C["onescol"][:],
                                     arf[:, A_G2 + (fc % 2) * 256:
                                         A_G2 + (fc % 2) * 256 + 256],
                                     start=(fc == 0), stop=(fc == 31),
                                     skip_group_check=True), inc="pe")
                        ytr_entries = []
                        ytr_fcs = []
            emit("vector", [("pe", now("pe"))],
                 lambda e: e.tensor_scalar(rstdrow_sb[0:1, :], pb[5][0:1, 0:256],
                                           1.0 / INTER, None, ALU.mult), inc="dve")
            S["pb5_dve"] = now("dve")
            emit("scalar", [("dve", now("dve"))],
                 lambda e: e.activation(rstdrow_sb[0:1, :], rstdrow_sb[0:1, :],
                                        AF.Ln, bias=C["cvals"][0:1, 0:1]), inc="act")
            emit("scalar", [],
                 lambda e: e.activation(rstdrow_sb[0:1, :], rstdrow_sb[0:1, :],
                                        AF.Exp, scale=-0.5), inc="act")
            emit("tensor", [("act", now("act")), ("dve", argE_ev[NH - 1]),
                            ("dve", pb7_last)],
                 lambda e: e.matmul(pb7[:, 0:256], C["onesr"][:], rstdrow_sb[0:1, :],
                                    start=True, stop=True, skip_group_check=True),
                 inc="pe")
            emit("vector", [("pe", now("pe"))],
                 lambda e, cl=cl: e.tensor_copy(
                     arf[:, A_RSTD + cl * 256:A_RSTD + (cl + 1) * 256], pb7[:, 0:256]),
                 inc="dve")
            pb7_last = now("dve")
            S["hb2"] = (toff_ev[NH - 2], yr_ev[NH - 2])
            S["hb3"] = (toff_ev[NH - 1], yr_ev[NH - 1])
            S["pb4_dve"] = argE_ev[NH - 2]
        S["ssm_pe"] = now("pe")
        rstd_dve = now("dve")
        S["arena_dve"] = rstd_dve
        if span == 0:
            dbg_tap("g", gate_sb[:])
            dbg_tap("rstd", arf[:, A_RSTD:A_RSTD + 512])

        tbb.set_banks(["pb6", "pb2b", "pb3b"],
                      new_guards={"pb2b": [("act", S["hb2"][0]), ("dve", S["hb2"][1])],
                                  "pb3b": [("act", S["hb3"][0]), ("dve", S["hb3"][1])]})
        op_bank_dve = {0: 0, 1: 0}
        ytr_pb01 = [("dve", tbb.drain_of("pb0b")), ("dve", tbb.drain_of("pb1b"))]
        for og in range(16):
            cw = []
            if span == 0:
                cw.append(("cvB%d" % (og % 2), (og // 2 + 1) * 16))
            if og < 2:
                cw.extend(r_wgu.freed[-2:])
            k = r_wout.load(lambda e, slot, og=og: e.dma_start(
                out=wsBb[:, slot * 4096: slot * 4096 + 4096],
                in_=d_w_out_b[og]), engine="sync", extra_waits=cw)
            bank = og % 2
            pw = [r_wout.use_wait(k), ("dve", rstd_dve), ("act", bank_evac[bank])]
            pw += [w for w in ytr_pb01 if w[1] > 0]
            if op_bank_dve[bank]:
                pw.append(("dve", op_bank_dve[bank]))
            obuf = (k % 2) * 4096
            def omms(e, obuf=obuf, bank=bank):
                last = None
                for kc in range(32):
                    last = e.matmul(
                        pb[bank][:],
                        wsBb[:, obuf + kc * 128: obuf + (kc + 1) * 128],
                        gate_sb[:, kc * SPAN: kc * SPAN + 512],
                        start=(kc == 0), stop=(kc == 31))
                return last
            emit("tensor", pw, omms, inc="pe")
            r_wout.free(("pe", now("pe")))
            o_pe = now("pe")
            for cl in range(2):
                fsl = arf[:, A_FFOUT + cl * 256:A_FFOUT + (cl + 1) * 256]
                emit("vector", [("pe", o_pe)],
                     lambda e, bank=bank, cl=cl, fsl=fsl: e.tensor_tensor(
                         fsl, pb[bank][:, cl * 256:(cl + 1) * 256],
                         arf[:, A_RSTD + cl * 256:A_RSTD + (cl + 1) * 256], ALU.mult),
                     inc="dve")
                emit("vector", [],
                     lambda e, og=og, fsl=fsl: e.tensor_scalar(
                         fsl, fsl, C["bout"][:, og:og + 1], None, ALU.add), inc="dve")
                pev = now("dve")
                tw2 = [("dve", pev), ("dve", pb7_last)]
                def ofill(e, fsl=fsl):
                    e.transpose(pb7[:, 0:128], fsl[:, 0:128], C["identf"][:])
                    return e.transpose(pb7[:, 128:256], fsl[:, 128:256], C["identf"][:])
                emit("tensor", tw2, ofill, inc="pe")
                ofp = now("pe")
                def odrain(e, ofp=ofp, cl=cl, og=og):
                    e.wait_ge(sems["pe"], ofp)
                    last = None
                    for q in range(2):
                        tq = cl * 2 + q
                        last = e.tensor_tensor(
                            out_sb[:, tq * H + og * 128: tq * H + (og + 1) * 128],
                            out_sb[:, tq * H + og * 128: tq * H + (og + 1) * 128],
                            pb7[:, q * 128:(q + 1) * 128], ALU.add)
                    return last
                emit("vector", [], odrain, inc="dve")
                pb7_last = now("dve")
                out_add_ev[cl * 2] = pb7_last
                out_add_ev[cl * 2 + 1] = pb7_last
            op_bank_dve[bank] = now("dve")
        S["outproj_pe"] = now("pe")
        S["pb01_dve"] = max(op_bank_dve[0], op_bank_dve[1])
        if span == 0:
            dbg_tap("outsb", out_sb[:])

        for tq in range(4):
            emit("sync", [("dve", out_add_ev[tq])],
                 lambda e, tq=tq, row0=row0: e.dma_start(
                     out=dout[row0 + tq * 128: row0 + (tq + 1) * 128, :],
                     in_=out_sb[:, tq * H:(tq + 1) * H]), inc="st", inc_by=16)

    total_st = now("st")
    total_dbg = now("dbg")
    def fin(e):
        if total_dbg:
            e.wait_ge(sems["dbg"], total_dbg)
        e.wait_ge(sems["st"], total_st)
        return e.nop()
    emit("sync", [], fin)

    nops = {k: sum(1 for en, _ in ops if en == k)
            for k in ["tensor", "vector", "scalar", "gpsimd", "sync"]}
    print("op groups per engine:", nops)

    with nc.Block() as block:
        @block.tensor
        def _(e):
            for en, f in ops:
                if en == "tensor":
                    f(e)

        @block.vector
        def _(e):
            for en, f in ops:
                if en == "vector":
                    f(e)

        @block.scalar
        def _(e):
            for en, f in ops:
                if en == "scalar":
                    f(e)

        @block.gpsimd
        def _(e):
            for en, f in ops:
                if en == "gpsimd":
                    f(e)

        @block.sync
        def _(e):
            for en, f in ops:
                if en == "sync":
                    f(e)

    ctx.close()
    return nc


def kernel(**inputs):
    g = prep_host(inputs)
    x = _f32(inputs["L0_attn_hidden_states"])
    nc = build_program()
    in_maps = []
    for core in range(NCORES):
        m = dict(g)
        m["x"] = np.ascontiguousarray(
            x[core * SEQ_PER_CORE:(core + 1) * SEQ_PER_CORE].reshape(SEQ_PER_CORE * L, H))
        in_maps.append(m)
    res = run_bass_kernel_spmd(nc, in_maps, core_ids=list(range(NCORES)))
    out = np.stack([np.asarray(res.results[c]["out"]).reshape(SEQ_PER_CORE, L, H)
                    for c in range(NCORES)])
    return out.reshape(16, L, H)


if __name__ == "__main__":
    import reference as refmod
    ins = refmod.setup_inputs(0)
    o = kernel(**{k: np.asarray(v) for k, v in ins.items()})
    print(o.shape, o.dtype)


# version 04
# speedup vs baseline: 1.4526x; 1.2253x over previous
import numpy as np
import ml_dtypes
from contextlib import ExitStack

import concourse.bass as bass
import concourse.mybir as mybir
from concourse.bass_utils import run_bass_kernel_spmd

F32 = mybir.dt.float32
BF16 = mybir.dt.bfloat16
I8 = mybir.dt.int8
U8 = mybir.dt.uint8
AF = mybir.ActivationFunctionType
ALU = mybir.AluOpType
AXX = mybir.AxisListType.X

NCORES = 8
L = 2048
H = 2048
NH = 64
INTER = 4096
SEQ_PER_CORE = 2
SPAN = 512
NSPAN = SEQ_PER_CORE * L // SPAN
KC = H // 128
OCH = 68
CLAMP = 10.0

DBG = {}


def _f32(x):
    return np.ascontiguousarray(np.asarray(x, dtype=np.float32))


def prep_host(inputs):
    g = {}
    w_in = _f32(inputs["L0_attn_in_proj_w"])
    w_in_p = np.zeros((OCH * 128, H), np.float32)
    w_in_p[:8512] = w_in
    g["w_in"] = np.ascontiguousarray(
        w_in_p.T.reshape(KC, 128, 34, 256).transpose(2, 0, 1, 3))
    b_in = np.zeros((OCH * 128,), np.float32)
    b_in[:8512] = _f32(inputs["L0_attn_in_proj_b"])
    g["b_in"] = np.ascontiguousarray(b_in.reshape(OCH, 128).T)
    g["convw"] = np.ascontiguousarray(
        _f32(inputs["L0_attn_conv_w"]).reshape(34, 128, 4).transpose(1, 0, 2).reshape(128, 136))
    g["dtb"] = _f32(inputs["L0_attn_dt_bias"]).reshape(64, 1)
    g["alog"] = _f32(inputs["L0_attn_A_log"]).reshape(64, 1)
    g["d_row"] = _f32(inputs["L0_attn_D"]).reshape(1, 64)
    g["nw"] = np.ascontiguousarray(_f32(inputs["L0_attn_norm_w"]).reshape(32, 128).T)
    g["w_out"] = np.ascontiguousarray(
        _f32(inputs["L0_attn_out_proj_w"]).T.reshape(32, 128, 16, 128).transpose(2, 0, 1, 3))
    g["b_out"] = np.ascontiguousarray(_f32(inputs["L0_attn_out_proj_b"]).reshape(16, 128).T)
    g["w_g"] = np.ascontiguousarray(
        np.asarray(inputs["L0_ffn_w_gate"]).reshape(KC, 128, 32, 256).transpose(2, 0, 1, 3))
    g["w_u"] = np.ascontiguousarray(
        np.asarray(inputs["L0_ffn_w_up"]).reshape(KC, 128, 32, 256).transpose(2, 0, 1, 3))
    g["w_d"] = np.ascontiguousarray(
        np.asarray(inputs["L0_ffn_w_down"]).reshape(4, 16, 128, 8, 256).transpose(3, 0, 1, 2, 4))
    g["s_g"] = np.ascontiguousarray(_f32(inputs["L0_ffn_s_gate"]).reshape(64, 128).T)
    g["s_u"] = np.ascontiguousarray(_f32(inputs["L0_ffn_s_up"]).reshape(64, 128).T)
    g["s_d"] = np.ascontiguousarray(_f32(inputs["L0_ffn_s_down"]).reshape(16, 128).T)
    g["ident_b"] = np.eye(128, dtype=ml_dtypes.bfloat16)
    g["ident_f"] = np.eye(128, dtype=np.float32)
    g["onesr"] = np.ones((1, 128), np.float32)
    g["onescol"] = np.ones((128, 1), np.float32)
    s = np.arange(128)[:, None]
    t = np.arange(256)[None, :]
    g["mask0"] = (t >= s).astype(ml_dtypes.bfloat16)
    cv = np.zeros((128, 2), np.float32)
    cv[:, 0] = 1e-5
    cv[:, 1] = 1.0
    g["cvals"] = cv
    return g


def build_program():
    nc = bass.Bass("TRN2", target_bir_lowering=False, debug=False)

    dx = nc.dram_tensor("x", [SEQ_PER_CORE * L, H], F32, kind="ExternalInput").ap()
    dout = nc.dram_tensor("out", [SEQ_PER_CORE * L, H], F32, kind="ExternalOutput").ap()
    dr = {}
    for n, shape, dt in [
        ("w_in", [34, KC, 128, 256], F32), ("b_in", [128, OCH], F32),
        ("convw", [128, 136], F32), ("dtb", [64, 1], F32), ("alog", [64, 1], F32),
        ("d_row", [1, 64], F32), ("nw", [128, 32], F32),
        ("w_out", [16, 32, 128, 128], F32), ("b_out", [128, 16], F32),
        ("w_g", [32, KC, 128, 256], I8), ("w_u", [32, KC, 128, 256], I8),
        ("w_d", [8, 4, 16, 128, 256], I8),
        ("s_g", [128, 64], F32), ("s_u", [128, 64], F32), ("s_d", [128, 16], F32),
        ("ident_b", [128, 128], BF16), ("ident_f", [128, 128], F32),
        ("onesr", [1, 128], F32), ("onescol", [128, 1], F32),
        ("mask0", [128, 256], BF16), ("cvals", [128, 2], F32),
    ]:
        dr[n] = nc.dram_tensor(n, shape, dt, kind="ExternalInput").ap()
    d_w_in_b = nc.dram_tensor("w_in_b", [34, 128, 4096], BF16, kind="Internal").ap()
    d_w_out_b = nc.dram_tensor("w_out_b", [16, 128, 4096], BF16, kind="Internal").ap()
    ddbg = {n: nc.dram_tensor("dbg_" + n, list(s), F32, kind="ExternalOutput").ap()
            for n, s in DBG.items()}

    ctx = ExitStack()
    sb = lambda n, s, d: ctx.enter_context(nc.sbuf_tensor(n, s, d))
    psb = lambda n, s, d: ctx.enter_context(nc.psum_tensor(n, s, d))

    C = {}
    for n, shape, dt in [
        ("identb", [128, 128], BF16), ("identf", [128, 128], F32),
        ("onesr", [1, 128], F32), ("onescol", [128, 1], F32),
        ("mask0", [128, 256], BF16), ("b_in", [128, OCH], F32),
        ("convw", [128, 136], F32), ("dtb", [64, 1], F32), ("alog", [64, 1], F32),
        ("d_row", [1, 64], F32), ("nw", [128, 32], F32), ("bout", [128, 16], F32),
        ("sg", [128, 64], F32), ("su", [128, 64], F32), ("sd", [128, 16], F32),
        ("acol", [64, 1], F32), ("bias66", [64, 1], F32),
        ("DB", [128, 64], F32), ("cvals", [128, 2], F32),
    ]:
        C[n] = sb("c_" + n, shape, dt)

    xn_sb = sb("xn", [128, KC * SPAN], BF16)
    gate_sb = sb("gate", [128, 32 * SPAN], BF16)
    xtm_sb = sb("xtm", [128, 4 * INTER], BF16)
    out_sb = sb("outtm", [128, 4 * H], F32)
    rbf_sb = sb("rbf", [128, NH * 64], BF16)
    arena = sb("arena", [128, 32768], U8)
    arf = arena[:].bitcast(F32)
    arb = arena[:].bitcast(BF16)
    lnconv = sb("lnconv", [128, 6208], U8)
    lcf = lnconv[:].bitcast(F32)
    ssb = sb("ssb", [128, 2 * 384], BF16)
    ybuf = sb("ybuf", [128, 4 * 256], BF16)
    xpb = sb("xpb", [128, 4 * 64], BF16)
    bc_sb = sb("bcfm", [128, 2 * SPAN], BF16)
    btm_sb = sb("btm", [128, 4 * 128], BF16)
    dtraw_sb = sb("dtraw", [64, SPAN], F32)
    halo_sb = sb("halo", [128, 34 * 3], F32)
    ffp_sb = sb("ffp", [128, 1024], BF16)
    tsil_sb = sb("tsil", [128, 1024], BF16)
    lnsm = sb("lnsm", [128, 16], F32)
    wsA = sb("wsA", [128, 16384], U8)
    wsB = sb("wsB", [128, 16384], U8)
    wsAb = wsA[:].bitcast(BF16)
    wsBb = wsB[:].bitcast(BF16)

    A_GM0, A_GM1 = 0, 256
    A_ARG = 384
    A_E = 1152
    A_G2, A_RSTD = 1920, 2432
    A_DT = 2944
    A_CATM = 4224
    A_RSR = 4800
    A_XPB = 5120
    A_SSX = 5632
    A_T2 = A_ARG + 384
    A_FFOUT = A_G2
    A_CAR = 6144

    rstdrow_sb = arf[:, A_RSR:A_RSR + 256]
    etrow_sb = arf[:, A_RSR + 256:A_RSR + 320]
    etcol_sb = arf[0:64, A_RSR + 320:A_RSR + 321]

    def fuse_t(i):
        return arb[:, i * 512:(i + 1) * 512]

    print("sbuf bytes remaining:", nc.sbuf_bytes_remaining)

    pb = [psb(f"pb{i}", [128, 512], F32) for i in range(6)]
    pb6 = psb("pb6", [128, 1024], BF16)
    pb7 = psb("pb7", [128, 512], F32)
    pbb = {"pb6": pb6[:],
           "pb0b": pb[0][:].bitcast(BF16), "pb1b": pb[1][:].bitcast(BF16),
           "pb2b": pb[2][:].bitcast(BF16), "pb3b": pb[3][:].bitcast(BF16)}

    sems, cnt = {}, {}
    for name in ["ldx", "ldc", "ldcr0", "ldcr1", "st", "dbg",
                 "win0", "win1", "wout0", "wout1", "wgu0", "wgu1", "wd0", "wd1",
                 "cvA0", "cvA1", "cvB0", "cvB1", "cvl0", "cvl1", "cvm0", "cvm1",
                 "pe", "dve", "act", "pool"]:
        sems[name] = ctx.enter_context(nc.semaphore(name))
        cnt[name] = 0

    ops = []
    SELF_CH = {"vector": "dve", "scalar": "act", "gpsimd": "pool"}

    def emit(en, waits, fn, inc=None, inc_by=1):
        if en in SELF_CH and cnt[SELF_CH[en]] > 0:
            waits = list(waits) + [(SELF_CH[en], cnt[SELF_CH[en]])]
        if cnt["dbg"] > 0:
            waits = list(waits) + [("dbg", cnt["dbg"])]
        for c, v in waits:
            assert v <= cnt[c], f"wait on future event {c}:{v} > {cnt[c]}"
        if inc is not None:
            cnt[inc] += inc_by
        w = [(sems[c], v) for c, v in waits if v > 0]
        s_inc = (sems[inc], inc_by) if inc else None
        def f(eng, w=w, s_inc=s_inc, fn=fn):
            for sh, v in w:
                eng.wait_ge(sh, v)
            instr = fn(eng)
            if s_inc is not None:
                instr.then_inc(s_inc[0], s_inc[1])
        ops.append((en, f))

    def now(chn):
        return cnt[chn]

    class Ring:
        def __init__(self, ld0, ld1, depth=2):
            self.ld = [ld0, ld1]
            self.depth = depth
            self.loadrec = []
            self.freed = []
        def load(self, fn, engine="gpsimd", extra_waits=(), sem_override=None):
            k = len(self.loadrec)
            waits = list(extra_waits)
            j = k - self.depth
            if j >= 0:
                assert j < len(self.freed), "block consumed before reuse"
                waits.append(self.freed[j])
            slot = k % self.depth
            sname = (sem_override or self.ld)[slot]
            emit(engine, waits, lambda e, slot=slot: fn(e, slot),
                 inc=sname, inc_by=16)
            self.loadrec.append((sname, cnt[sname]))
            return k
        def use_wait(self, k):
            return self.loadrec[k]
        def free(self, ev):
            self.freed.append(ev)

    r_win = Ring("win0", "win1")
    r_wout = Ring("wout0", "wout1")
    r_wgu = Ring("wgu0", "wgu1")
    r_wd = Ring("wd0", "wd1")

    class TB:
        def __init__(self):
            self.state = {}
            self.cur = []
            self.i = 0
        def set_banks(self, keys, new_guards=None):
            for k in keys:
                if k not in self.state:
                    self.state[k] = {"last": 0, "guards": []}
                if new_guards and k in new_guards:
                    self.state[k]["guards"].extend(new_guards[k])
            self.cur = list(keys)
            self.i = 0
        def drain_of(self, key):
            return self.state[key]["last"] if key in self.state else 0
        def batch(self, entries, extra=()):
            key = self.cur[self.i]
            self.i = (self.i + 1) % len(self.cur)
            st = self.state[key]
            bank = pbb[key]
            assert len(entries) <= 8
            waits = list(extra) + st["guards"]
            st["guards"] = []
            if st["last"]:
                waits.append(("dve", st["last"]))
            def fill(e, entries=entries, bank=bank):
                last = None
                for j, (inap, idap, outap, evf) in enumerate(entries):
                    last = e.transpose(bank[:, j * 128:(j + 1) * 128], inap, idap)
                return last
            emit("tensor", waits, fill, inc="pe")
            fill_pe = now("pe")
            def drain(e, entries=entries, bank=bank, fill_pe=fill_pe):
                e.wait_ge(sems["pe"], fill_pe)
                last = None
                for j, (inap, idap, outap, evf) in enumerate(entries):
                    slot = bank[:, j * 128:(j + 1) * 128]
                    if evf is None:
                        last = e.tensor_copy(outap, slot)
                    else:
                        last = evf(e, slot)
                return last
            emit("vector", [], drain, inc="dve")
            st["last"] = now("dve")
            return fill_pe, st["last"]

    tbb = TB()

    def dbg_tap(name, src_ap):
        if name in DBG:
            emit("gpsimd", [("pe", now("pe")), ("dve", now("dve")), ("act", now("act"))],
                 lambda e, s=src_ap, d=ddbg[name]: e.dma_start(out=d, in_=s),
                 inc="dbg", inc_by=16)

    const_pairs = [
        (C["identb"], dr["ident_b"]), (C["identf"], dr["ident_f"]),
        (C["onesr"], dr["onesr"]), (C["onescol"], dr["onescol"]),
        (C["mask0"], dr["mask0"]), (C["b_in"], dr["b_in"]),
        (C["convw"], dr["convw"]), (C["dtb"], dr["dtb"]), (C["alog"], dr["alog"]),
        (C["d_row"], dr["d_row"]), (C["nw"], dr["nw"]), (C["bout"], dr["b_out"]),
        (C["sg"], dr["s_g"]), (C["su"], dr["s_u"]), (C["sd"], dr["s_d"]),
        (C["cvals"], dr["cvals"]),
    ]
    for dst, src in const_pairs:
        emit("sync", [], lambda e, d=dst, s=src: e.dma_start(out=d[:], in_=s),
             inc="ldc", inc_by=16)
    NCONST = now("ldc")
    emit("vector", [("ldc", NCONST)], lambda e: e.memset(rbf_sb[:], 0.0), inc="dve")
    emit("vector", [], lambda e: e.memset(halo_sb[:], 0.0), inc="dve")
    emit("scalar", [("ldc", NCONST)],
         lambda e: e.activation(C["acol"][:], C["alog"][:], AF.Exp), inc="act")
    emit("vector", [("act", now("act"))],
         lambda e: e.tensor_scalar(C["acol"][:], C["acol"][:], -1.0, None, ALU.mult),
         inc="dve")
    emit("vector", [("ldc", NCONST)],
         lambda e: e.tensor_tensor(C["bias66"][:], C["b_in"][0:64, 66:67], C["dtb"][:],
                                   ALU.add), inc="dve")
    emit("tensor", [("ldc", NCONST)],
         lambda e: e.matmul(pb7[:, 0:64], C["onesr"][:], C["d_row"][:],
                            start=True, stop=True), inc="pe")
    emit("vector", [("pe", now("pe"))],
         lambda e: e.tensor_copy(C["DB"][:], pb7[:, 0:64]), inc="dve")
    pb7_last = now("dve")
    INIT_DVE = now("dve")
    INIT_ACT = now("act")

    for og in range(34):
        k = r_win.load(lambda e, slot, og=og: e.dma_start(
            out=wsAb[:, slot * 4096: slot * 4096 + 4096],
            in_=dr["w_in"][og].rearrange("kc p o -> p kc o")),
            sem_override=("cvl0", "cvl1"))
        sname = "cvA%d" % (k % 2)
        emit("sync", [r_win.use_wait(k)],
             lambda e, og=og, k=k: e.dma_start(
                 out=d_w_in_b[og], in_=wsAb[:, (k % 2) * 4096: (k % 2) * 4096 + 4096]),
             inc=sname, inc_by=16)
        r_win.free((sname, cnt[sname]))
    for og in range(16):
        k = r_wout.load(lambda e, slot, og=og: e.dma_start(
            out=wsBb[:, slot * 4096: slot * 4096 + 4096],
            in_=dr["w_out"][og].rearrange("kc p o -> p kc o")),
            sem_override=("cvm0", "cvm1"))
        sname = "cvB%d" % (k % 2)
        emit("sync", [r_wout.use_wait(k)],
             lambda e, og=og, k=k: e.dma_start(
                 out=d_w_out_b[og], in_=wsBb[:, (k % 2) * 4096: (k % 2) * 4096 + 4096]),
             inc=sname, inc_by=16)
        r_wout.free((sname, cnt[sname]))

    rupd4 = [INIT_DVE] * 16
    car_k = 0
    car_free_pe = []
    S = {"outproj_pe": 0, "ssm_pe": 0, "ffn_pe": 0, "arena_dve": 0,
         "pb01_dve": 0, "pb01_act": 0, "pb4_act": 0,
         "hb2": (0, 0), "hb3": (0, 0), "pb4_dve": 0, "pb5_dve": 0,
         "yo0_dve": 0, "yo1_dve": 0}

    for span in range(NSPAN):
        row0 = (span // (L // SPAN)) * L + (span % (L // SPAN)) * SPAN
        seq_start = (span % (L // SPAN)) == 0

        st_before = now("st")
        for tq in range(4):
            emit("sync", [("st", st_before)],
                 lambda e, tq=tq, row0=row0: e.dma_start(
                     out=out_sb[:, tq * H:(tq + 1) * H],
                     in_=dx[row0 + tq * 128: row0 + (tq + 1) * 128, :]),
                 inc="ldx", inc_by=16)
        ldx_now = now("ldx")

        tbb.set_banks(["pb6", "pb2b", "pb3b"],
                      new_guards=({"pb2b": [("act", S["hb2"][0]), ("dve", S["hb2"][1])],
                                   "pb3b": [("act", S["hb3"][0]), ("dve", S["hb3"][1])]}
                                  if span else None))
        xn_guard = [("pe", S["ffn_pe"])] if span else []
        xn_act_ev = []
        xn_fill_pe = []
        for tq in range(4):
            xt = out_sb[:, tq * H:(tq + 1) * H]
            cb = (tq % 2) * 8
            w0 = [("ldx", ldx_now), ("dve", INIT_DVE)]
            if tq >= 2:
                w0.append(("act", xn_act_ev[tq - 2]))
            emit("vector", w0,
                 lambda e, xt=xt, cb=cb: e.reduce_sum(lnsm[:, cb:cb + 1], xt, axis=AXX),
                 inc="dve")
            sq_w = [("ldx", ldx_now), ("act", INIT_ACT)]
            if span:
                sq_w.append(("pe", S["ssm_pe"]))
                sq_w.append(("dve", S["arena_dve"]))
            emit("scalar", sq_w,
                 lambda e, xt=xt, cb=cb: e.activation(arf[:, 0:2048], xt, AF.Square,
                                                      accum_out=lnsm[:, cb + 1:cb + 2]),
                 inc="act")
            emit("vector", [("act", now("act"))],
                 lambda e, cb=cb: e.tensor_scalar(lnsm[:, cb + 2:cb + 3],
                                                  lnsm[:, cb:cb + 1], 1.0 / H, None,
                                                  ALU.mult), inc="dve")
            emit("vector", [],
                 lambda e, cb=cb: e.tensor_tensor(lnsm[:, cb + 3:cb + 4],
                                                  lnsm[:, cb + 2:cb + 3],
                                                  lnsm[:, cb + 2:cb + 3], ALU.mult),
                 inc="dve")
            emit("vector", [],
                 lambda e, cb=cb: e.scalar_tensor_tensor(
                     lnsm[:, cb + 4:cb + 5], lnsm[:, cb + 1:cb + 2], 1.0 / H,
                     lnsm[:, cb + 3:cb + 4], ALU.mult, ALU.subtract), inc="dve")
            emit("scalar", [("dve", now("dve"))],
                 lambda e, cb=cb: e.activation(lnsm[:, cb + 5:cb + 6],
                                               lnsm[:, cb + 4:cb + 5], AF.Ln,
                                               bias=C["cvals"][:, 0:1]), inc="act")
            emit("scalar", [],
                 lambda e, cb=cb: e.activation(lnsm[:, cb + 5:cb + 6],
                                               lnsm[:, cb + 5:cb + 6], AF.Exp,
                                               scale=-0.5), inc="act")
            emit("vector", [("act", now("act"))],
                 lambda e, cb=cb: e.tensor_tensor(lnsm[:, cb + 6:cb + 7],
                                                  lnsm[:, cb + 2:cb + 3],
                                                  lnsm[:, cb + 5:cb + 6], ALU.mult),
                 inc="dve")
            emit("vector", [],
                 lambda e, cb=cb: e.tensor_scalar(lnsm[:, cb + 6:cb + 7],
                                                  lnsm[:, cb + 6:cb + 7], -1.0, None,
                                                  ALU.mult), inc="dve")
            slot = tq % 2
            xnw = [("dve", now("dve"))]
            if tq >= 2:
                xnw.append(("pe", xn_fill_pe[tq - 2]))
            emit("scalar", xnw,
                 lambda e, xt=xt, slot=slot, cb=cb: e.activation(
                     arb[:, 4096 + slot * 2048: 4096 + (slot + 1) * 2048], xt,
                     AF.Identity, bias=lnsm[:, cb + 6:cb + 7],
                     scale=lnsm[:, cb + 5:cb + 6]), inc="act")
            xn_act_ev.append(now("act"))
            xa = now("act")
            fp_last = 0
            for half in range(2):
                entries = []
                for j in range(8):
                    kc = half * 8 + j
                    entries.append((
                        arb[:, 4096 + slot * 2048 + kc * 128:
                            4096 + slot * 2048 + (kc + 1) * 128],
                        C["identb"][:],
                        xn_sb[:, kc * SPAN + tq * 128: kc * SPAN + (tq + 1) * 128],
                        None))
                fp, _ = tbb.batch(entries, extra=[("act", xa)] + xn_guard)
                fp_last = fp
            xn_fill_pe.append(fp_last)
        ln_done_dve = now("dve")
        if span == 0:
            dbg_tap("xn", xn_sb[:])

        gate_guard = [("pe", S["outproj_pe"])] if span else []
        conv_dve_ev = [0] * 34
        silu_act_ev = [0] * 34
        xtr_fill_pe = [0] * 34
        bank_evac = {0: 0, 1: 0}
        pb01_guard = ([("dve", S["pb01_dve"])] if span else [])
        dtraw_act = 0
        for og in range(34):
            cw = []
            if span == 0:
                cw.append(("cvA%d" % (og % 2), (og // 2 + 1) * 16))
            if og < 2:
                cw.extend(r_wd.freed[-2:])
            k = r_win.load(lambda e, slot, og=og: e.dma_start(
                out=wsAb[:, slot * 4096: slot * 4096 + 4096],
                in_=d_w_in_b[og]), engine="sync", extra_waits=cw)
            for oc in range(2):
                o = og * 2 + oc
                if o == 67:
                    continue
                bank = o % 2
                pw = [r_win.use_wait(k), ("dve", ln_done_dve)]
                if bank_evac[bank]:
                    pw.append(("act", bank_evac[bank]))
                else:
                    pw += pb01_guard
                bufbase = (k % 2) * 4096
                def mms(e, bufbase=bufbase, oc=oc, bank=bank):
                    last = None
                    for kc in range(KC):
                        last = e.matmul(
                            pb[bank][:],
                            wsAb[:, bufbase + kc * 256 + oc * 128:
                                 bufbase + kc * 256 + oc * 128 + 128],
                            xn_sb[:, kc * SPAN: kc * SPAN + 512],
                            start=(kc == 0), stop=(kc == KC - 1))
                    return last
                emit("tensor", pw, mms, inc="pe")
                if oc == 1 or o == 66:
                    r_win.free(("pe", now("pe")))
                mmw = now("pe")
                if o < 32:
                    fc = o
                    emit("scalar", [("pe", mmw)] + gate_guard,
                         lambda e, fc=fc, bank=bank: e.activation(
                             gate_sb[:, fc * SPAN:(fc + 1) * SPAN], pb[bank][:],
                             AF.Silu, bias=C["b_in"][:, fc:fc + 1]), inc="act")
                elif o < 66:
                    fc = o - 32
                    ring = fc % 3
                    hbct = lcf[:, ring * 516:(ring + 1) * 516]
                    gw = [("pe", mmw)]
                    if fc >= 3:
                        gw.append(("dve", conv_dve_ev[fc - 3]))
                    emit("scalar", gw,
                         lambda e, hbct=hbct, bank=bank, o=o: e.activation(
                             hbct[:, 3:515], pb[bank][:], AF.Identity,
                             bias=C["b_in"][:, o:o + 1]), inc="act")
                    hev = now("act")
                    def halo_fill(e, hbct=hbct, fc=fc, seq_start=seq_start):
                        if seq_start:
                            return e.memset(hbct[:, 0:3], 0.0)
                        return e.tensor_copy(hbct[:, 0:3], halo_sb[:, fc * 3:fc * 3 + 3])
                    emit("vector", [("act", hev)], halo_fill, inc="dve")
                    acc = arf[:, 4096 + (fc % 2) * 512: 4096 + (fc % 2) * 512 + 512]
                    accw = [("act", silu_act_ev[fc - 2])] if fc >= 2 else []
                    emit("vector", accw,
                         lambda e, hbct=hbct, acc=acc, fc=fc: e.tensor_scalar(
                             acc, hbct[:, 0:512], C["convw"][:, fc * 4:fc * 4 + 1],
                             None, ALU.mult), inc="dve")
                    for j in range(1, 4):
                        emit("vector", [],
                             lambda e, hbct=hbct, acc=acc, fc=fc, j=j: e.scalar_tensor_tensor(
                                 acc, hbct[:, j:512 + j],
                                 C["convw"][:, fc * 4 + j:fc * 4 + j + 1],
                                 acc, ALU.mult, ALU.add), inc="dve")
                    emit("vector", [],
                         lambda e, hbct=hbct, fc=fc: e.tensor_copy(
                             halo_sb[:, fc * 3:fc * 3 + 3], hbct[:, 512:515]), inc="dve")
                    conv_dve_ev[fc] = now("dve")
                    if fc < 32:
                        dst = arb[:, 10240 + (fc % 2) * 512: 10240 + (fc % 2) * 512 + 512]
                    elif fc == 32:
                        dst = bc_sb[:, 0:512]
                    else:
                        dst = bc_sb[:, 512:1024]
                    sw = [("dve", now("dve"))]
                    if 2 <= fc and fc - 2 < 32:
                        sw.append(("pe", xtr_fill_pe[fc - 2]))
                    emit("scalar", sw,
                         lambda e, acc=acc, dst=dst: e.activation(dst, acc, AF.Silu),
                         inc="act")
                    silu_act_ev[fc] = now("act")
                    if fc < 32:
                        entries = []
                        for tt in range(4):
                            entries.append((
                                arb[:, 10240 + (fc % 2) * 512 + tt * 128:
                                    10240 + (fc % 2) * 512 + (tt + 1) * 128],
                                C["identb"][:],
                                xtm_sb[:, tt * INTER + fc * 128: tt * INTER + (fc + 1) * 128],
                                None))
                        xg = [("act", silu_act_ev[fc])]
                        if span:
                            xg.append(("pe", S["ssm_pe"]))
                        fp, _ = tbb.batch(entries, extra=xg)
                        xtr_fill_pe[fc] = fp
                    elif fc == 32:
                        entries = []
                        for tt in range(4):
                            entries.append((
                                bc_sb[:, tt * 128:(tt + 1) * 128], C["identb"][:],
                                btm_sb[:, tt * 128:(tt + 1) * 128], None))
                        bg2 = [("act", silu_act_ev[32])]
                        if span:
                            bg2.append(("pe", S["ssm_pe"]))
                        tbb.batch(entries, extra=bg2)
                else:
                    emit("scalar", [("pe", mmw)],
                         lambda e, bank=bank: e.activation(
                             dtraw_sb[:, :], pb[bank][0:64, :], AF.Identity,
                             bias=C["bias66"][:]), inc="act")
                    dtraw_act = now("act")
                bank_evac[bank] = now("act")
        if span == 0:
            dbg_tap("gate", gate_sb[:])
            dbg_tap("xtm", xtm_sb[:])
            dbg_tap("bcfm", bc_sb[:])
            dbg_tap("dtraw", dtraw_sb[:])

        tbb.set_banks(["pb6", "pb0b", "pb1b"],
                      new_guards={"pb0b": [("act", bank_evac[0]),
                                           ("dve", S["yo0_dve"])] + pb01_guard,
                                  "pb1b": [("act", bank_evac[1]),
                                           ("dve", S["yo1_dve"])] + pb01_guard})
        gu_bank_guard = {2: [("dve", tbb.drain_of("pb2b")), ("act", S["hb2"][0]),
                             ("dve", S["hb2"][1])],
                         3: [("dve", tbb.drain_of("pb3b")), ("act", S["hb3"][0]),
                             ("dve", S["hb3"][1])],
                         4: [("dve", S["pb4_dve"]), ("act", S["pb4_act"])],
                         5: [("dve", S["pb5_dve"])]}
        gu_bank_act = {2: 0, 3: 0, 4: 0, 5: 0}
        gu_bank_dve = {2: 0, 3: 0, 4: 0, 5: 0}
        out_add_ev = [0, 0, 0, 0]
        outadd_guard_act = xn_act_ev[3]
        tsil_ev = [0, 0]
        dn_ev = {2: 0, 3: 0, 4: 0, 5: 0}
        for ih in range(2):
            for ig in range(16):
                igg = ih * 16 + ig
                xgw = r_wout.freed[-2:] if (ih == 0 and ig == 0) else ()
                kg = r_wgu.load(lambda e, slot, igg=igg: e.dma_start(
                    out=wsBb[:, slot * 4096: slot * 4096 + 4096],
                    in_=dr["w_g"][igg].rearrange("kc p o -> p kc o")),
                    extra_waits=xgw)
                gbuf = (kg % 2) * 4096
                for isub in range(2):
                    bg_ = 2 + isub
                    gw = [r_wgu.use_wait(kg)] + gu_bank_guard[bg_]
                    gu_bank_guard[bg_] = []
                    if gu_bank_act[bg_]:
                        gw.append(("act", gu_bank_act[bg_]))
                    if dn_ev[bg_]:
                        gw.append(("dve", dn_ev[bg_]))
                    def gmms(e, gbuf=gbuf, isub=isub, bg_=bg_):
                        last = None
                        for kc in range(KC):
                            last = e.matmul(
                                pb[bg_][:],
                                wsBb[:, gbuf + kc * 256 + isub * 128:
                                     gbuf + kc * 256 + isub * 128 + 128],
                                xn_sb[:, kc * SPAN: kc * SPAN + 512],
                                start=(kc == 0), stop=(kc == KC - 1))
                        return last
                    emit("tensor", gw, gmms, inc="pe")
                r_wgu.free(("pe", now("pe")))
                g_pe = now("pe")
                ku = r_wgu.load(lambda e, slot, igg=igg: e.dma_start(
                    out=wsBb[:, slot * 4096: slot * 4096 + 4096],
                    in_=dr["w_u"][igg].rearrange("kc p o -> p kc o")))
                ubuf = (ku % 2) * 4096
                for isub in range(2):
                    bu_ = 4 + isub
                    uw = [r_wgu.use_wait(ku)] + gu_bank_guard[bu_]
                    gu_bank_guard[bu_] = []
                    if gu_bank_dve[bu_]:
                        uw.append(("dve", gu_bank_dve[bu_]))
                    if dn_ev[bu_]:
                        uw.append(("dve", dn_ev[bu_]))
                    def umms(e, ubuf=ubuf, isub=isub, bu_=bu_):
                        last = None
                        for kc in range(KC):
                            last = e.matmul(
                                pb[bu_][:],
                                wsBb[:, ubuf + kc * 256 + isub * 128:
                                     ubuf + kc * 256 + isub * 128 + 128],
                                xn_sb[:, kc * SPAN: kc * SPAN + 512],
                                start=(kc == 0), stop=(kc == KC - 1))
                        return last
                    emit("tensor", uw, umms, inc="pe")
                r_wgu.free(("pe", now("pe")))
                u_pe = now("pe")
                for isub in range(2):
                    i_g = igg * 2 + isub
                    il = ig * 2 + isub
                    tw = [("pe", g_pe)]
                    if tsil_ev[isub]:
                        tw.append(("dve", tsil_ev[isub]))
                    emit("scalar", tw,
                         lambda e, isub=isub, i_g=i_g: e.activation(
                             tsil_sb[:, isub * 512:(isub + 1) * 512],
                             pb[2 + isub][:], AF.Silu,
                             scale=C["sg"][:, i_g:i_g + 1]), inc="act")
                    emit("vector", [("act", now("act")), ("pe", u_pe)],
                         lambda e, isub=isub, i_g=i_g, il=il: e.scalar_tensor_tensor(
                             fuse_t(il), pb[4 + isub][:], C["su"][:, i_g:i_g + 1],
                             tsil_sb[:, isub * 512:(isub + 1) * 512],
                             ALU.mult, ALU.mult), inc="dve")
                    tsil_ev[isub] = now("dve")
                    gu_bank_act[2 + isub] = now("act")
                    gu_bank_dve[4 + isub] = now("dve")
            last_fuse_dve = now("dve")
            if span == 0 and ih == 0:
                dbg_tap("fuse", arb[:, 0:16384])
            for fop in range(8):
                bA, bB = (2, 3) if fop % 2 == 0 else (4, 5)
                fw = [("dve", last_fuse_dve)]
                for b_ in (bA, bB):
                    if dn_ev[b_]:
                        fw.append(("dve", dn_ev[b_]))
                    else:
                        if gu_bank_act[b_]:
                            fw.append(("act", gu_bank_act[b_]))
                        if gu_bank_dve[b_]:
                            fw.append(("dve", gu_bank_dve[b_]))
                first = True
                for ihq2 in range(2):
                    ihq = ih * 2 + ihq2
                    xdw = (r_win.freed[-2:]
                           if (ih == 0 and fop == 0 and ihq2 == 0) else ())
                    kd = r_wd.load(lambda e, slot, fop=fop, ihq=ihq: e.dma_start(
                        out=wsAb[:, slot * 4096: slot * 4096 + 4096],
                        in_=dr["w_d"][fop, ihq].rearrange("ic p o -> p ic o")),
                        extra_waits=xdw)
                    dbuf = (kd % 2) * 4096
                    w = (list(fw) if first else []) + [r_wd.use_wait(kd)]
                    def dmms(e, dbuf=dbuf, bA=bA, bB=bB, first=first, ihq2=ihq2):
                        last = None
                        for fh, bk in ((0, bA), (1, bB)):
                            for ic in range(16):
                                last = e.matmul(
                                    pb[bk][:],
                                    wsAb[:, dbuf + ic * 256 + fh * 128:
                                         dbuf + ic * 256 + fh * 128 + 128],
                                    fuse_t(ihq2 * 16 + ic),
                                    start=(first and ic == 0),
                                    stop=(ihq2 == 1 and ic == 15))
                        return last
                    emit("tensor", w, dmms, inc="pe")
                    r_wd.free(("pe", now("pe")))
                    first = False
                d_pe = now("pe")
                for fh, bk in ((0, bA), (1, bB)):
                    fo = fop * 2 + fh
                    pslot = fo % 2
                    emit("vector", [("pe", d_pe)],
                         lambda e, bk=bk, fo=fo, pslot=pslot: e.tensor_scalar(
                             ffp_sb[:, pslot * 512:(pslot + 1) * 512],
                             pb[bk][:], C["sd"][:, fo:fo + 1], None, ALU.mult),
                         inc="dve")
                    pev = now("dve")
                    dn_ev[bk] = pev
                    entries = []
                    for tq in range(4):
                        def evf(e, slot, tq=tq, fo=fo):
                            return e.tensor_tensor(
                                out_sb[:, tq * H + fo * 128: tq * H + (fo + 1) * 128],
                                out_sb[:, tq * H + fo * 128: tq * H + (fo + 1) * 128],
                                slot, ALU.add)
                        entries.append((
                            ffp_sb[:, pslot * 512 + tq * 128: pslot * 512 + (tq + 1) * 128],
                            C["identb"][:], None, evf))
                    _, dd = tbb.batch(entries,
                                      extra=[("dve", pev), ("act", outadd_guard_act),
                                             ("ldx", ldx_now)])
                    for tq in range(4):
                        out_add_ev[tq] = dd
        S["ffn_pe"] = now("pe")
        ffn_dn_ev = dict(dn_ev)

        tbb.set_banks(["pb6"])
        for cl in range(2):
            DT = A_DT
            dt_fm = arf[0:64, DT:DT + 256]
            ca_fm = arf[0:64, DT + 256:DT + 512]
            lndt = arf[0:64, DT + 512:DT + 768]
            scr1 = arf[0:64, DT + 768:DT + 1024]
            scr2 = arf[0:64, DT + 1024:DT + 1280]
            dts = dtraw_sb[:, cl * 256:(cl + 1) * 256]
            dtg = [("act", dtraw_act), ("pe", now("pe"))]
            emit("scalar", dtg,
                 lambda e, dts=dts, scr1=scr1: e.activation(scr1, dts, AF.Abs),
                 inc="act")
            emit("scalar", [],
                 lambda e, scr1=scr1: e.activation(scr1, scr1, AF.Exp, scale=-1.0),
                 inc="act")
            emit("scalar", [],
                 lambda e, scr1=scr1: e.activation(scr1, scr1, AF.Ln,
                                                   bias=C["cvals"][0:64, 1:2]),
                 inc="act")
            emit("vector", [("act", now("act"))],
                 lambda e, dts=dts, scr2=scr2: e.tensor_scalar(
                     scr2, dts, 0.0, None, ALU.max), inc="dve")
            emit("vector", [],
                 lambda e, dt_fm=dt_fm, scr1=scr1, scr2=scr2: e.tensor_tensor(
                     dt_fm, scr1, scr2, ALU.add), inc="dve")
            emit("scalar", [("dve", now("dve"))],
                 lambda e, lndt=lndt, dt_fm=dt_fm: e.activation(lndt, dt_fm, AF.Ln),
                 inc="act")
            emit("vector", [],
                 lambda e, scr1=scr1, dt_fm=dt_fm: e.tensor_scalar(
                     scr1, dt_fm, C["acol"][:], None, ALU.mult), inc="dve")
            emit("vector", [],
                 lambda e, ca_fm=ca_fm, scr1=scr1: e.tensor_tensor_scan(
                     ca_fm, scr1, scr1, 0.0, ALU.add, ALU.bypass), inc="dve")
            ca_dve = now("dve")
            emit("vector", [("act", now("act"))],
                 lambda e, scr2=scr2, lndt=lndt, ca_fm=ca_fm: e.tensor_tensor(
                     scr2, lndt, ca_fm, ALU.subtract), inc="dve")
            emit("scalar", [("dve", now("dve"))],
                 lambda e, scr1=scr1, ca_fm=ca_fm: e.activation(
                     scr1, ca_fm, AF.Exp, scale=-1.0, bias=ca_fm[:, 255:256]),
                 inc="act")
            emit("vector", [("act", now("act"))],
                 lambda e, scr1=scr1, dt_fm=dt_fm: e.tensor_tensor(
                     scr1, scr1, dt_fm, ALU.mult), inc="dve")
            emit("scalar", [],
                 lambda e, ca_fm=ca_fm: e.activation(etcol_sb[:], ca_fm[:, 255:256],
                                                     AF.Exp), inc="act")
            emit("scalar", [("dve", now("dve"))],
                 lambda e, dt_fm=dt_fm, ca_fm=ca_fm: e.activation(dt_fm, ca_fm, AF.Exp),
                 inc="act")
            sm_dve = now("dve")
            sm_act = now("act")
            tw = [("dve", sm_dve), ("act", sm_act)]
            if pb7_last:
                tw.append(("dve", pb7_last))
            def dtfill(e, dt_fm=dt_fm, scr1=scr1, scr2=scr2, lndt=lndt):
                for q in range(2):
                    e.transpose(pb7[:, q * 64:(q + 1) * 64],
                                dt_fm[:, q * 128:(q + 1) * 128], C["identf"][0:64, 0:64])
                for q in range(2):
                    e.transpose(pb7[:, 128 + q * 64:128 + (q + 1) * 64],
                                scr2[:, q * 128:(q + 1) * 128], C["identf"][0:64, 0:64])
                for q in range(2):
                    e.transpose(pb7[:, 256 + q * 64:256 + (q + 1) * 64],
                                scr1[:, q * 128:(q + 1) * 128], C["identf"][0:64, 0:64])
                e.transpose(pb7[:, 384 + 0 * 64:384 + 64],
                            lndt[:, 0:128], C["identf"][0:64, 0:64])
                return e.transpose(pb7[:, 448:512],
                                   lndt[:, 128:256], C["identf"][0:64, 0:64])
            emit("tensor", tw, dtfill, inc="pe")
            dtf_pe = now("pe")
            def dtdrain(e, dtf_pe=dtf_pe):
                e.wait_ge(sems["pe"], dtf_pe)
                return e.tensor_copy(arf[:, A_CATM:A_CATM + 512], pb7[:, 0:512])
            emit("vector", [], dtdrain, inc="dve")
            catm_dve = now("dve")
            emit("vector", [],
                 lambda e: e.tensor_tensor(arf[:, A_ARG + 256:A_ARG + 384],
                                           arf[:, A_CATM + 128:A_CATM + 256],
                                           arf[:, A_CATM + 384:A_CATM + 512],
                                           ALU.subtract), inc="dve")
            nca_dve = now("dve")
            def etfill(e):
                return e.transpose(pb7[0:1, 0:64], etcol_sb[:], C["identf"][0:64, 0:64])
            emit("tensor", [("dve", catm_dve)], etfill, inc="pe")
            etf_pe = now("pe")
            def etdrain(e, etf_pe=etf_pe):
                e.wait_ge(sems["pe"], etf_pe)
                return e.tensor_copy(etrow_sb[0:1, :], pb7[0:1, 0:64])
            emit("vector", [], etdrain, inc="dve")
            emit("tensor", [("dve", now("dve"))],
                 lambda e: e.matmul(pb7[:, 64:128], C["onesr"][:], etrow_sb[0:1, :],
                                    start=True, stop=True, skip_group_check=True),
                 inc="pe")
            emit("vector", [("pe", now("pe"))],
                 lambda e: e.tensor_copy(arf[:, A_CATM + 512:A_CATM + 576],
                                         pb7[:, 64:128]), inc="dve")
            etb_dve = now("dve")
            pb7_last = etb_dve

            gmw = [("act", silu_act_ev[33]), ("act", now("act")), ("dve", now("dve"))]
            def gmm(e, cl=cl):
                e.matmul(pb[4][:, 0:256], bc_sb[:, cl * 256:cl * 256 + 128],
                         bc_sb[:, 512 + cl * 256:512 + (cl + 1) * 256],
                         start=True, stop=False, skip_group_check=True)
                return e.matmul(pb[4][:, 256:512],
                                bc_sb[:, cl * 256 + 128:cl * 256 + 256],
                                bc_sb[:, 512 + cl * 256:512 + (cl + 1) * 256],
                                start=False, stop=True, skip_group_check=True)
            emit("tensor", gmw, gmm, inc="pe")
            emit("vector", [("pe", now("pe"))],
                 lambda e: e.tensor_tensor(arf[:, A_GM0:A_GM0 + 256], pb[4][:, 0:256],
                                           C["mask0"][:], ALU.mult), inc="dve")
            emit("vector", [],
                 lambda e: e.tensor_tensor(arf[:, A_GM1:A_GM1 + 128], pb[4][:, 384:512],
                                           C["mask0"][:, 0:128], ALU.mult), inc="dve")
            gm_dve = now("dve")
            if span == 0 and cl == 0:
                dbg_tap("gm", arf[:, 0:384])

            if seq_start and cl == 0:
                emit("vector", [("pe", S["ssm_pe"])] if span else [],
                     lambda e: e.memset(rbf_sb[:], 0.0), inc="dve")
                for g_ in range(16):
                    rupd4[g_] = now("dve")

            def S_ap(slot):
                if slot < 2:
                    return ssb[:, slot * 384:(slot + 1) * 384]
                return arb[:, 11264 + (slot - 2) * 384: 11264 + (slot - 1) * 384]

            PD = 4
            E_ev = [0] * NH
            S_ev = [0] * NH
            ymm_ev = [0] * NH
            t2_ev = [0] * 16
            y4_ev = [0] * 16
            st_pe_ev = [0] * 16
            gdrain = [0] * 32
            for hh in range(NH + PD):
                if hh >= PD:
                    h = hh - PD
                    g = h // 4
                    hg = h % 4
                    gb = pb[2 + g % 2]
                    sap = S_ap(h % 4)
                    yw = [("pool", S_ev[h])]
                    if hg == 0:
                        if g >= 2:
                            yw.append(("dve", y4_ev[g - 2]))
                        else:
                            yw.append(("dve", ffn_dn_ev.get(2 + g % 2, 0)))
                            ga = gu_bank_act.get(2 + g % 2, 0)
                            if ga:
                                yw.append(("act", ga))
                            hb = S["hb2"] if g % 2 == 0 else S["hb3"]
                            if hb[1]:
                                yw.append(("dve", hb[1]))
                    tt0, tt1 = 2 * cl, 2 * cl + 1
                    def ydmms(e, gb=gb, sap=sap, hg=hg, h=h, tt0=tt0, tt1=tt1):
                        hc = h * 64
                        e.matmul(gb[:, hg * 128:hg * 128 + 64], sap[:, 0:128],
                                 xtm_sb[:, tt0 * INTER + hc:tt0 * INTER + hc + 64],
                                 start=(hg == 0), stop=False, skip_group_check=True)
                        e.matmul(gb[:, hg * 128 + 64:hg * 128 + 128],
                                 sap[:, 128:256],
                                 xtm_sb[:, tt0 * INTER + hc:tt0 * INTER + hc + 64],
                                 start=False, stop=False, skip_group_check=True)
                        return e.matmul(gb[:, hg * 128 + 64:hg * 128 + 128],
                                        sap[:, 256:384],
                                        xtm_sb[:, tt1 * INTER + hc:tt1 * INTER + hc + 64],
                                        start=False, stop=(hg == 3),
                                        skip_group_check=True)
                    emit("tensor", yw, ydmms, inc="pe")
                    ymm_ev[h] = now("pe")
                    if hg == 3:
                        for sh in range(2):
                            xw = [("dve", catm_dve)]
                            if g >= 2:
                                xw.append(("pe", st_pe_ev[g - 2]))
                            emit("vector", xw,
                                 lambda e, sh=sh, g=g, cl=cl: e.tensor_tensor(
                                     arb[:, 10240 + ((g % 2) * 2 + sh) * 256:
                                         10240 + ((g % 2) * 2 + sh) * 256 + 256]
                                     .rearrange("p (h w) -> p h w", h=4),
                                     xtm_sb[:, (2 * cl + sh) * INTER + g * 256:
                                            (2 * cl + sh) * INTER + g * 256 + 256]
                                     .rearrange("p (h w) -> p h w", h=4),
                                     arf[:, A_CATM + 256 + sh * 64 + 4 * g:
                                         A_CATM + 256 + sh * 64 + 4 * g + 4]
                                     .unsqueeze(2).broadcast_to([128, 4, 64]),
                                     ALU.mult), inc="dve")
                        xp_dve = now("dve")
                        yob = pb[g % 2]
                        ow = [("dve", rupd4[g])]
                        if g >= 2:
                            ow.append(("dve", t2_ev[g - 2]))
                        else:
                            ow.append(("dve", tbb.drain_of("pb0b" if g % 2 == 0
                                                           else "pb1b")))
                            yo0 = S.get("yo%d_dve" % (g % 2), 0)
                            if yo0:
                                ow.append(("dve", yo0))
                        def yomms(e, yob=yob, g=g, cl=cl):
                            e.matmul(yob[:, 0:256],
                                     bc_sb[:, 512 + cl * 256:512 + cl * 256 + 128],
                                     rbf_sb[:, g * 256:(g + 1) * 256],
                                     start=True, stop=True, skip_group_check=True)
                            return e.matmul(yob[:, 256:512],
                                            bc_sb[:, 512 + cl * 256 + 128:
                                                  512 + cl * 256 + 256],
                                            rbf_sb[:, g * 256:(g + 1) * 256],
                                            start=False, stop=True,
                                            skip_group_check=True)
                        emit("tensor", ow, yomms, inc="pe")
                        yo_pe = now("pe")
                        stw = [("dve", xp_dve)]
                        if g >= 1:
                            stw.append(("dve", rupd4[g - 1]))
                        else:
                            stw.append(("dve", max(S["pb5_dve"],
                                                   ffn_dn_ev.get(5, 0),
                                                   gu_bank_dve.get(5, 0))))
                        def stmms(e, g=g, cl=cl):
                            e.matmul(pb[5][:, 0:256],
                                     btm_sb[:, (2 * cl) * 128:(2 * cl) * 128 + 128],
                                     arb[:, 10240 + ((g % 2) * 2 + 0) * 256:
                                         10240 + ((g % 2) * 2 + 0) * 256 + 256],
                                     start=True, stop=False, skip_group_check=True)
                            return e.matmul(pb[5][:, 0:256],
                                            btm_sb[:, (2 * cl + 1) * 128:
                                                   (2 * cl + 1) * 128 + 128],
                                            arb[:, 10240 + ((g % 2) * 2 + 1) * 256:
                                                10240 + ((g % 2) * 2 + 1) * 256 + 256],
                                            start=False, stop=True,
                                            skip_group_check=True)
                        emit("tensor", stw, stmms, inc="pe")
                        st_pe_ev[g] = now("pe")
                        t2 = arf[:, A_T2:A_T2 + 256]
                        t2_3 = t2.rearrange("p (h w) -> p h w", h=4)
                        gslot = g % 2
                        for q in range(2):
                            ecarep = (arf[:, A_CATM + q * 64 + 4 * g:
                                          A_CATM + q * 64 + 4 * g + 4]
                                      .unsqueeze(2).broadcast_to([128, 4, 64]))
                            emit("vector", [("pe", yo_pe)] if q == 0 else [],
                                 lambda e, yob=yob, q=q, t2_3=t2_3, ecarep=ecarep:
                                 e.tensor_tensor(
                                     t2_3,
                                     yob[:, q * 256:(q + 1) * 256]
                                     .rearrange("p (h w) -> p h w", h=4),
                                     ecarep, ALU.mult), inc="dve")
                            emit("vector", [],
                                 lambda e, gb=gb, q=q, t2_3=t2_3: e.tensor_tensor(
                                     t2_3, t2_3,
                                     gb[:].rearrange("p (hg c) -> p hg c", hg=4)
                                     [:, :, q * 64:(q + 1) * 64], ALU.add), inc="dve")
                            ydst = (ybuf[:]
                                    .rearrange("p (s pair qq hip w) -> p s pair qq hip w",
                                               s=2, pair=2, qq=2, hip=2)
                                    [:, gslot, :, q, :, :])
                            drep = (C["DB"][:, 4 * g:4 * g + 4]
                                    .rearrange("p (pair hip) -> p pair hip", pair=2)
                                    .unsqueeze(3).broadcast_to([128, 2, 2, 64]))
                            t2_4 = t2.rearrange("p (pair hip w) -> p pair hip w",
                                                pair=2, hip=2)
                            emit("vector", [],
                                 lambda e, ydst=ydst, g=g, cl=cl, q=q, drep=drep:
                                 e.tensor_tensor(
                                     ydst,
                                     xtm_sb[:, (2 * cl + q) * INTER + g * 256:
                                            (2 * cl + q) * INTER + g * 256 + 256]
                                     .rearrange("p (pair hip w) -> p pair hip w",
                                                pair=2, hip=2),
                                     drep, ALU.mult), inc="dve")
                            emit("vector", [],
                                 lambda e, ydst=ydst, t2_4=t2_4: e.tensor_tensor(
                                     ydst, ydst, t2_4, ALU.add), inc="dve")
                        t2_ev[g] = now("dve")
                        y4_ev[g] = now("dve")
                        etrep = (arf[:, A_CATM + 512 + 4 * g:A_CATM + 512 + 4 * g + 4]
                                 .unsqueeze(2).broadcast_to([128, 4, 64]))
                        rb4 = rbf_sb[:, g * 256:(g + 1) * 256].rearrange(
                            "p (h w) -> p h w", h=4)
                        emit("vector", [("pe", st_pe_ev[g])],
                             lambda e, rb4=rb4, etrep=etrep: e.tensor_tensor(
                                 rb4, rb4, etrep, ALU.mult), inc="dve")
                        emit("vector", [],
                             lambda e, rb4=rb4: e.tensor_tensor(
                                 rb4, rb4,
                                 pb[5][:, 0:256].rearrange("p (h w) -> p h w", h=4),
                                 ALU.add), inc="dve")
                        rupd4[g] = now("dve")
                        entries = []
                        for pr in range(2):
                            fc = 2 * g + pr
                            def evf(e, slot, fc=fc, cl=cl, pr=pr, q=0):
                                gsl = gate_sb[:, fc * SPAN + cl * 256:
                                              fc * SPAN + cl * 256 + 128]
                                return e.scalar_tensor_tensor(
                                    gsl, slot, C["nw"][:, fc:fc + 1], gsl,
                                    ALU.mult, ALU.mult)
                            def evf1(e, slot, fc=fc, cl=cl):
                                gsl = gate_sb[:, fc * SPAN + cl * 256 + 128:
                                              fc * SPAN + cl * 256 + 256]
                                return e.scalar_tensor_tensor(
                                    gsl, slot, C["nw"][:, fc:fc + 1], gsl,
                                    ALU.mult, ALU.mult)
                            entries.append((
                                ybuf[:, gslot * 512 + pr * 256:
                                     gslot * 512 + pr * 256 + 128],
                                C["identb"][:], None, evf))
                            entries.append((
                                ybuf[:, gslot * 512 + pr * 256 + 128:
                                     gslot * 512 + pr * 256 + 256],
                                C["identb"][:], None, evf1))
                        _, pd_ = tbb.batch(entries, extra=[("dve", y4_ev[g])])
                        gdrain[2 * g] = pd_
                        gdrain[2 * g + 1] = pd_
                if hh < NH:
                    h = hh
                    r = h % 2
                    g4 = h // 4
                    if h % 4 == 0:
                        cw = [("dve", ca_dve)]
                        if car_k >= 2:
                            cw.append(("pe", car_free_pe[car_k - 2]))
                        emit("gpsimd", cw,
                             lambda e, g4=g4, ca_fm=ca_fm: e.dma_start(
                                 out=arf[0:1, A_CAR + (g4 % 2) * 1024:
                                         A_CAR + (g4 % 2) * 1024 + 1024],
                                 in_=ca_fm[g4 * 4:(g4 + 1) * 4, :]),
                             inc="ldcr%d" % (car_k % 2), inc_by=16)
                        car_k += 1
                    cabbank = pb[4] if r == 0 else pb7
                    cw = [("ldcr%d" % ((car_k - 1) % 2), ((car_k - 1) // 2 + 1) * 16)]
                    if h >= 2:
                        cw.append(("act", E_ev[h - 2]))
                    elif h == 0:
                        cw.append(("dve", gm_dve))
                    else:
                        cw.append(("dve", etb_dve))
                    emit("tensor", cw,
                         lambda e, cabbank=cabbank, g4=g4, h=h: e.matmul(
                             cabbank[:, 0:256], C["onesr"][:],
                             arf[0:1, A_CAR + (g4 % 2) * 1024 + (h % 4) * 256:
                                 A_CAR + (g4 % 2) * 1024 + (h % 4) * 256 + 256],
                             start=True, stop=True, skip_group_check=True), inc="pe")
                    if h % 4 == 3:
                        car_free_pe.append(now("pe"))
                    cab_pe = now("pe")
                    eset = A_E + r * 384
                    absl = arf[:, A_ARG + r * 128:A_ARG + r * 128 + 128]
                    nca0 = arf[:, A_ARG + 256 + h:A_ARG + 256 + h + 1]
                    nca1 = arf[:, A_ARG + 320 + h:A_ARG + 320 + h + 1]
                    bcol0 = arf[:, A_CATM + 128 + h:A_CATM + 128 + h + 1]
                    lnd0 = arf[:, A_CATM + 384 + h:A_CATM + 384 + h + 1]
                    lnd1 = arf[:, A_CATM + 448 + h:A_CATM + 448 + h + 1]
                    ew = [("pe", cab_pe)]
                    if h >= 2:
                        ew.append(("pool", S_ev[h - 2]))
                    elif h == 1:
                        ew.append(("dve", nca_dve))
                    else:
                        ew.append(("dve", nca_dve))
                    emit("scalar", ew,
                         lambda e, cabbank=cabbank, eset=eset, bcol0=bcol0: e.activation(
                             arf[:, eset + 128:eset + 256], cabbank[:, 128:256],
                             AF.Exp, bias=bcol0), inc="act")
                    emit("scalar", [],
                         lambda e, cabbank=cabbank, absl=absl, nca0=nca0: e.activation(
                             absl, cabbank[:, 0:128], AF.Abs, bias=nca0), inc="act")
                    emit("scalar", [],
                         lambda e, absl=absl, eset=eset, lnd0=lnd0: e.activation(
                             arf[:, eset:eset + 128], absl, AF.Exp, scale=-1.0,
                             bias=lnd0), inc="act")
                    emit("scalar", [],
                         lambda e, cabbank=cabbank, absl=absl, nca1=nca1: e.activation(
                             absl, cabbank[:, 128:256], AF.Abs, bias=nca1), inc="act")
                    emit("scalar", [],
                         lambda e, absl=absl, eset=eset, lnd1=lnd1: e.activation(
                             arf[:, eset + 256:eset + 384], absl, AF.Exp, scale=-1.0,
                             bias=lnd1), inc="act")
                    E_ev[h] = now("act")
                    sw = [("act", E_ev[h])]
                    if h >= PD:
                        sw.append(("pe", ymm_ev[h - PD]))
                    if h == 0:
                        sw.append(("dve", gm_dve))
                    sap = S_ap(h % 4)
                    emit("gpsimd", sw,
                         lambda e, sap=sap, eset=eset: e.tensor_tensor(
                             sap[:, 0:128], arf[:, A_GM0:A_GM0 + 128],
                             arf[:, eset:eset + 128], ALU.mult), inc="pool")
                    emit("gpsimd", [],
                         lambda e, sap=sap, eset=eset: e.tensor_tensor(
                             sap[:, 128:256], arf[:, A_GM0 + 128:A_GM0 + 256],
                             arf[:, eset + 128:eset + 256], ALU.mult), inc="pool")
                    emit("gpsimd", [],
                         lambda e, sap=sap, eset=eset: e.tensor_tensor(
                             sap[:, 256:384], arf[:, A_GM1:A_GM1 + 128],
                             arf[:, eset + 256:eset + 384], ALU.mult), inc="pool")
                    S_ev[h] = now("pool")
            rms_pe = [0] * 32
            for fc in range(32):
                sqw = [("dve", gdrain[fc])]
                if fc >= 2:
                    sqw.append(("pe", rms_pe[fc - 2]))
                emit("scalar", sqw,
                     lambda e, fc=fc, cl=cl: e.activation(
                         arf[:, A_G2 + (fc % 2) * 256:A_G2 + (fc % 2) * 256 + 256],
                         gate_sb[:, fc * SPAN + cl * 256: fc * SPAN + (cl + 1) * 256],
                         AF.Square), inc="act")
                sqe = now("act")
                rmsw = [("act", sqe)]
                if fc == 0:
                    rmsw.append(("dve", rupd4[15]))
                emit("tensor", rmsw,
                     lambda e, fc=fc: e.matmul(
                         pb[5][0:1, 256:512], C["onescol"][:],
                         arf[:, A_G2 + (fc % 2) * 256:A_G2 + (fc % 2) * 256 + 256],
                         start=(fc == 0), stop=(fc == 31),
                         skip_group_check=True), inc="pe")
                rms_pe[fc] = now("pe")
            emit("vector", [("pe", now("pe"))],
                 lambda e: e.tensor_scalar(rstdrow_sb[0:1, :], pb[5][0:1, 256:512],
                                           1.0 / INTER, None, ALU.mult), inc="dve")
            S["pb5_dve"] = now("dve")
            emit("scalar", [("dve", now("dve"))],
                 lambda e: e.activation(rstdrow_sb[0:1, :], rstdrow_sb[0:1, :],
                                        AF.Ln, bias=C["cvals"][0:1, 0:1]), inc="act")
            emit("scalar", [],
                 lambda e: e.activation(rstdrow_sb[0:1, :], rstdrow_sb[0:1, :],
                                        AF.Exp, scale=-0.5), inc="act")
            emit("tensor", [("act", now("act")), ("act", E_ev[NH - 1]),
                            ("dve", pb7_last)],
                 lambda e: e.matmul(pb7[:, 0:256], C["onesr"][:], rstdrow_sb[0:1, :],
                                    start=True, stop=True, skip_group_check=True),
                 inc="pe")
            emit("vector", [("pe", now("pe"))],
                 lambda e, cl=cl: e.tensor_copy(
                     arf[:, A_RSTD + cl * 256:A_RSTD + (cl + 1) * 256], pb7[:, 0:256]),
                 inc="dve")
            pb7_last = now("dve")
            S["hb2"] = (0, y4_ev[14])
            S["hb3"] = (0, y4_ev[15])
            S["pb4_act"] = E_ev[NH - 2]
            S["yo0_dve"] = t2_ev[14]
            S["yo1_dve"] = t2_ev[15]
        S["ssm_pe"] = now("pe")
        rstd_dve = now("dve")
        S["arena_dve"] = rstd_dve
        if span == 0:
            dbg_tap("g", gate_sb[:])
            dbg_tap("rstd", arf[:, A_RSTD:A_RSTD + 512])

        tbb.set_banks(["pb6", "pb2b", "pb3b"],
                      new_guards={"pb2b": [("act", S["hb2"][0]), ("dve", S["hb2"][1])],
                                  "pb3b": [("act", S["hb3"][0]), ("dve", S["hb3"][1])]})
        op_bank_dve = {0: 0, 1: 0}
        ytr_pb01 = [("dve", S["yo0_dve"]), ("dve", S["yo1_dve"])]
        for og in range(16):
            cw = []
            if span == 0:
                cw.append(("cvB%d" % (og % 2), (og // 2 + 1) * 16))
            if og < 2:
                cw.extend(r_wgu.freed[-2:])
            k = r_wout.load(lambda e, slot, og=og: e.dma_start(
                out=wsBb[:, slot * 4096: slot * 4096 + 4096],
                in_=d_w_out_b[og]), engine="sync", extra_waits=cw)
            bank = og % 2
            pw = [r_wout.use_wait(k), ("dve", rstd_dve), ("act", bank_evac[bank])]
            pw += [w for w in ytr_pb01 if w[1] > 0]
            if op_bank_dve[bank]:
                pw.append(("dve", op_bank_dve[bank]))
            obuf = (k % 2) * 4096
            def omms(e, obuf=obuf, bank=bank):
                last = None
                for kc in range(32):
                    last = e.matmul(
                        pb[bank][:],
                        wsBb[:, obuf + kc * 128: obuf + (kc + 1) * 128],
                        gate_sb[:, kc * SPAN: kc * SPAN + 512],
                        start=(kc == 0), stop=(kc == 31))
                return last
            emit("tensor", pw, omms, inc="pe")
            r_wout.free(("pe", now("pe")))
            o_pe = now("pe")
            for cl in range(2):
                fsl = arf[:, A_FFOUT + cl * 256:A_FFOUT + (cl + 1) * 256]
                emit("vector", [("pe", o_pe)],
                     lambda e, bank=bank, cl=cl, fsl=fsl: e.tensor_tensor(
                         fsl, pb[bank][:, cl * 256:(cl + 1) * 256],
                         arf[:, A_RSTD + cl * 256:A_RSTD + (cl + 1) * 256], ALU.mult),
                     inc="dve")
                emit("vector", [],
                     lambda e, og=og, fsl=fsl: e.tensor_scalar(
                         fsl, fsl, C["bout"][:, og:og + 1], None, ALU.add), inc="dve")
                pev = now("dve")
                tw2 = [("dve", pev), ("dve", pb7_last)]
                def ofill(e, fsl=fsl):
                    e.transpose(pb7[:, 0:128], fsl[:, 0:128], C["identf"][:])
                    return e.transpose(pb7[:, 128:256], fsl[:, 128:256], C["identf"][:])
                emit("tensor", tw2, ofill, inc="pe")
                ofp = now("pe")
                def odrain(e, ofp=ofp, cl=cl, og=og):
                    e.wait_ge(sems["pe"], ofp)
                    last = None
                    for q in range(2):
                        tq = cl * 2 + q
                        last = e.tensor_tensor(
                            out_sb[:, tq * H + og * 128: tq * H + (og + 1) * 128],
                            out_sb[:, tq * H + og * 128: tq * H + (og + 1) * 128],
                            pb7[:, q * 128:(q + 1) * 128], ALU.add)
                    return last
                emit("vector", [], odrain, inc="dve")
                pb7_last = now("dve")
                out_add_ev[cl * 2] = pb7_last
                out_add_ev[cl * 2 + 1] = pb7_last
            op_bank_dve[bank] = now("dve")
        S["outproj_pe"] = now("pe")
        S["pb01_dve"] = max(op_bank_dve[0], op_bank_dve[1])
        if span == 0:
            dbg_tap("outsb", out_sb[:])

        for tq in range(4):
            emit("sync", [("dve", out_add_ev[tq])],
                 lambda e, tq=tq, row0=row0: e.dma_start(
                     out=dout[row0 + tq * 128: row0 + (tq + 1) * 128, :],
                     in_=out_sb[:, tq * H:(tq + 1) * H]), inc="st", inc_by=16)

    total_st = now("st")
    total_dbg = now("dbg")
    def fin(e):
        if total_dbg:
            e.wait_ge(sems["dbg"], total_dbg)
        e.wait_ge(sems["st"], total_st)
        return e.nop()
    emit("sync", [], fin)

    nops = {k: sum(1 for en, _ in ops if en == k)
            for k in ["tensor", "vector", "scalar", "gpsimd", "sync"]}
    print("op groups per engine:", nops)

    with nc.Block() as block:
        @block.tensor
        def _(e):
            for en, f in ops:
                if en == "tensor":
                    f(e)

        @block.vector
        def _(e):
            for en, f in ops:
                if en == "vector":
                    f(e)

        @block.scalar
        def _(e):
            for en, f in ops:
                if en == "scalar":
                    f(e)

        @block.gpsimd
        def _(e):
            for en, f in ops:
                if en == "gpsimd":
                    f(e)

        @block.sync
        def _(e):
            for en, f in ops:
                if en == "sync":
                    f(e)

    ctx.close()
    return nc


def kernel(**inputs):
    g = prep_host(inputs)
    x = _f32(inputs["L0_attn_hidden_states"])
    nc = build_program()
    in_maps = []
    for core in range(NCORES):
        m = dict(g)
        m["x"] = np.ascontiguousarray(
            x[core * SEQ_PER_CORE:(core + 1) * SEQ_PER_CORE].reshape(SEQ_PER_CORE * L, H))
        in_maps.append(m)
    res = run_bass_kernel_spmd(nc, in_maps, core_ids=list(range(NCORES)))
    out = np.stack([np.asarray(res.results[c]["out"]).reshape(SEQ_PER_CORE, L, H)
                    for c in range(NCORES)])
    return out.reshape(16, L, H)


if __name__ == "__main__":
    import reference as refmod
    ins = refmod.setup_inputs(0)
    o = kernel(**{k: np.asarray(v) for k, v in ins.items()})
    print(o.shape, o.dtype)


# version 05
# speedup vs baseline: 1.9165x; 1.3194x over previous
import numpy as np
import ml_dtypes
from contextlib import ExitStack

import concourse.bass as bass
import concourse.mybir as mybir
from concourse.bass_utils import run_bass_kernel_spmd

F32 = mybir.dt.float32
BF16 = mybir.dt.bfloat16
I8 = mybir.dt.int8
U8 = mybir.dt.uint8
AF = mybir.ActivationFunctionType
ALU = mybir.AluOpType
AXX = mybir.AxisListType.X

NCORES = 8
L = 2048
H = 2048
NH = 64
INTER = 4096
SEQ_PER_CORE = 2
SPAN = 512
NSPAN = SEQ_PER_CORE * L // SPAN
KC = H // 128
OCH = 68
CLAMP = 10.0

DBG = {}


def _f32(x):
    return np.ascontiguousarray(np.asarray(x, dtype=np.float32))


def prep_host(inputs):
    g = {}
    w_in = _f32(inputs["L0_attn_in_proj_w"])
    w_in_p = np.zeros((OCH * 128, H), np.float32)
    w_in_p[:8512] = w_in
    g["w_in"] = np.ascontiguousarray(
        w_in_p.T.reshape(KC, 128, 34, 256).transpose(2, 0, 1, 3))
    b_in = np.zeros((OCH * 128,), np.float32)
    b_in[:8512] = _f32(inputs["L0_attn_in_proj_b"])
    g["b_in"] = np.ascontiguousarray(b_in.reshape(OCH, 128).T)
    g["convw"] = np.ascontiguousarray(
        _f32(inputs["L0_attn_conv_w"]).reshape(34, 128, 4).transpose(1, 0, 2).reshape(128, 136))
    g["dtb"] = _f32(inputs["L0_attn_dt_bias"]).reshape(64, 1)
    g["alog"] = _f32(inputs["L0_attn_A_log"]).reshape(64, 1)
    g["d_row"] = _f32(inputs["L0_attn_D"]).reshape(1, 64)
    g["nw"] = np.ascontiguousarray(_f32(inputs["L0_attn_norm_w"]).reshape(32, 128).T)
    g["w_out"] = np.ascontiguousarray(
        _f32(inputs["L0_attn_out_proj_w"]).T.reshape(32, 128, 16, 128).transpose(2, 0, 1, 3))
    g["b_out"] = np.ascontiguousarray(_f32(inputs["L0_attn_out_proj_b"]).reshape(16, 128).T)
    g["w_g"] = np.ascontiguousarray(
        np.asarray(inputs["L0_ffn_w_gate"]).reshape(KC, 128, 32, 256).transpose(2, 0, 1, 3))
    g["w_u"] = np.ascontiguousarray(
        np.asarray(inputs["L0_ffn_w_up"]).reshape(KC, 128, 32, 256).transpose(2, 0, 1, 3))
    g["w_d"] = np.ascontiguousarray(
        np.asarray(inputs["L0_ffn_w_down"]).reshape(4, 16, 128, 8, 256).transpose(3, 0, 1, 2, 4))
    g["s_g"] = np.ascontiguousarray(_f32(inputs["L0_ffn_s_gate"]).reshape(64, 128).T)
    g["s_u"] = np.ascontiguousarray(_f32(inputs["L0_ffn_s_up"]).reshape(64, 128).T)
    g["s_d"] = np.ascontiguousarray(_f32(inputs["L0_ffn_s_down"]).reshape(16, 128).T)
    g["ident_b"] = np.eye(128, dtype=ml_dtypes.bfloat16)
    g["ident_f"] = np.eye(128, dtype=np.float32)
    g["onesr"] = np.ones((1, 128), np.float32)
    g["onescol"] = np.ones((128, 1), np.float32)
    s = np.arange(128)[:, None]
    t = np.arange(256)[None, :]
    g["mask0"] = (t >= s).astype(ml_dtypes.bfloat16)
    cv = np.zeros((128, 2), np.float32)
    cv[:, 0] = 1e-5
    cv[:, 1] = 1.0
    g["cvals"] = cv
    return g


def build_program():
    nc = bass.Bass("TRN2", target_bir_lowering=False, debug=False)

    dx = nc.dram_tensor("x", [SEQ_PER_CORE * L, H], F32, kind="ExternalInput").ap()
    dout = nc.dram_tensor("out", [SEQ_PER_CORE * L, H], F32, kind="ExternalOutput").ap()
    dr = {}
    for n, shape, dt in [
        ("w_in", [34, KC, 128, 256], F32), ("b_in", [128, OCH], F32),
        ("convw", [128, 136], F32), ("dtb", [64, 1], F32), ("alog", [64, 1], F32),
        ("d_row", [1, 64], F32), ("nw", [128, 32], F32),
        ("w_out", [16, 32, 128, 128], F32), ("b_out", [128, 16], F32),
        ("w_g", [32, KC, 128, 256], I8), ("w_u", [32, KC, 128, 256], I8),
        ("w_d", [8, 4, 16, 128, 256], I8),
        ("s_g", [128, 64], F32), ("s_u", [128, 64], F32), ("s_d", [128, 16], F32),
        ("ident_b", [128, 128], BF16), ("ident_f", [128, 128], F32),
        ("onesr", [1, 128], F32), ("onescol", [128, 1], F32),
        ("mask0", [128, 256], BF16), ("cvals", [128, 2], F32),
    ]:
        dr[n] = nc.dram_tensor(n, shape, dt, kind="ExternalInput").ap()
    d_w_in_b = nc.dram_tensor("w_in_b", [34, 128, 4096], BF16, kind="Internal").ap()
    d_w_out_b = nc.dram_tensor("w_out_b", [16, 128, 4096], BF16, kind="Internal").ap()
    ddbg = {n: nc.dram_tensor("dbg_" + n, list(s), F32, kind="ExternalOutput").ap()
            for n, s in DBG.items()}

    ctx = ExitStack()
    sb = lambda n, s, d: ctx.enter_context(nc.sbuf_tensor(n, s, d))
    psb = lambda n, s, d: ctx.enter_context(nc.psum_tensor(n, s, d))

    C = {}
    for n, shape, dt in [
        ("identb", [128, 128], BF16), ("identf", [128, 128], F32),
        ("onesr", [1, 128], F32), ("onescol", [128, 1], F32),
        ("mask0", [128, 256], BF16), ("b_in", [128, OCH], F32),
        ("convw", [128, 136], F32), ("dtb", [64, 1], F32), ("alog", [64, 1], F32),
        ("d_row", [1, 64], F32), ("nw", [128, 32], F32), ("bout", [128, 16], F32),
        ("sg", [128, 64], F32), ("su", [128, 64], F32), ("sd", [128, 16], F32),
        ("acol", [64, 1], F32), ("bias66", [64, 1], F32),
        ("DB", [128, 64], F32), ("cvals", [128, 2], F32),
    ]:
        C[n] = sb("c_" + n, shape, dt)

    xn_sb = sb("xn", [128, KC * SPAN], BF16)
    gate_sb = sb("gate", [128, 32 * SPAN], BF16)
    xtm_sb = sb("xtm", [128, 4 * INTER], BF16)
    out_sb = sb("outtm", [128, 4 * H], F32)
    rbf_sb = sb("rbf", [128, NH * 64], BF16)
    arena = sb("arena", [128, 32768], U8)
    arf = arena[:].bitcast(F32)
    arb = arena[:].bitcast(BF16)
    lnconv = sb("lnconv", [128, 6208], U8)
    lcf = lnconv[:].bitcast(F32)
    ssb = sb("ssb", [128, 2 * 384], BF16)
    ybuf = sb("ybuf", [128, 4 * 256], BF16)
    xpb = sb("xpb", [128, 4 * 64], BF16)
    bc_sb = sb("bcfm", [128, 2 * SPAN], BF16)
    btm_sb = sb("btm", [128, 4 * 128], BF16)
    dtraw_sb = sb("dtraw", [64, SPAN], F32)
    halo_sb = sb("halo", [128, 34 * 3], F32)
    ffp_sb = sb("ffp", [128, 1024], BF16)
    tsil_sb = sb("tsil", [128, 1024], BF16)
    lnsm = sb("lnsm", [128, 16], F32)
    wsA = sb("wsA", [128, 16384], U8)
    wsB = sb("wsB", [128, 16384], U8)
    wsAb = wsA[:].bitcast(BF16)
    wsBb = wsB[:].bitcast(BF16)

    A_GM0, A_GM1 = 0, 256
    A_ARG = 384
    A_E = 1152
    A_G2, A_RSTD = 1920, 2432
    A_DT = 2944
    A_CATM = 4224
    A_RSR = 4800
    A_XPB = 5120
    A_SSX = 5632
    A_T2 = A_ARG + 384
    A_FFOUT = A_E
    A_CAR = 6144

    rstdrow_sb = arf[:, A_RSR:A_RSR + 256]
    etrow_sb = arf[:, A_RSR + 256:A_RSR + 320]
    etcol_sb = arf[0:64, A_RSR + 320:A_RSR + 321]

    def fuse_t(i):
        return arb[:, i * 512:(i + 1) * 512]

    print("sbuf bytes remaining:", nc.sbuf_bytes_remaining)

    pb = [psb(f"pb{i}", [128, 512], F32) for i in range(6)]
    pb6 = psb("pb6", [128, 1024], BF16)
    pb7 = psb("pb7", [128, 512], F32)
    pbb = {"pb6": pb6[:],
           "pb0b": pb[0][:].bitcast(BF16), "pb1b": pb[1][:].bitcast(BF16),
           "pb2b": pb[2][:].bitcast(BF16), "pb3b": pb[3][:].bitcast(BF16)}

    sems, cnt = {}, {}
    for name in ["ldx", "ldc", "ldcr0", "ldcr1", "st", "dbg",
                 "win0", "win1", "wout0", "wout1", "wgu0", "wgu1", "wd0", "wd1",
                 "cvA0", "cvA1", "cvB0", "cvB1", "cvl0", "cvl1", "cvm0", "cvm1",
                 "pe", "dve", "act", "pool"]:
        sems[name] = ctx.enter_context(nc.semaphore(name))
        cnt[name] = 0

    ops = []
    SELF_CH = {"vector": "dve", "scalar": "act", "gpsimd": "pool"}

    def emit(en, waits, fn, inc=None, inc_by=1):
        if en in SELF_CH and cnt[SELF_CH[en]] > 0:
            waits = list(waits) + [(SELF_CH[en], cnt[SELF_CH[en]])]
        if cnt["dbg"] > 0:
            waits = list(waits) + [("dbg", cnt["dbg"])]
        for c, v in waits:
            assert v <= cnt[c], f"wait on future event {c}:{v} > {cnt[c]}"
        if inc is not None:
            cnt[inc] += inc_by
        w = [(sems[c], v) for c, v in waits if v > 0]
        s_inc = (sems[inc], inc_by) if inc else None
        def f(eng, w=w, s_inc=s_inc, fn=fn):
            for sh, v in w:
                eng.wait_ge(sh, v)
            instr = fn(eng)
            if s_inc is not None:
                instr.then_inc(s_inc[0], s_inc[1])
        ops.append((en, f))

    def now(chn):
        return cnt[chn]

    class Ring:
        def __init__(self, ld0, ld1, depth=2):
            self.ld = [ld0, ld1]
            self.depth = depth
            self.loadrec = []
            self.freed = []
        def load(self, fn, engine="gpsimd", extra_waits=(), sem_override=None):
            k = len(self.loadrec)
            waits = list(extra_waits)
            j = k - self.depth
            if j >= 0:
                assert j < len(self.freed), "block consumed before reuse"
                waits.append(self.freed[j])
            slot = k % self.depth
            sname = (sem_override or self.ld)[slot]
            emit(engine, waits, lambda e, slot=slot: fn(e, slot),
                 inc=sname, inc_by=16)
            self.loadrec.append((sname, cnt[sname]))
            return k
        def use_wait(self, k):
            return self.loadrec[k]
        def free(self, ev):
            self.freed.append(ev)

    r_win = Ring("win0", "win1")
    r_wout = Ring("wout0", "wout1")
    r_wgu = Ring("wgu0", "wgu1")
    r_wd = Ring("wd0", "wd1")

    class TB:
        def __init__(self):
            self.state = {}
            self.cur = []
            self.i = 0
        def set_banks(self, keys, new_guards=None):
            for k in keys:
                if k not in self.state:
                    self.state[k] = {"last": 0, "guards": []}
                if new_guards and k in new_guards:
                    self.state[k]["guards"].extend(new_guards[k])
            self.cur = list(keys)
            self.i = 0
        def drain_of(self, key):
            return self.state[key]["last"] if key in self.state else 0
        def batch(self, entries, extra=()):
            key = self.cur[self.i]
            self.i = (self.i + 1) % len(self.cur)
            st = self.state[key]
            bank = pbb[key]
            assert len(entries) <= 8
            waits = list(extra) + st["guards"]
            st["guards"] = []
            if st["last"]:
                waits.append(("dve", st["last"]))
            def fill(e, entries=entries, bank=bank):
                last = None
                for j, (inap, idap, outap, evf) in enumerate(entries):
                    last = e.transpose(bank[:, j * 128:(j + 1) * 128], inap, idap)
                return last
            emit("tensor", waits, fill, inc="pe")
            fill_pe = now("pe")
            def drain(e, entries=entries, bank=bank, fill_pe=fill_pe):
                e.wait_ge(sems["pe"], fill_pe)
                last = None
                for j, (inap, idap, outap, evf) in enumerate(entries):
                    slot = bank[:, j * 128:(j + 1) * 128]
                    if evf is None:
                        last = e.tensor_copy(outap, slot)
                    else:
                        last = evf(e, slot)
                return last
            emit("vector", [], drain, inc="dve")
            st["last"] = now("dve")
            return fill_pe, st["last"]

    tbb = TB()

    def dbg_tap(name, src_ap):
        if name in DBG:
            emit("gpsimd", [("pe", now("pe")), ("dve", now("dve")), ("act", now("act"))],
                 lambda e, s=src_ap, d=ddbg[name]: e.dma_start(out=d, in_=s),
                 inc="dbg", inc_by=16)

    const_pairs = [
        (C["identb"], dr["ident_b"]), (C["identf"], dr["ident_f"]),
        (C["onesr"], dr["onesr"]), (C["onescol"], dr["onescol"]),
        (C["mask0"], dr["mask0"]), (C["b_in"], dr["b_in"]),
        (C["convw"], dr["convw"]), (C["dtb"], dr["dtb"]), (C["alog"], dr["alog"]),
        (C["d_row"], dr["d_row"]), (C["nw"], dr["nw"]), (C["bout"], dr["b_out"]),
        (C["sg"], dr["s_g"]), (C["su"], dr["s_u"]), (C["sd"], dr["s_d"]),
        (C["cvals"], dr["cvals"]),
    ]
    for dst, src in const_pairs:
        emit("sync", [], lambda e, d=dst, s=src: e.dma_start(out=d[:], in_=s),
             inc="ldc", inc_by=16)
    NCONST = now("ldc")
    emit("vector", [("ldc", NCONST)], lambda e: e.memset(rbf_sb[:], 0.0), inc="dve")
    emit("vector", [], lambda e: e.memset(halo_sb[:], 0.0), inc="dve")
    emit("scalar", [("ldc", NCONST)],
         lambda e: e.activation(C["acol"][:], C["alog"][:], AF.Exp), inc="act")
    emit("vector", [("act", now("act"))],
         lambda e: e.tensor_scalar(C["acol"][:], C["acol"][:], -1.0, None, ALU.mult),
         inc="dve")
    emit("vector", [("ldc", NCONST)],
         lambda e: e.tensor_tensor(C["bias66"][:], C["b_in"][0:64, 66:67], C["dtb"][:],
                                   ALU.add), inc="dve")
    emit("tensor", [("ldc", NCONST)],
         lambda e: e.matmul(pb7[:, 0:64], C["onesr"][:], C["d_row"][:],
                            start=True, stop=True), inc="pe")
    emit("vector", [("pe", now("pe"))],
         lambda e: e.tensor_copy(C["DB"][:], pb7[:, 0:64]), inc="dve")
    pb7_last = now("dve")
    INIT_DVE = now("dve")
    INIT_ACT = now("act")

    for og in range(34):
        k = r_win.load(lambda e, slot, og=og: e.dma_start(
            out=wsAb[:, slot * 4096: slot * 4096 + 4096],
            in_=dr["w_in"][og].rearrange("kc p o -> p kc o")),
            sem_override=("cvl0", "cvl1"))
        sname = "cvA%d" % (k % 2)
        emit("sync", [r_win.use_wait(k)],
             lambda e, og=og, k=k: e.dma_start(
                 out=d_w_in_b[og], in_=wsAb[:, (k % 2) * 4096: (k % 2) * 4096 + 4096]),
             inc=sname, inc_by=16)
        r_win.free((sname, cnt[sname]))
    for og in range(16):
        k = r_wout.load(lambda e, slot, og=og: e.dma_start(
            out=wsBb[:, slot * 4096: slot * 4096 + 4096],
            in_=dr["w_out"][og].rearrange("kc p o -> p kc o")),
            sem_override=("cvm0", "cvm1"))
        sname = "cvB%d" % (k % 2)
        emit("sync", [r_wout.use_wait(k)],
             lambda e, og=og, k=k: e.dma_start(
                 out=d_w_out_b[og], in_=wsBb[:, (k % 2) * 4096: (k % 2) * 4096 + 4096]),
             inc=sname, inc_by=16)
        r_wout.free((sname, cnt[sname]))

    rupd4 = [INIT_DVE] * 16
    car_k = 0
    car_free_pe = []
    S = {"outproj_pe": 0, "ssm_pe": 0, "ffn_pe": 0, "arena_dve": 0,
         "pb01_dve": 0, "pb01_act": 0, "pb4_act": 0,
         "hb2": (0, 0), "hb3": (0, 0), "pb4_dve": 0, "pb5_dve": 0,
         "yo0_dve": 0, "yo1_dve": 0}

    for span in range(NSPAN):
        row0 = (span // (L // SPAN)) * L + (span % (L // SPAN)) * SPAN
        seq_start = (span % (L // SPAN)) == 0

        st_before = now("st")
        for tq in range(4):
            emit("sync", [("st", st_before)],
                 lambda e, tq=tq, row0=row0: e.dma_start(
                     out=out_sb[:, tq * H:(tq + 1) * H],
                     in_=dx[row0 + tq * 128: row0 + (tq + 1) * 128, :]),
                 inc="ldx", inc_by=16)
        ldx_now = now("ldx")

        tbb.set_banks(["pb6", "pb2b", "pb3b"],
                      new_guards=({"pb2b": [("act", S["hb2"][0]), ("dve", S["hb2"][1])],
                                   "pb3b": [("act", S["hb3"][0]), ("dve", S["hb3"][1])]}
                                  if span else None))
        xn_guard = [("pe", S["ffn_pe"])] if span else []
        xn_act_ev = []
        xn_fill_pe = []
        for tq in range(4):
            xt = out_sb[:, tq * H:(tq + 1) * H]
            cb = (tq % 2) * 8
            w0 = [("ldx", ldx_now), ("dve", INIT_DVE)]
            if tq >= 2:
                w0.append(("act", xn_act_ev[tq - 2]))
            emit("vector", w0,
                 lambda e, xt=xt, cb=cb: e.reduce_sum(lnsm[:, cb:cb + 1], xt, axis=AXX),
                 inc="dve")
            sq_w = [("ldx", ldx_now), ("act", INIT_ACT)]
            if span:
                sq_w.append(("pe", S["ssm_pe"]))
                sq_w.append(("dve", S["arena_dve"]))
            emit("scalar", sq_w,
                 lambda e, xt=xt, cb=cb: e.activation(arf[:, 0:2048], xt, AF.Square,
                                                      accum_out=lnsm[:, cb + 1:cb + 2]),
                 inc="act")
            emit("vector", [("act", now("act"))],
                 lambda e, cb=cb: e.tensor_scalar(lnsm[:, cb + 2:cb + 3],
                                                  lnsm[:, cb:cb + 1], 1.0 / H, None,
                                                  ALU.mult), inc="dve")
            emit("vector", [],
                 lambda e, cb=cb: e.tensor_tensor(lnsm[:, cb + 3:cb + 4],
                                                  lnsm[:, cb + 2:cb + 3],
                                                  lnsm[:, cb + 2:cb + 3], ALU.mult),
                 inc="dve")
            emit("vector", [],
                 lambda e, cb=cb: e.scalar_tensor_tensor(
                     lnsm[:, cb + 4:cb + 5], lnsm[:, cb + 1:cb + 2], 1.0 / H,
                     lnsm[:, cb + 3:cb + 4], ALU.mult, ALU.subtract), inc="dve")
            emit("scalar", [("dve", now("dve"))],
                 lambda e, cb=cb: e.activation(lnsm[:, cb + 5:cb + 6],
                                               lnsm[:, cb + 4:cb + 5], AF.Ln,
                                               bias=C["cvals"][:, 0:1]), inc="act")
            emit("scalar", [],
                 lambda e, cb=cb: e.activation(lnsm[:, cb + 5:cb + 6],
                                               lnsm[:, cb + 5:cb + 6], AF.Exp,
                                               scale=-0.5), inc="act")
            emit("vector", [("act", now("act"))],
                 lambda e, cb=cb: e.tensor_tensor(lnsm[:, cb + 6:cb + 7],
                                                  lnsm[:, cb + 2:cb + 3],
                                                  lnsm[:, cb + 5:cb + 6], ALU.mult),
                 inc="dve")
            emit("vector", [],
                 lambda e, cb=cb: e.tensor_scalar(lnsm[:, cb + 6:cb + 7],
                                                  lnsm[:, cb + 6:cb + 7], -1.0, None,
                                                  ALU.mult), inc="dve")
            slot = tq % 2
            xnw = [("dve", now("dve"))]
            if tq >= 2:
                xnw.append(("pe", xn_fill_pe[tq - 2]))
            emit("scalar", xnw,
                 lambda e, xt=xt, slot=slot, cb=cb: e.activation(
                     arb[:, 4096 + slot * 2048: 4096 + (slot + 1) * 2048], xt,
                     AF.Identity, bias=lnsm[:, cb + 6:cb + 7],
                     scale=lnsm[:, cb + 5:cb + 6]), inc="act")
            xn_act_ev.append(now("act"))
            xa = now("act")
            fp_last = 0
            for half in range(2):
                entries = []
                for j in range(8):
                    kc = half * 8 + j
                    entries.append((
                        arb[:, 4096 + slot * 2048 + kc * 128:
                            4096 + slot * 2048 + (kc + 1) * 128],
                        C["identb"][:],
                        xn_sb[:, kc * SPAN + tq * 128: kc * SPAN + (tq + 1) * 128],
                        None))
                fp, _ = tbb.batch(entries, extra=[("act", xa)] + xn_guard)
                fp_last = fp
            xn_fill_pe.append(fp_last)
        ln_done_dve = now("dve")
        if span == 0:
            dbg_tap("xn", xn_sb[:])

        gate_guard = [("pe", S["outproj_pe"])] if span else []
        conv_dve_ev = [0] * 34
        silu_act_ev = [0] * 34
        xtr_fill_pe = [0] * 34
        xtr_pend = []
        bank_evac = {0: 0, 1: 0}
        pb01_guard = ([("dve", S["pb01_dve"])] if span else [])
        dtraw_act = 0
        for og in range(34):
            cw = []
            if span == 0:
                cw.append(("cvA%d" % (og % 2), (og // 2 + 1) * 16))
            if og < 2:
                cw.extend(r_wd.freed[-2:])
            k = r_win.load(lambda e, slot, og=og: e.dma_start(
                out=wsAb[:, slot * 4096: slot * 4096 + 4096],
                in_=d_w_in_b[og]), engine="sync", extra_waits=cw)
            for oc in range(2):
                o = og * 2 + oc
                if o == 67:
                    continue
                bank = o % 2
                pw = [r_win.use_wait(k), ("dve", ln_done_dve)]
                if bank_evac[bank]:
                    pw.append(("act", bank_evac[bank]))
                else:
                    pw += pb01_guard
                bufbase = (k % 2) * 4096
                def mms(e, bufbase=bufbase, oc=oc, bank=bank):
                    last = None
                    for kc in range(KC):
                        last = e.matmul(
                            pb[bank][:],
                            wsAb[:, bufbase + kc * 256 + oc * 128:
                                 bufbase + kc * 256 + oc * 128 + 128],
                            xn_sb[:, kc * SPAN: kc * SPAN + 512],
                            start=(kc == 0), stop=(kc == KC - 1))
                    return last
                emit("tensor", pw, mms, inc="pe")
                if oc == 1 or o == 66:
                    r_win.free(("pe", now("pe")))
                mmw = now("pe")
                if o < 32:
                    fc = o
                    emit("scalar", [("pe", mmw)] + gate_guard,
                         lambda e, fc=fc, bank=bank: e.activation(
                             gate_sb[:, fc * SPAN:(fc + 1) * SPAN], pb[bank][:],
                             AF.Silu, bias=C["b_in"][:, fc:fc + 1]), inc="act")
                elif o < 66:
                    fc = o - 32
                    ring = fc % 3
                    hbct = lcf[:, ring * 516:(ring + 1) * 516]
                    gw = [("pe", mmw)]
                    if fc >= 3:
                        gw.append(("dve", conv_dve_ev[fc - 3]))
                    emit("scalar", gw,
                         lambda e, hbct=hbct, bank=bank, o=o: e.activation(
                             hbct[:, 3:515], pb[bank][:], AF.Identity,
                             bias=C["b_in"][:, o:o + 1]), inc="act")
                    hev = now("act")
                    def halo_fill(e, hbct=hbct, fc=fc, seq_start=seq_start):
                        if seq_start:
                            return e.memset(hbct[:, 0:3], 0.0)
                        return e.tensor_copy(hbct[:, 0:3], halo_sb[:, fc * 3:fc * 3 + 3])
                    emit("vector", [("act", hev)], halo_fill, inc="dve")
                    acc = arf[:, 4096 + (fc % 2) * 512: 4096 + (fc % 2) * 512 + 512]
                    accw = [("act", silu_act_ev[fc - 2])] if fc >= 2 else []
                    emit("vector", accw,
                         lambda e, hbct=hbct, acc=acc, fc=fc: e.tensor_scalar(
                             acc, hbct[:, 0:512], C["convw"][:, fc * 4:fc * 4 + 1],
                             None, ALU.mult), inc="dve")
                    for j in range(1, 4):
                        emit("vector", [],
                             lambda e, hbct=hbct, acc=acc, fc=fc, j=j: e.scalar_tensor_tensor(
                                 acc, hbct[:, j:512 + j],
                                 C["convw"][:, fc * 4 + j:fc * 4 + j + 1],
                                 acc, ALU.mult, ALU.add), inc="dve")
                    emit("vector", [],
                         lambda e, hbct=hbct, fc=fc: e.tensor_copy(
                             halo_sb[:, fc * 3:fc * 3 + 3], hbct[:, 512:515]), inc="dve")
                    conv_dve_ev[fc] = now("dve")
                    if fc < 32:
                        dst = arb[:, 10240 + (fc % 4) * 512: 10240 + (fc % 4) * 512 + 512]
                    elif fc == 32:
                        dst = bc_sb[:, 0:512]
                    else:
                        dst = bc_sb[:, 512:1024]
                    sw = [("dve", now("dve"))]
                    if 4 <= fc and fc - 4 < 32:
                        sw.append(("pe", xtr_fill_pe[fc - 4]))
                    emit("scalar", sw,
                         lambda e, acc=acc, dst=dst: e.activation(dst, acc, AF.Silu),
                         inc="act")
                    silu_act_ev[fc] = now("act")
                    if fc < 32:
                        entries = []
                        for tt in range(4):
                            entries.append((
                                arb[:, 10240 + (fc % 4) * 512 + tt * 128:
                                    10240 + (fc % 4) * 512 + (tt + 1) * 128],
                                C["identb"][:],
                                xtm_sb[:, tt * INTER + fc * 128: tt * INTER + (fc + 1) * 128],
                                None))
                        xg = [("act", silu_act_ev[fc])]
                        if span:
                            xg.append(("pe", S["ssm_pe"]))
                        xtr_pend.append((fc, entries, xg))
                        if len(xtr_pend) > 2:
                            pfc, pent, pxg = xtr_pend.pop(0)
                            fp, _ = tbb.batch(pent, extra=pxg)
                            xtr_fill_pe[pfc] = fp
                    elif fc == 32:
                        while xtr_pend:
                            pfc, pent, pxg = xtr_pend.pop(0)
                            fp, _ = tbb.batch(pent, extra=pxg)
                            xtr_fill_pe[pfc] = fp
                        entries = []
                        for tt in range(4):
                            entries.append((
                                bc_sb[:, tt * 128:(tt + 1) * 128], C["identb"][:],
                                btm_sb[:, tt * 128:(tt + 1) * 128], None))
                        bg2 = [("act", silu_act_ev[32])]
                        if span:
                            bg2.append(("pe", S["ssm_pe"]))
                        tbb.batch(entries, extra=bg2)
                else:
                    emit("scalar", [("pe", mmw)],
                         lambda e, bank=bank: e.activation(
                             dtraw_sb[:, :], pb[bank][0:64, :], AF.Identity,
                             bias=C["bias66"][:]), inc="act")
                    dtraw_act = now("act")
                bank_evac[bank] = now("act")
        if span == 0:
            dbg_tap("gate", gate_sb[:])
            dbg_tap("xtm", xtm_sb[:])
            dbg_tap("bcfm", bc_sb[:])
            dbg_tap("dtraw", dtraw_sb[:])

        tbb.set_banks(["pb6", "pb0b", "pb1b"],
                      new_guards={"pb0b": [("act", bank_evac[0]),
                                           ("dve", S["yo0_dve"])] + pb01_guard,
                                  "pb1b": [("act", bank_evac[1]),
                                           ("dve", S["yo1_dve"])] + pb01_guard})
        gu_bank_guard = {2: [("dve", tbb.drain_of("pb2b")), ("act", S["hb2"][0]),
                             ("dve", S["hb2"][1])],
                         3: [("dve", tbb.drain_of("pb3b")), ("act", S["hb3"][0]),
                             ("dve", S["hb3"][1])],
                         4: [("dve", S["pb4_dve"]), ("act", S["pb4_act"])],
                         5: [("dve", S["pb5_dve"])]}
        gu_bank_act = {2: 0, 3: 0, 4: 0, 5: 0}
        gu_bank_dve = {2: 0, 3: 0, 4: 0, 5: 0}
        out_add_ev = [0, 0, 0, 0]
        outadd_guard_act = xn_act_ev[3]
        tsil_ev = [0, 0]
        dn_ev = {2: 0, 3: 0, 4: 0, 5: 0}
        ffn_tr_pend = []
        for ih in range(2):
            for ig in range(16):
                igg = ih * 16 + ig
                xgw = r_wout.freed[-2:] if (ih == 0 and ig == 0) else ()
                kg = r_wgu.load(lambda e, slot, igg=igg: e.dma_start(
                    out=wsBb[:, slot * 4096: slot * 4096 + 4096],
                    in_=dr["w_g"][igg].rearrange("kc p o -> p kc o")),
                    extra_waits=xgw)
                gbuf = (kg % 2) * 4096
                for isub in range(2):
                    bg_ = 2 + isub
                    gw = [r_wgu.use_wait(kg)] + gu_bank_guard[bg_]
                    gu_bank_guard[bg_] = []
                    if gu_bank_act[bg_]:
                        gw.append(("act", gu_bank_act[bg_]))
                    if dn_ev[bg_]:
                        gw.append(("dve", dn_ev[bg_]))
                    def gmms(e, gbuf=gbuf, isub=isub, bg_=bg_):
                        last = None
                        for kc in range(KC):
                            last = e.matmul(
                                pb[bg_][:],
                                wsBb[:, gbuf + kc * 256 + isub * 128:
                                     gbuf + kc * 256 + isub * 128 + 128],
                                xn_sb[:, kc * SPAN: kc * SPAN + 512],
                                start=(kc == 0), stop=(kc == KC - 1))
                        return last
                    emit("tensor", gw, gmms, inc="pe")
                r_wgu.free(("pe", now("pe")))
                g_pe = now("pe")
                ku = r_wgu.load(lambda e, slot, igg=igg: e.dma_start(
                    out=wsBb[:, slot * 4096: slot * 4096 + 4096],
                    in_=dr["w_u"][igg].rearrange("kc p o -> p kc o")))
                ubuf = (ku % 2) * 4096
                for isub in range(2):
                    bu_ = 4 + isub
                    uw = [r_wgu.use_wait(ku)] + gu_bank_guard[bu_]
                    gu_bank_guard[bu_] = []
                    if gu_bank_dve[bu_]:
                        uw.append(("dve", gu_bank_dve[bu_]))
                    if dn_ev[bu_]:
                        uw.append(("dve", dn_ev[bu_]))
                    def umms(e, ubuf=ubuf, isub=isub, bu_=bu_):
                        last = None
                        for kc in range(KC):
                            last = e.matmul(
                                pb[bu_][:],
                                wsBb[:, ubuf + kc * 256 + isub * 128:
                                     ubuf + kc * 256 + isub * 128 + 128],
                                xn_sb[:, kc * SPAN: kc * SPAN + 512],
                                start=(kc == 0), stop=(kc == KC - 1))
                        return last
                    emit("tensor", uw, umms, inc="pe")
                r_wgu.free(("pe", now("pe")))
                u_pe = now("pe")
                for isub in range(2):
                    i_g = igg * 2 + isub
                    il = ig * 2 + isub
                    tw = [("pe", g_pe)]
                    if tsil_ev[isub]:
                        tw.append(("dve", tsil_ev[isub]))
                    emit("scalar", tw,
                         lambda e, isub=isub, i_g=i_g: e.activation(
                             tsil_sb[:, isub * 512:(isub + 1) * 512],
                             pb[2 + isub][:], AF.Silu,
                             scale=C["sg"][:, i_g:i_g + 1]), inc="act")
                    emit("vector", [("act", now("act")), ("pe", u_pe)],
                         lambda e, isub=isub, i_g=i_g, il=il: e.scalar_tensor_tensor(
                             fuse_t(il), pb[4 + isub][:], C["su"][:, i_g:i_g + 1],
                             tsil_sb[:, isub * 512:(isub + 1) * 512],
                             ALU.mult, ALU.mult), inc="dve")
                    tsil_ev[isub] = now("dve")
                    gu_bank_act[2 + isub] = now("act")
                    gu_bank_dve[4 + isub] = now("dve")
            last_fuse_dve = now("dve")
            if span == 0 and ih == 0:
                dbg_tap("fuse", arb[:, 0:16384])
            for fop in range(8):
                bA, bB = (2, 3) if fop % 2 == 0 else (4, 5)
                fw = [("dve", last_fuse_dve)]
                for b_ in (bA, bB):
                    if dn_ev[b_]:
                        fw.append(("dve", dn_ev[b_]))
                    else:
                        if gu_bank_act[b_]:
                            fw.append(("act", gu_bank_act[b_]))
                        if gu_bank_dve[b_]:
                            fw.append(("dve", gu_bank_dve[b_]))
                first = True
                for ihq2 in range(2):
                    ihq = ih * 2 + ihq2
                    xdw = (r_win.freed[-2:]
                           if (ih == 0 and fop == 0 and ihq2 == 0) else ())
                    kd = r_wd.load(lambda e, slot, fop=fop, ihq=ihq: e.dma_start(
                        out=wsAb[:, slot * 4096: slot * 4096 + 4096],
                        in_=dr["w_d"][fop, ihq].rearrange("ic p o -> p ic o")),
                        extra_waits=xdw)
                    dbuf = (kd % 2) * 4096
                    w = (list(fw) if first else []) + [r_wd.use_wait(kd)]
                    def dmms(e, dbuf=dbuf, bA=bA, bB=bB, first=first, ihq2=ihq2):
                        last = None
                        for fh, bk in ((0, bA), (1, bB)):
                            for ic in range(16):
                                last = e.matmul(
                                    pb[bk][:],
                                    wsAb[:, dbuf + ic * 256 + fh * 128:
                                         dbuf + ic * 256 + fh * 128 + 128],
                                    fuse_t(ihq2 * 16 + ic),
                                    start=(first and ic == 0),
                                    stop=(ihq2 == 1 and ic == 15))
                        return last
                    emit("tensor", w, dmms, inc="pe")
                    r_wd.free(("pe", now("pe")))
                    first = False
                d_pe = now("pe")
                while ffn_tr_pend:
                    pent, pex = ffn_tr_pend.pop(0)
                    _, dd = tbb.batch(pent, extra=pex)
                    for tq in range(4):
                        out_add_ev[tq] = dd
                for fh, bk in ((0, bA), (1, bB)):
                    fo = fop * 2 + fh
                    pslot = fo % 2
                    emit("vector", [("pe", d_pe)],
                         lambda e, bk=bk, fo=fo, pslot=pslot: e.tensor_scalar(
                             ffp_sb[:, pslot * 512:(pslot + 1) * 512],
                             pb[bk][:], C["sd"][:, fo:fo + 1], None, ALU.mult),
                         inc="dve")
                    pev = now("dve")
                    dn_ev[bk] = pev
                    entries = []
                    for tq in range(4):
                        def evf(e, slot, tq=tq, fo=fo):
                            return e.tensor_tensor(
                                out_sb[:, tq * H + fo * 128: tq * H + (fo + 1) * 128],
                                out_sb[:, tq * H + fo * 128: tq * H + (fo + 1) * 128],
                                slot, ALU.add)
                        entries.append((
                            ffp_sb[:, pslot * 512 + tq * 128: pslot * 512 + (tq + 1) * 128],
                            C["identb"][:], None, evf))
                    ffn_tr_pend.append((entries,
                                        [("dve", pev), ("act", outadd_guard_act),
                                         ("ldx", ldx_now)]))
        while ffn_tr_pend:
            pent, pex = ffn_tr_pend.pop(0)
            _, dd = tbb.batch(pent, extra=pex)
            for tq in range(4):
                out_add_ev[tq] = dd
        S["ffn_pe"] = now("pe")
        ffn_dn_ev = dict(dn_ev)

        tbb.set_banks(["pb6"])
        for cl in range(2):
            DT = A_DT
            dt_fm = arf[0:64, DT:DT + 256]
            ca_fm = arf[0:64, DT + 256:DT + 512]
            lndt = arf[0:64, DT + 512:DT + 768]
            scr1 = arf[0:64, DT + 768:DT + 1024]
            scr2 = arf[0:64, DT + 1024:DT + 1280]
            dts = dtraw_sb[:, cl * 256:(cl + 1) * 256]
            dtg = [("act", dtraw_act), ("pe", now("pe"))]
            emit("scalar", dtg,
                 lambda e, dts=dts, scr1=scr1: e.activation(scr1, dts, AF.Abs),
                 inc="act")
            emit("scalar", [],
                 lambda e, scr1=scr1: e.activation(scr1, scr1, AF.Exp, scale=-1.0),
                 inc="act")
            emit("scalar", [],
                 lambda e, scr1=scr1: e.activation(scr1, scr1, AF.Ln,
                                                   bias=C["cvals"][0:64, 1:2]),
                 inc="act")
            emit("vector", [("act", now("act"))],
                 lambda e, dts=dts, scr2=scr2: e.tensor_scalar(
                     scr2, dts, 0.0, None, ALU.max), inc="dve")
            emit("vector", [],
                 lambda e, dt_fm=dt_fm, scr1=scr1, scr2=scr2: e.tensor_tensor(
                     dt_fm, scr1, scr2, ALU.add), inc="dve")
            emit("scalar", [("dve", now("dve"))],
                 lambda e, lndt=lndt, dt_fm=dt_fm: e.activation(lndt, dt_fm, AF.Ln),
                 inc="act")
            emit("vector", [],
                 lambda e, scr1=scr1, dt_fm=dt_fm: e.tensor_scalar(
                     scr1, dt_fm, C["acol"][:], None, ALU.mult), inc="dve")
            emit("vector", [],
                 lambda e, ca_fm=ca_fm, scr1=scr1: e.tensor_tensor_scan(
                     ca_fm, scr1, scr1, 0.0, ALU.add, ALU.bypass), inc="dve")
            ca_dve = now("dve")
            emit("vector", [("act", now("act"))],
                 lambda e, scr2=scr2, lndt=lndt, ca_fm=ca_fm: e.tensor_tensor(
                     scr2, lndt, ca_fm, ALU.subtract), inc="dve")
            emit("scalar", [("dve", now("dve"))],
                 lambda e, scr1=scr1, ca_fm=ca_fm: e.activation(
                     scr1, ca_fm, AF.Exp, scale=-1.0, bias=ca_fm[:, 255:256]),
                 inc="act")
            emit("vector", [("act", now("act"))],
                 lambda e, scr1=scr1, dt_fm=dt_fm: e.tensor_tensor(
                     scr1, scr1, dt_fm, ALU.mult), inc="dve")
            emit("scalar", [],
                 lambda e, ca_fm=ca_fm: e.activation(etcol_sb[:], ca_fm[:, 255:256],
                                                     AF.Exp), inc="act")
            emit("scalar", [("dve", now("dve"))],
                 lambda e, dt_fm=dt_fm, ca_fm=ca_fm: e.activation(dt_fm, ca_fm, AF.Exp),
                 inc="act")
            sm_dve = now("dve")
            sm_act = now("act")
            tw = [("dve", sm_dve), ("act", sm_act)]
            if pb7_last:
                tw.append(("dve", pb7_last))
            def dtfill(e, dt_fm=dt_fm, scr1=scr1, scr2=scr2, lndt=lndt):
                for q in range(2):
                    e.transpose(pb7[:, q * 64:(q + 1) * 64],
                                dt_fm[:, q * 128:(q + 1) * 128], C["identf"][0:64, 0:64])
                for q in range(2):
                    e.transpose(pb7[:, 128 + q * 64:128 + (q + 1) * 64],
                                scr2[:, q * 128:(q + 1) * 128], C["identf"][0:64, 0:64])
                for q in range(2):
                    e.transpose(pb7[:, 256 + q * 64:256 + (q + 1) * 64],
                                scr1[:, q * 128:(q + 1) * 128], C["identf"][0:64, 0:64])
                e.transpose(pb7[:, 384 + 0 * 64:384 + 64],
                            lndt[:, 0:128], C["identf"][0:64, 0:64])
                return e.transpose(pb7[:, 448:512],
                                   lndt[:, 128:256], C["identf"][0:64, 0:64])
            emit("tensor", tw, dtfill, inc="pe")
            dtf_pe = now("pe")
            def dtdrain(e, dtf_pe=dtf_pe):
                e.wait_ge(sems["pe"], dtf_pe)
                return e.tensor_copy(arf[:, A_CATM:A_CATM + 512], pb7[:, 0:512])
            emit("vector", [], dtdrain, inc="dve")
            catm_dve = now("dve")
            emit("vector", [],
                 lambda e: e.tensor_tensor(arf[:, A_ARG + 256:A_ARG + 384],
                                           arf[:, A_CATM + 128:A_CATM + 256],
                                           arf[:, A_CATM + 384:A_CATM + 512],
                                           ALU.subtract), inc="dve")
            nca_dve = now("dve")
            def etfill(e):
                return e.transpose(pb7[0:1, 0:64], etcol_sb[:], C["identf"][0:64, 0:64])
            emit("tensor", [("dve", catm_dve)], etfill, inc="pe")
            etf_pe = now("pe")
            def etdrain(e, etf_pe=etf_pe):
                e.wait_ge(sems["pe"], etf_pe)
                return e.tensor_copy(etrow_sb[0:1, :], pb7[0:1, 0:64])
            emit("vector", [], etdrain, inc="dve")
            emit("tensor", [("dve", now("dve"))],
                 lambda e: e.matmul(pb7[:, 64:128], C["onesr"][:], etrow_sb[0:1, :],
                                    start=True, stop=True, skip_group_check=True),
                 inc="pe")
            emit("vector", [("pe", now("pe"))],
                 lambda e: e.tensor_copy(arf[:, A_CATM + 512:A_CATM + 576],
                                         pb7[:, 64:128]), inc="dve")
            etb_dve = now("dve")
            pb7_last = etb_dve

            gmw = [("act", silu_act_ev[33]), ("act", now("act")), ("dve", now("dve"))]
            def gmm(e, cl=cl):
                e.matmul(pb[4][:, 0:256], bc_sb[:, cl * 256:cl * 256 + 128],
                         bc_sb[:, 512 + cl * 256:512 + (cl + 1) * 256],
                         start=True, stop=False, skip_group_check=True)
                return e.matmul(pb[4][:, 256:512],
                                bc_sb[:, cl * 256 + 128:cl * 256 + 256],
                                bc_sb[:, 512 + cl * 256:512 + (cl + 1) * 256],
                                start=False, stop=True, skip_group_check=True)
            emit("tensor", gmw, gmm, inc="pe")
            emit("vector", [("pe", now("pe"))],
                 lambda e: e.tensor_tensor(arf[:, A_GM0:A_GM0 + 256], pb[4][:, 0:256],
                                           C["mask0"][:], ALU.mult), inc="dve")
            emit("vector", [],
                 lambda e: e.tensor_tensor(arf[:, A_GM1:A_GM1 + 128], pb[4][:, 384:512],
                                           C["mask0"][:, 0:128], ALU.mult), inc="dve")
            gm_dve = now("dve")
            if span == 0 and cl == 0:
                dbg_tap("gm", arf[:, 0:384])

            if seq_start and cl == 0:
                emit("vector", [("pe", S["ssm_pe"])] if span else [],
                     lambda e: e.memset(rbf_sb[:], 0.0), inc="dve")
                for g_ in range(16):
                    rupd4[g_] = now("dve")

            def S_ap(slot):
                if slot < 2:
                    return ssb[:, slot * 384:(slot + 1) * 384]
                return arb[:, 11264 + (slot - 2) * 384: 11264 + (slot - 1) * 384]

            PD = 4
            E_ev = [0] * NH
            S_ev = [0] * NH
            ymm_ev = [0] * NH
            cab_pe_ev = [0] * 32
            t2_ev = [0] * 16
            y4_ev = [0] * 16
            st_pe_ev = [0] * 16
            gdrain = [0] * 32
            ytr_pend = []
            ytr_fill = [0] * 16
            car_base = car_k
            for g4 in range(2):
                cw = [("dve", ca_dve)]
                if car_k >= 2:
                    cw.append(("pe", car_free_pe[car_k - 2]))
                emit("gpsimd", cw,
                     lambda e, g4=g4, ca_fm=ca_fm: e.dma_start(
                         out=arf[0:1, A_CAR + (g4 % 2) * 1024:
                                 A_CAR + (g4 % 2) * 1024 + 1024],
                         in_=ca_fm[g4 * 4:(g4 + 1) * 4, :]),
                     inc="ldcr%d" % (car_k % 2), inc_by=16)
                car_k += 1
            for hh in range(NH + PD):
                if hh >= PD:
                    h = hh - PD
                    g = h // 4
                    hg = h % 4
                    gb = pb[2 + g % 2]
                    sap = S_ap(h % 4)
                    yw = [("pool", S_ev[h])]
                    if hg == 0:
                        if g >= 2:
                            yw.append(("dve", y4_ev[g - 2]))
                        else:
                            yw.append(("dve", ffn_dn_ev.get(2 + g % 2, 0)))
                            ga = gu_bank_act.get(2 + g % 2, 0)
                            if ga:
                                yw.append(("act", ga))
                            hb = S["hb2"] if g % 2 == 0 else S["hb3"]
                            if hb[1]:
                                yw.append(("dve", hb[1]))
                    tt0, tt1 = 2 * cl, 2 * cl + 1
                    def ydmms(e, gb=gb, sap=sap, hg=hg, h=h, tt0=tt0, tt1=tt1):
                        hc = h * 64
                        e.matmul(gb[:, hg * 128:hg * 128 + 64], sap[:, 0:128],
                                 xtm_sb[:, tt0 * INTER + hc:tt0 * INTER + hc + 64],
                                 start=(hg == 0), stop=False, skip_group_check=True)
                        e.matmul(gb[:, hg * 128 + 64:hg * 128 + 128],
                                 sap[:, 128:256],
                                 xtm_sb[:, tt0 * INTER + hc:tt0 * INTER + hc + 64],
                                 start=False, stop=False, skip_group_check=True)
                        return e.matmul(gb[:, hg * 128 + 64:hg * 128 + 128],
                                        sap[:, 256:384],
                                        xtm_sb[:, tt1 * INTER + hc:tt1 * INTER + hc + 64],
                                        start=False, stop=(hg == 3),
                                        skip_group_check=True)
                    emit("tensor", yw, ydmms, inc="pe")
                    ymm_ev[h] = now("pe")
                    if hg == 3:
                        for sh in range(2):
                            xw = [("dve", catm_dve)]
                            if g >= 2:
                                xw.append(("pe", st_pe_ev[g - 2]))
                            emit("vector", xw,
                                 lambda e, sh=sh, g=g, cl=cl: e.tensor_tensor(
                                     arb[:, 10240 + ((g % 2) * 2 + sh) * 256:
                                         10240 + ((g % 2) * 2 + sh) * 256 + 256]
                                     .rearrange("p (h w) -> p h w", h=4),
                                     xtm_sb[:, (2 * cl + sh) * INTER + g * 256:
                                            (2 * cl + sh) * INTER + g * 256 + 256]
                                     .rearrange("p (h w) -> p h w", h=4),
                                     arf[:, A_CATM + 256 + sh * 64 + 4 * g:
                                         A_CATM + 256 + sh * 64 + 4 * g + 4]
                                     .unsqueeze(2).broadcast_to([128, 4, 64]),
                                     ALU.mult), inc="dve")
                        xp_dve = now("dve")
                        yob = pb[g % 2]
                        ow = [("dve", rupd4[g])]
                        if g >= 2:
                            ow.append(("dve", t2_ev[g - 2]))
                        else:
                            ow.append(("dve", tbb.drain_of("pb0b" if g % 2 == 0
                                                           else "pb1b")))
                            yo0 = S.get("yo%d_dve" % (g % 2), 0)
                            if yo0:
                                ow.append(("dve", yo0))
                        def yomms(e, yob=yob, g=g, cl=cl):
                            e.matmul(yob[:, 0:256],
                                     bc_sb[:, 512 + cl * 256:512 + cl * 256 + 128],
                                     rbf_sb[:, g * 256:(g + 1) * 256],
                                     start=True, stop=True, skip_group_check=True)
                            return e.matmul(yob[:, 256:512],
                                            bc_sb[:, 512 + cl * 256 + 128:
                                                  512 + cl * 256 + 256],
                                            rbf_sb[:, g * 256:(g + 1) * 256],
                                            start=False, stop=True,
                                            skip_group_check=True)
                        emit("tensor", ow, yomms, inc="pe")
                        yo_pe = now("pe")
                        stw = [("dve", xp_dve)]
                        if g >= 1:
                            stw.append(("dve", rupd4[g - 1]))
                        else:
                            stw.append(("dve", max(S["pb5_dve"],
                                                   ffn_dn_ev.get(5, 0),
                                                   gu_bank_dve.get(5, 0))))
                        def stmms(e, g=g, cl=cl):
                            e.matmul(pb[5][:, 0:256],
                                     btm_sb[:, (2 * cl) * 128:(2 * cl) * 128 + 128],
                                     arb[:, 10240 + ((g % 2) * 2 + 0) * 256:
                                         10240 + ((g % 2) * 2 + 0) * 256 + 256],
                                     start=True, stop=False, skip_group_check=True)
                            return e.matmul(pb[5][:, 0:256],
                                            btm_sb[:, (2 * cl + 1) * 128:
                                                   (2 * cl + 1) * 128 + 128],
                                            arb[:, 10240 + ((g % 2) * 2 + 1) * 256:
                                                10240 + ((g % 2) * 2 + 1) * 256 + 256],
                                            start=False, stop=True,
                                            skip_group_check=True)
                        emit("tensor", stw, stmms, inc="pe")
                        st_pe_ev[g] = now("pe")
                        t2 = arf[:, A_T2:A_T2 + 256]
                        t2_3 = t2.rearrange("p (h w) -> p h w", h=4)
                        gslot = g % 2
                        for q in range(2):
                            ecarep = (arf[:, A_CATM + q * 64 + 4 * g:
                                          A_CATM + q * 64 + 4 * g + 4]
                                      .unsqueeze(2).broadcast_to([128, 4, 64]))
                            emit("vector", [("pe", yo_pe)] if q == 0 else [],
                                 lambda e, yob=yob, q=q, t2_3=t2_3, ecarep=ecarep:
                                 e.tensor_tensor(
                                     t2_3,
                                     yob[:, q * 256:(q + 1) * 256]
                                     .rearrange("p (h w) -> p h w", h=4),
                                     ecarep, ALU.mult), inc="dve")
                            emit("vector", [],
                                 lambda e, gb=gb, q=q, t2_3=t2_3: e.tensor_tensor(
                                     t2_3, t2_3,
                                     gb[:].rearrange("p (hg c) -> p hg c", hg=4)
                                     [:, :, q * 64:(q + 1) * 64], ALU.add), inc="dve")
                            ydst = (ybuf[:]
                                    .rearrange("p (s pair qq hip w) -> p s pair qq hip w",
                                               s=2, pair=2, qq=2, hip=2)
                                    [:, gslot, :, q, :, :])
                            drep = (C["DB"][:, 4 * g:4 * g + 4]
                                    .rearrange("p (pair hip) -> p pair hip", pair=2)
                                    .unsqueeze(3).broadcast_to([128, 2, 2, 64]))
                            t2_4 = t2.rearrange("p (pair hip w) -> p pair hip w",
                                                pair=2, hip=2)
                            emit("vector", [],
                                 lambda e, ydst=ydst, g=g, cl=cl, q=q, drep=drep:
                                 e.tensor_tensor(
                                     ydst,
                                     xtm_sb[:, (2 * cl + q) * INTER + g * 256:
                                            (2 * cl + q) * INTER + g * 256 + 256]
                                     .rearrange("p (pair hip w) -> p pair hip w",
                                                pair=2, hip=2),
                                     drep, ALU.mult), inc="dve")
                            emit("vector", [],
                                 lambda e, ydst=ydst, t2_4=t2_4: e.tensor_tensor(
                                     ydst, ydst, t2_4, ALU.add), inc="dve")
                        t2_ev[g] = now("dve")
                        y4_ev[g] = now("dve")
                        etrep = (arf[:, A_CATM + 512 + 4 * g:A_CATM + 512 + 4 * g + 4]
                                 .unsqueeze(2).broadcast_to([128, 4, 64]))
                        rb4 = rbf_sb[:, g * 256:(g + 1) * 256].rearrange(
                            "p (h w) -> p h w", h=4)
                        emit("vector", [("pe", st_pe_ev[g])],
                             lambda e, rb4=rb4, etrep=etrep: e.tensor_tensor(
                                 rb4, rb4, etrep, ALU.mult), inc="dve")
                        emit("vector", [],
                             lambda e, rb4=rb4: e.tensor_tensor(
                                 rb4, rb4,
                                 pb[5][:, 0:256].rearrange("p (h w) -> p h w", h=4),
                                 ALU.add), inc="dve")
                        rupd4[g] = now("dve")
                        for pr in range(2):
                            fc = 2 * g + pr
                            def evf(e, slot, fc=fc, cl=cl):
                                gsl = gate_sb[:, fc * SPAN + cl * 256:
                                              fc * SPAN + cl * 256 + 128]
                                return e.scalar_tensor_tensor(
                                    gsl, slot, C["nw"][:, fc:fc + 1], gsl,
                                    ALU.mult, ALU.mult)
                            def evf1(e, slot, fc=fc, cl=cl):
                                gsl = gate_sb[:, fc * SPAN + cl * 256 + 128:
                                              fc * SPAN + cl * 256 + 256]
                                return e.scalar_tensor_tensor(
                                    gsl, slot, C["nw"][:, fc:fc + 1], gsl,
                                    ALU.mult, ALU.mult)
                            ytr_pend.append((
                                ybuf[:, gslot * 512 + pr * 256:
                                     gslot * 512 + pr * 256 + 128],
                                C["identb"][:], None, evf))
                            ytr_pend.append((
                                ybuf[:, gslot * 512 + pr * 256 + 128:
                                     gslot * 512 + pr * 256 + 256],
                                C["identb"][:], None, evf1))
                        if g % 2 == 1:
                            _, pd_ = tbb.batch(ytr_pend,
                                               extra=[("dve", y4_ev[g])])
                            ytr_pend = []
                            for fcq in range(2 * g - 2, 2 * g + 2):
                                gdrain[fcq] = pd_
                if hh < NH:
                    h = hh
                    r = h % 2
                    g4 = h // 4
                    pr_ = h // 2
                    cabbank = pb[4] if pr_ % 2 == 0 else pb7
                    if r == 0:
                        cw = [("ldcr%d" % (g4 % 2),
                               ((car_base + g4) // 2 + 1) * 16)]
                        if pr_ >= 2:
                            cw.append(("act", E_ev[2 * (pr_ - 2) + 1]))
                        elif pr_ == 0:
                            cw.append(("dve", gm_dve))
                        else:
                            cw.append(("dve", etb_dve))
                        emit("tensor", cw,
                             lambda e, cabbank=cabbank, g4=g4, h=h: e.matmul(
                                 cabbank[:, 0:512], C["onesr"][:],
                                 arf[0:1, A_CAR + (g4 % 2) * 1024 + (h % 4) * 256:
                                     A_CAR + (g4 % 2) * 1024 + (h % 4) * 256 + 512],
                                 start=True, stop=True, skip_group_check=True),
                             inc="pe")
                        cab_pe_ev[pr_] = now("pe")
                        if h % 4 == 2:
                            car_free_pe.append(now("pe"))
                            if g4 + 2 < 16:
                                gn = g4 + 2
                                cw2 = [("dve", ca_dve)]
                                if car_k >= 2:
                                    cw2.append(("pe", car_free_pe[car_k - 2]))
                                emit("gpsimd", cw2,
                                     lambda e, gn=gn, ca_fm=ca_fm: e.dma_start(
                                         out=arf[0:1, A_CAR + (gn % 2) * 1024:
                                                 A_CAR + (gn % 2) * 1024 + 1024],
                                         in_=ca_fm[gn * 4:(gn + 1) * 4, :]),
                                     inc="ldcr%d" % (car_k % 2), inc_by=16)
                                car_k += 1
                    cab_pe = cab_pe_ev[pr_]
                    eset = A_E + r * 384
                    absl = arf[:, A_ARG + r * 128:A_ARG + r * 128 + 128]
                    nca0 = arf[:, A_ARG + 256 + h:A_ARG + 256 + h + 1]
                    nca1 = arf[:, A_ARG + 320 + h:A_ARG + 320 + h + 1]
                    bcol0 = arf[:, A_CATM + 128 + h:A_CATM + 128 + h + 1]
                    lnd0 = arf[:, A_CATM + 384 + h:A_CATM + 384 + h + 1]
                    lnd1 = arf[:, A_CATM + 448 + h:A_CATM + 448 + h + 1]
                    ew = [("pe", cab_pe)]
                    if h >= 2:
                        ew.append(("pool", S_ev[h - 2]))
                    else:
                        ew.append(("dve", nca_dve))
                    cbase = (h % 2) * 256
                    emit("scalar", ew,
                         lambda e, cabbank=cabbank, eset=eset, bcol0=bcol0, cbase=cbase:
                         e.activation(
                             arf[:, eset + 128:eset + 256],
                             cabbank[:, cbase + 128:cbase + 256],
                             AF.Exp, bias=bcol0), inc="act")
                    emit("scalar", [],
                         lambda e, cabbank=cabbank, absl=absl, nca0=nca0, cbase=cbase:
                         e.activation(
                             absl, cabbank[:, cbase:cbase + 128], AF.Abs, bias=nca0),
                         inc="act")
                    emit("scalar", [],
                         lambda e, absl=absl, eset=eset, lnd0=lnd0: e.activation(
                             arf[:, eset:eset + 128], absl, AF.Exp, scale=-1.0,
                             bias=lnd0), inc="act")
                    emit("scalar", [],
                         lambda e, cabbank=cabbank, absl=absl, nca1=nca1, cbase=cbase:
                         e.activation(
                             absl, cabbank[:, cbase + 128:cbase + 256], AF.Abs,
                             bias=nca1), inc="act")
                    emit("scalar", [],
                         lambda e, absl=absl, eset=eset, lnd1=lnd1: e.activation(
                             arf[:, eset + 256:eset + 384], absl, AF.Exp, scale=-1.0,
                             bias=lnd1), inc="act")
                    E_ev[h] = now("act")
                    sw = [("act", E_ev[h])]
                    if h >= PD:
                        sw.append(("pe", ymm_ev[h - PD]))
                    if h == 0:
                        sw.append(("dve", gm_dve))
                    sap = S_ap(h % 4)
                    emit("gpsimd", sw,
                         lambda e, sap=sap, eset=eset: e.tensor_tensor(
                             sap[:, 0:128], arf[:, A_GM0:A_GM0 + 128],
                             arf[:, eset:eset + 128], ALU.mult), inc="pool")
                    emit("gpsimd", [],
                         lambda e, sap=sap, eset=eset: e.tensor_tensor(
                             sap[:, 128:256], arf[:, A_GM0 + 128:A_GM0 + 256],
                             arf[:, eset + 128:eset + 256], ALU.mult), inc="pool")
                    emit("gpsimd", [],
                         lambda e, sap=sap, eset=eset: e.tensor_tensor(
                             sap[:, 256:384], arf[:, A_GM1:A_GM1 + 128],
                             arf[:, eset + 256:eset + 384], ALU.mult), inc="pool")
                    S_ev[h] = now("pool")
            rms_pe = [0] * 32
            for fc in range(32):
                sqw = [("dve", gdrain[fc])]
                if fc >= 2:
                    sqw.append(("pe", rms_pe[fc - 2]))
                emit("scalar", sqw,
                     lambda e, fc=fc, cl=cl: e.activation(
                         arf[:, A_G2 + (fc % 2) * 256:A_G2 + (fc % 2) * 256 + 256],
                         gate_sb[:, fc * SPAN + cl * 256: fc * SPAN + (cl + 1) * 256],
                         AF.Square), inc="act")
                sqe = now("act")
                rmsw = [("act", sqe)]
                if fc == 0:
                    rmsw.append(("dve", rupd4[15]))
                emit("tensor", rmsw,
                     lambda e, fc=fc: e.matmul(
                         pb[5][0:1, 256:512], C["onescol"][:],
                         arf[:, A_G2 + (fc % 2) * 256:A_G2 + (fc % 2) * 256 + 256],
                         start=(fc == 0), stop=(fc == 31),
                         skip_group_check=True), inc="pe")
                rms_pe[fc] = now("pe")
            emit("vector", [("pe", now("pe"))],
                 lambda e: e.tensor_scalar(rstdrow_sb[0:1, :], pb[5][0:1, 256:512],
                                           1.0 / INTER, None, ALU.mult), inc="dve")
            S["pb5_dve"] = now("dve")
            emit("scalar", [("dve", now("dve"))],
                 lambda e: e.activation(rstdrow_sb[0:1, :], rstdrow_sb[0:1, :],
                                        AF.Ln, bias=C["cvals"][0:1, 0:1]), inc="act")
            emit("scalar", [],
                 lambda e: e.activation(rstdrow_sb[0:1, :], rstdrow_sb[0:1, :],
                                        AF.Exp, scale=-0.5), inc="act")
            emit("tensor", [("act", now("act")), ("act", E_ev[NH - 1]),
                            ("dve", pb7_last)],
                 lambda e: e.matmul(pb7[:, 0:256], C["onesr"][:], rstdrow_sb[0:1, :],
                                    start=True, stop=True, skip_group_check=True),
                 inc="pe")
            emit("vector", [("pe", now("pe"))],
                 lambda e, cl=cl: e.tensor_copy(
                     arf[:, A_RSTD + cl * 256:A_RSTD + (cl + 1) * 256], pb7[:, 0:256]),
                 inc="dve")
            pb7_last = now("dve")
            S["hb2"] = (0, y4_ev[14])
            S["hb3"] = (0, y4_ev[15])
            S["pb4_act"] = E_ev[NH - 2]
            S["yo0_dve"] = t2_ev[14]
            S["yo1_dve"] = t2_ev[15]
        S["ssm_pe"] = now("pe")
        rstd_dve = now("dve")
        S["arena_dve"] = rstd_dve
        if span == 0:
            dbg_tap("g", gate_sb[:])
            dbg_tap("rstd", arf[:, A_RSTD:A_RSTD + 512])

        tbb.set_banks(["pb6", "pb2b", "pb3b"],
                      new_guards={"pb2b": [("act", S["hb2"][0]), ("dve", S["hb2"][1])],
                                  "pb3b": [("act", S["hb3"][0]), ("dve", S["hb3"][1])]})
        op_bank_dve = {0: 0, 1: 0}
        otr_pend = []
        otr_fill = {}
        ytr_pb01 = [("dve", S["yo0_dve"]), ("dve", S["yo1_dve"])]
        for og in range(16):
            cw = []
            if span == 0:
                cw.append(("cvB%d" % (og % 2), (og // 2 + 1) * 16))
            if og < 2:
                cw.extend(r_wgu.freed[-2:])
            k = r_wout.load(lambda e, slot, og=og: e.dma_start(
                out=wsBb[:, slot * 4096: slot * 4096 + 4096],
                in_=d_w_out_b[og]), engine="sync", extra_waits=cw)
            bank = og % 2
            pw = [r_wout.use_wait(k), ("dve", rstd_dve), ("act", bank_evac[bank])]
            pw += [w for w in ytr_pb01 if w[1] > 0]
            if op_bank_dve[bank]:
                pw.append(("dve", op_bank_dve[bank]))
            obuf = (k % 2) * 4096
            def omms(e, obuf=obuf, bank=bank):
                last = None
                for kc in range(32):
                    last = e.matmul(
                        pb[bank][:],
                        wsBb[:, obuf + kc * 128: obuf + (kc + 1) * 128],
                        gate_sb[:, kc * SPAN: kc * SPAN + 512],
                        start=(kc == 0), stop=(kc == 31))
                return last
            emit("tensor", pw, omms, inc="pe")
            r_wout.free(("pe", now("pe")))
            o_pe = now("pe")
            while otr_pend:
                pcl, pog, pev2 = otr_pend.pop(0)
                fslp = arf[:, A_FFOUT + ((pog % 2) * 2 + pcl) * 256:
                           A_FFOUT + ((pog % 2) * 2 + pcl) * 256 + 256]
                tw2 = [("dve", pev2), ("dve", pb7_last)]
                def ofill(e, fslp=fslp):
                    e.transpose(pb7[:, 0:128], fslp[:, 0:128], C["identf"][:])
                    return e.transpose(pb7[:, 128:256], fslp[:, 128:256],
                                       C["identf"][:])
                emit("tensor", tw2, ofill, inc="pe")
                ofp = now("pe")
                def odrain(e, ofp=ofp, pcl=pcl, pog=pog):
                    e.wait_ge(sems["pe"], ofp)
                    last = None
                    for q in range(2):
                        tq = pcl * 2 + q
                        last = e.tensor_tensor(
                            out_sb[:, tq * H + pog * 128: tq * H + (pog + 1) * 128],
                            out_sb[:, tq * H + pog * 128: tq * H + (pog + 1) * 128],
                            pb7[:, q * 128:(q + 1) * 128], ALU.add)
                    return last
                emit("vector", [], odrain, inc="dve")
                pb7_last = now("dve")
                out_add_ev[pcl * 2] = pb7_last
                out_add_ev[pcl * 2 + 1] = pb7_last
            for cl in range(2):
                fsl = arf[:, A_FFOUT + ((og % 2) * 2 + cl) * 256:
                          A_FFOUT + ((og % 2) * 2 + cl) * 256 + 256]
                fw2 = [("pe", o_pe)]
                if og >= 2:
                    fw2.append(("pe", otr_fill.get((og - 2, cl), 0)))
                emit("vector", [w for w in fw2 if w[1] > 0],
                     lambda e, bank=bank, cl=cl, fsl=fsl: e.tensor_tensor(
                         fsl, pb[bank][:, cl * 256:(cl + 1) * 256],
                         arf[:, A_RSTD + cl * 256:A_RSTD + (cl + 1) * 256], ALU.mult),
                     inc="dve")
                emit("vector", [],
                     lambda e, og=og, fsl=fsl: e.tensor_scalar(
                         fsl, fsl, C["bout"][:, og:og + 1], None, ALU.add), inc="dve")
                otr_pend.append((cl, og, now("dve")))
            op_bank_dve[bank] = now("dve")
        while otr_pend:
            pcl, pog, pev2 = otr_pend.pop(0)
            fslp = arf[:, A_FFOUT + ((pog % 2) * 2 + pcl) * 256:
                       A_FFOUT + ((pog % 2) * 2 + pcl) * 256 + 256]
            tw2 = [("dve", pev2), ("dve", pb7_last)]
            def ofill(e, fslp=fslp):
                e.transpose(pb7[:, 0:128], fslp[:, 0:128], C["identf"][:])
                return e.transpose(pb7[:, 128:256], fslp[:, 128:256], C["identf"][:])
            emit("tensor", tw2, ofill, inc="pe")
            ofp = now("pe")
            def odrain(e, ofp=ofp, pcl=pcl, pog=pog):
                e.wait_ge(sems["pe"], ofp)
                last = None
                for q in range(2):
                    tq = pcl * 2 + q
                    last = e.tensor_tensor(
                        out_sb[:, tq * H + pog * 128: tq * H + (pog + 1) * 128],
                        out_sb[:, tq * H + pog * 128: tq * H + (pog + 1) * 128],
                        pb7[:, q * 128:(q + 1) * 128], ALU.add)
                return last
            emit("vector", [], odrain, inc="dve")
            pb7_last = now("dve")
            out_add_ev[pcl * 2] = pb7_last
            out_add_ev[pcl * 2 + 1] = pb7_last
        S["outproj_pe"] = now("pe")
        S["pb01_dve"] = max(op_bank_dve[0], op_bank_dve[1])
        if span == 0:
            dbg_tap("outsb", out_sb[:])

        for tq in range(4):
            emit("sync", [("dve", out_add_ev[tq])],
                 lambda e, tq=tq, row0=row0: e.dma_start(
                     out=dout[row0 + tq * 128: row0 + (tq + 1) * 128, :],
                     in_=out_sb[:, tq * H:(tq + 1) * H]), inc="st", inc_by=16)

    total_st = now("st")
    total_dbg = now("dbg")
    def fin(e):
        if total_dbg:
            e.wait_ge(sems["dbg"], total_dbg)
        e.wait_ge(sems["st"], total_st)
        return e.nop()
    emit("sync", [], fin)

    nops = {k: sum(1 for en, _ in ops if en == k)
            for k in ["tensor", "vector", "scalar", "gpsimd", "sync"]}
    print("op groups per engine:", nops)

    with nc.Block() as block:
        @block.tensor
        def _(e):
            for en, f in ops:
                if en == "tensor":
                    f(e)

        @block.vector
        def _(e):
            for en, f in ops:
                if en == "vector":
                    f(e)

        @block.scalar
        def _(e):
            for en, f in ops:
                if en == "scalar":
                    f(e)

        @block.gpsimd
        def _(e):
            for en, f in ops:
                if en == "gpsimd":
                    f(e)

        @block.sync
        def _(e):
            for en, f in ops:
                if en == "sync":
                    f(e)

    ctx.close()
    return nc


def kernel(**inputs):
    g = prep_host(inputs)
    x = _f32(inputs["L0_attn_hidden_states"])
    nc = build_program()
    in_maps = []
    for core in range(NCORES):
        m = dict(g)
        m["x"] = np.ascontiguousarray(
            x[core * SEQ_PER_CORE:(core + 1) * SEQ_PER_CORE].reshape(SEQ_PER_CORE * L, H))
        in_maps.append(m)
    res = run_bass_kernel_spmd(nc, in_maps, core_ids=list(range(NCORES)))
    out = np.stack([np.asarray(res.results[c]["out"]).reshape(SEQ_PER_CORE, L, H)
                    for c in range(NCORES)])
    return out.reshape(16, L, H)


if __name__ == "__main__":
    import reference as refmod
    ins = refmod.setup_inputs(0)
    o = kernel(**{k: np.asarray(v) for k, v in ins.items()})
    print(o.shape, o.dtype)


# version 08
# speedup vs baseline: 1.9312x; 1.0077x over previous
import numpy as np
import ml_dtypes
from contextlib import ExitStack

import concourse.bass as bass
import concourse.mybir as mybir
from concourse.bass_utils import run_bass_kernel_spmd

F32 = mybir.dt.float32
BF16 = mybir.dt.bfloat16
I8 = mybir.dt.int8
U8 = mybir.dt.uint8
AF = mybir.ActivationFunctionType
ALU = mybir.AluOpType
AXX = mybir.AxisListType.X

NCORES = 8
L = 2048
H = 2048
NH = 64
INTER = 4096
SEQ_PER_CORE = 2
SPAN = 512
NSPAN = SEQ_PER_CORE * L // SPAN
KC = H // 128
OCH = 68
CLAMP = 10.0

DBG = {}


def _f32(x):
    return np.ascontiguousarray(np.asarray(x, dtype=np.float32))


def prep_host(inputs):
    g = {}
    w_in = _f32(inputs["L0_attn_in_proj_w"])
    w_in_p = np.zeros((OCH * 128, H), np.float32)
    w_in_p[:8512] = w_in
    g["w_in"] = np.ascontiguousarray(
        w_in_p.T.reshape(KC, 128, 34, 256).transpose(2, 0, 1, 3))
    b_in = np.zeros((OCH * 128,), np.float32)
    b_in[:8512] = _f32(inputs["L0_attn_in_proj_b"])
    g["b_in"] = np.ascontiguousarray(b_in.reshape(OCH, 128).T)
    g["convw"] = np.ascontiguousarray(
        _f32(inputs["L0_attn_conv_w"]).reshape(34, 128, 4).transpose(1, 0, 2).reshape(128, 136))
    g["dtb"] = _f32(inputs["L0_attn_dt_bias"]).reshape(64, 1)
    g["alog"] = _f32(inputs["L0_attn_A_log"]).reshape(64, 1)
    g["d_row"] = _f32(inputs["L0_attn_D"]).reshape(1, 64)
    g["nw"] = np.ascontiguousarray(_f32(inputs["L0_attn_norm_w"]).reshape(32, 128).T)
    g["w_out"] = np.ascontiguousarray(
        _f32(inputs["L0_attn_out_proj_w"]).T.reshape(32, 128, 16, 128).transpose(2, 0, 1, 3))
    g["b_out"] = np.ascontiguousarray(_f32(inputs["L0_attn_out_proj_b"]).reshape(16, 128).T)
    g["w_g"] = np.ascontiguousarray(
        np.asarray(inputs["L0_ffn_w_gate"]).reshape(KC, 128, 32, 256).transpose(2, 0, 1, 3))
    g["w_u"] = np.ascontiguousarray(
        np.asarray(inputs["L0_ffn_w_up"]).reshape(KC, 128, 32, 256).transpose(2, 0, 1, 3))
    g["w_d"] = np.ascontiguousarray(
        np.asarray(inputs["L0_ffn_w_down"]).reshape(4, 16, 128, 8, 256).transpose(3, 0, 1, 2, 4))
    g["s_g"] = np.ascontiguousarray(_f32(inputs["L0_ffn_s_gate"]).reshape(64, 128).T)
    g["s_u"] = np.ascontiguousarray(_f32(inputs["L0_ffn_s_up"]).reshape(64, 128).T)
    g["s_d"] = np.ascontiguousarray(_f32(inputs["L0_ffn_s_down"]).reshape(16, 128).T)
    g["ident_b"] = np.eye(128, dtype=ml_dtypes.bfloat16)
    g["ident_f"] = np.eye(128, dtype=np.float32)
    g["onesr"] = np.ones((1, 128), np.float32)
    g["onescol"] = np.ones((128, 1), np.float32)
    s = np.arange(128)[:, None]
    t = np.arange(256)[None, :]
    g["mask0"] = (t >= s).astype(ml_dtypes.bfloat16)
    cv = np.zeros((128, 2), np.float32)
    cv[:, 0] = 1e-5
    cv[:, 1] = 1.0
    g["cvals"] = cv
    return g


def build_program():
    nc = bass.Bass("TRN2", target_bir_lowering=False, debug=False)

    dx = nc.dram_tensor("x", [SEQ_PER_CORE * L, H], F32, kind="ExternalInput").ap()
    dout = nc.dram_tensor("out", [SEQ_PER_CORE * L, H], F32, kind="ExternalOutput").ap()
    dr = {}
    for n, shape, dt in [
        ("w_in", [34, KC, 128, 256], F32), ("b_in", [128, OCH], F32),
        ("convw", [128, 136], F32), ("dtb", [64, 1], F32), ("alog", [64, 1], F32),
        ("d_row", [1, 64], F32), ("nw", [128, 32], F32),
        ("w_out", [16, 32, 128, 128], F32), ("b_out", [128, 16], F32),
        ("w_g", [32, KC, 128, 256], I8), ("w_u", [32, KC, 128, 256], I8),
        ("w_d", [8, 4, 16, 128, 256], I8),
        ("s_g", [128, 64], F32), ("s_u", [128, 64], F32), ("s_d", [128, 16], F32),
        ("ident_b", [128, 128], BF16), ("ident_f", [128, 128], F32),
        ("onesr", [1, 128], F32), ("onescol", [128, 1], F32),
        ("mask0", [128, 256], BF16), ("cvals", [128, 2], F32),
    ]:
        dr[n] = nc.dram_tensor(n, shape, dt, kind="ExternalInput").ap()
    d_w_in_b = nc.dram_tensor("w_in_b", [34, 128, 4096], BF16, kind="Internal").ap()
    d_w_out_b = nc.dram_tensor("w_out_b", [16, 128, 4096], BF16, kind="Internal").ap()
    ddbg = {n: nc.dram_tensor("dbg_" + n, list(s), F32, kind="ExternalOutput").ap()
            for n, s in DBG.items()}

    ctx = ExitStack()
    sb = lambda n, s, d: ctx.enter_context(nc.sbuf_tensor(n, s, d))
    psb = lambda n, s, d: ctx.enter_context(nc.psum_tensor(n, s, d))

    C = {}
    for n, shape, dt in [
        ("identb", [128, 128], BF16), ("identf", [128, 128], F32),
        ("onesr", [1, 128], F32), ("onescol", [128, 1], F32),
        ("mask0", [128, 256], BF16), ("b_in", [128, OCH], F32),
        ("convw", [128, 136], F32), ("dtb", [64, 1], F32), ("alog", [64, 1], F32),
        ("d_row", [1, 64], F32), ("nw", [128, 32], F32), ("bout", [128, 16], F32),
        ("sg", [128, 64], F32), ("su", [128, 64], F32), ("sd", [128, 16], F32),
        ("acol", [64, 1], F32), ("bias66", [64, 1], F32),
        ("DB", [128, 64], F32), ("cvals", [128, 2], F32),
    ]:
        C[n] = sb("c_" + n, shape, dt)

    xn_sb = sb("xn", [128, KC * SPAN], BF16)
    gate_sb = sb("gate", [128, 32 * SPAN], BF16)
    xtm_sb = sb("xtm", [128, 4 * INTER], BF16)
    out_sb = sb("outtm", [128, 4 * H], F32)
    rbf_sb = sb("rbf", [128, NH * 64], BF16)
    arena = sb("arena", [128, 32768], U8)
    arf = arena[:].bitcast(F32)
    arb = arena[:].bitcast(BF16)
    lnconv = sb("lnconv", [128, 6208], U8)
    lcf = lnconv[:].bitcast(F32)
    ssb = sb("ssb", [128, 2 * 384], BF16)
    ybuf = sb("ybuf", [128, 4 * 256], BF16)
    xpb = sb("xpb", [128, 4 * 64], BF16)
    bc_sb = sb("bcfm", [128, 2 * SPAN], BF16)
    btm_sb = sb("btm", [128, 4 * 128], BF16)
    dtraw_sb = sb("dtraw", [64, SPAN], F32)
    halo_sb = sb("halo", [128, 34 * 3], F32)
    ffp_sb = sb("ffp", [128, 1024], BF16)
    tsil_sb = sb("tsil", [128, 1024], BF16)
    lnsm = sb("lnsm", [128, 16], F32)
    wsA = sb("wsA", [128, 16384], U8)
    wsB = sb("wsB", [128, 16384], U8)
    wsAb = wsA[:].bitcast(BF16)
    wsBb = wsB[:].bitcast(BF16)

    A_GM0, A_GM1 = 0, 256
    A_ARG = 384
    A_E = 1152
    A_G2, A_RSTD = 1920, 2432
    A_DT = 2944
    A_CATM = 4224
    A_RSR = 4800
    A_XPB = 5120
    A_SSX = 5632
    A_T2 = A_ARG + 384
    A_FFOUT = A_E
    A_CAR = 6144

    rstdrow_sb = arf[:, A_RSR:A_RSR + 256]
    etrow_sb = arf[:, A_RSR + 256:A_RSR + 320]
    etcol_sb = arf[0:64, A_RSR + 320:A_RSR + 321]

    def fuse_t(i):
        return arb[:, i * 512:(i + 1) * 512]

    print("sbuf bytes remaining:", nc.sbuf_bytes_remaining)

    pb = [psb(f"pb{i}", [128, 512], F32) for i in range(6)]
    pb6 = psb("pb6", [128, 1024], BF16)
    pb7 = psb("pb7", [128, 512], F32)
    pbb = {"pb6": pb6[:],
           "pb0b": pb[0][:].bitcast(BF16), "pb1b": pb[1][:].bitcast(BF16),
           "pb2b": pb[2][:].bitcast(BF16), "pb3b": pb[3][:].bitcast(BF16)}

    sems, cnt = {}, {}
    for name in ["ldx", "ldc", "ldcr0", "ldcr1", "st", "dbg",
                 "win0", "win1", "wout0", "wout1", "wgu0", "wgu1", "wd0", "wd1",
                 "cvA0", "cvA1", "cvB0", "cvB1", "cvl0", "cvl1", "cvm0", "cvm1",
                 "pe", "dve", "act", "pool"]:
        sems[name] = ctx.enter_context(nc.semaphore(name))
        cnt[name] = 0

    ops = []
    SELF_CH = {"vector": "dve", "scalar": "act", "gpsimd": "pool"}

    def emit(en, waits, fn, inc=None, inc_by=1):
        if en in SELF_CH and cnt[SELF_CH[en]] > 0:
            waits = list(waits) + [(SELF_CH[en], cnt[SELF_CH[en]])]
        if cnt["dbg"] > 0:
            waits = list(waits) + [("dbg", cnt["dbg"])]
        for c, v in waits:
            assert v <= cnt[c], f"wait on future event {c}:{v} > {cnt[c]}"
        if inc is not None:
            cnt[inc] += inc_by
        w = [(sems[c], v) for c, v in waits if v > 0]
        s_inc = (sems[inc], inc_by) if inc else None
        def f(eng, w=w, s_inc=s_inc, fn=fn):
            for sh, v in w:
                eng.wait_ge(sh, v)
            instr = fn(eng)
            if s_inc is not None:
                instr.then_inc(s_inc[0], s_inc[1])
        ops.append((en, f))

    def now(chn):
        return cnt[chn]

    class Ring:
        def __init__(self, ld0, ld1, depth=2):
            self.ld = [ld0, ld1]
            self.depth = depth
            self.loadrec = []
            self.freed = []
        def load(self, fn, engine="gpsimd", extra_waits=(), sem_override=None):
            k = len(self.loadrec)
            waits = list(extra_waits)
            j = k - self.depth
            if j >= 0:
                assert j < len(self.freed), "block consumed before reuse"
                waits.append(self.freed[j])
            slot = k % self.depth
            sname = (sem_override or self.ld)[slot]
            emit(engine, waits, lambda e, slot=slot: fn(e, slot),
                 inc=sname, inc_by=16)
            self.loadrec.append((sname, cnt[sname]))
            return k
        def use_wait(self, k):
            return self.loadrec[k]
        def free(self, ev):
            self.freed.append(ev)

    r_win = Ring("win0", "win1")
    r_wout = Ring("wout0", "wout1")
    r_wgu = Ring("wgu0", "wgu1")
    r_wd = Ring("wd0", "wd1")

    class TB:
        def __init__(self):
            self.state = {}
            self.cur = []
            self.i = 0
        def set_banks(self, keys, new_guards=None):
            for k in keys:
                if k not in self.state:
                    self.state[k] = {"last": 0, "guards": []}
                if new_guards and k in new_guards:
                    self.state[k]["guards"].extend(new_guards[k])
            self.cur = list(keys)
            self.i = 0
        def drain_of(self, key):
            return self.state[key]["last"] if key in self.state else 0
        def batch(self, entries, extra=()):
            key = self.cur[self.i]
            self.i = (self.i + 1) % len(self.cur)
            st = self.state[key]
            bank = pbb[key]
            assert len(entries) <= 8
            waits = list(extra) + st["guards"]
            st["guards"] = []
            if st["last"]:
                waits.append(("dve", st["last"]))
            def fill(e, entries=entries, bank=bank):
                last = None
                for j, (inap, idap, outap, evf) in enumerate(entries):
                    last = e.transpose(bank[:, j * 128:(j + 1) * 128], inap, idap)
                return last
            emit("tensor", waits, fill, inc="pe")
            fill_pe = now("pe")
            def drain(e, entries=entries, bank=bank, fill_pe=fill_pe):
                e.wait_ge(sems["pe"], fill_pe)
                last = None
                for j, (inap, idap, outap, evf) in enumerate(entries):
                    slot = bank[:, j * 128:(j + 1) * 128]
                    if evf is None:
                        last = e.tensor_copy(outap, slot)
                    else:
                        last = evf(e, slot)
                return last
            emit("vector", [], drain, inc="dve")
            st["last"] = now("dve")
            return fill_pe, st["last"]

    tbb = TB()

    def dbg_tap(name, src_ap):
        if name in DBG:
            emit("gpsimd", [("pe", now("pe")), ("dve", now("dve")), ("act", now("act"))],
                 lambda e, s=src_ap, d=ddbg[name]: e.dma_start(out=d, in_=s),
                 inc="dbg", inc_by=16)

    const_pairs = [
        (C["identb"], dr["ident_b"]), (C["identf"], dr["ident_f"]),
        (C["onesr"], dr["onesr"]), (C["onescol"], dr["onescol"]),
        (C["mask0"], dr["mask0"]), (C["b_in"], dr["b_in"]),
        (C["convw"], dr["convw"]), (C["dtb"], dr["dtb"]), (C["alog"], dr["alog"]),
        (C["d_row"], dr["d_row"]), (C["nw"], dr["nw"]), (C["bout"], dr["b_out"]),
        (C["sg"], dr["s_g"]), (C["su"], dr["s_u"]), (C["sd"], dr["s_d"]),
        (C["cvals"], dr["cvals"]),
    ]
    for dst, src in const_pairs:
        emit("sync", [], lambda e, d=dst, s=src: e.dma_start(out=d[:], in_=s),
             inc="ldc", inc_by=16)
    NCONST = now("ldc")
    emit("vector", [("ldc", NCONST)], lambda e: e.memset(rbf_sb[:], 0.0), inc="dve")
    emit("vector", [], lambda e: e.memset(halo_sb[:], 0.0), inc="dve")
    emit("scalar", [("ldc", NCONST)],
         lambda e: e.activation(C["acol"][:], C["alog"][:], AF.Exp), inc="act")
    emit("vector", [("act", now("act"))],
         lambda e: e.tensor_scalar(C["acol"][:], C["acol"][:], -1.0, None, ALU.mult),
         inc="dve")
    emit("vector", [("ldc", NCONST)],
         lambda e: e.tensor_tensor(C["bias66"][:], C["b_in"][0:64, 66:67], C["dtb"][:],
                                   ALU.add), inc="dve")
    emit("tensor", [("ldc", NCONST)],
         lambda e: e.matmul(pb7[:, 0:64], C["onesr"][:], C["d_row"][:],
                            start=True, stop=True), inc="pe")
    emit("vector", [("pe", now("pe"))],
         lambda e: e.tensor_copy(C["DB"][:], pb7[:, 0:64]), inc="dve")
    pb7_last = now("dve")
    INIT_DVE = now("dve")
    INIT_ACT = now("act")

    for tq in range(4):
        emit("sync", [],
             lambda e, tq=tq: e.dma_start(
                 out=out_sb[:, tq * H:(tq + 1) * H],
                 in_=dx[tq * 128:(tq + 1) * 128, :]),
             inc="ldx", inc_by=16)

    for og in range(34):
        k = r_win.load(lambda e, slot, og=og: e.dma_start(
            out=wsAb[:, slot * 4096: slot * 4096 + 4096],
            in_=dr["w_in"][og].rearrange("kc p o -> p kc o")),
            sem_override=("cvl0", "cvl1"))
        sname = "cvA%d" % (k % 2)
        emit("sync", [r_win.use_wait(k)],
             lambda e, og=og, k=k: e.dma_start(
                 out=d_w_in_b[og], in_=wsAb[:, (k % 2) * 4096: (k % 2) * 4096 + 4096]),
             inc=sname, inc_by=16)
        r_win.free((sname, cnt[sname]))
    for og in range(16):
        k = r_wout.load(lambda e, slot, og=og: e.dma_start(
            out=wsBb[:, slot * 4096: slot * 4096 + 4096],
            in_=dr["w_out"][og].rearrange("kc p o -> p kc o")),
            sem_override=("cvm0", "cvm1"))
        sname = "cvB%d" % (k % 2)
        emit("sync", [r_wout.use_wait(k)],
             lambda e, og=og, k=k: e.dma_start(
                 out=d_w_out_b[og], in_=wsBb[:, (k % 2) * 4096: (k % 2) * 4096 + 4096]),
             inc=sname, inc_by=16)
        r_wout.free((sname, cnt[sname]))

    rupd4 = [INIT_DVE] * 16
    car_k = 0
    car_free_pe = []
    S = {"outproj_pe": 0, "ssm_pe": 0, "ffn_pe": 0, "arena_dve": 0,
         "pb01_dve": 0, "pb01_act": 0, "pb4_act": 0,
         "hb2": (0, 0), "hb3": (0, 0), "pb4_dve": 0, "pb5_dve": 0,
         "yo0_dve": 0, "yo1_dve": 0}

    for span in range(NSPAN):
        row0 = (span // (L // SPAN)) * L + (span % (L // SPAN)) * SPAN
        seq_start = (span % (L // SPAN)) == 0

        st_before = now("st")
        if span > 0:
            for tq in range(4):
                emit("sync", [("st", st_before)],
                     lambda e, tq=tq, row0=row0: e.dma_start(
                         out=out_sb[:, tq * H:(tq + 1) * H],
                         in_=dx[row0 + tq * 128: row0 + (tq + 1) * 128, :]),
                     inc="ldx", inc_by=16)
        ldx_now = now("ldx")

        tbb.set_banks(["pb6", "pb2b", "pb3b"],
                      new_guards=({"pb2b": [("act", S["hb2"][0]), ("dve", S["hb2"][1])],
                                   "pb3b": [("act", S["hb3"][0]), ("dve", S["hb3"][1])]}
                                  if span else None))
        xn_guard = [("pe", S["ffn_pe"])] if span else []
        xn_act_ev = []
        xn_fill_pe = []
        for tq in range(4):
            xt = out_sb[:, tq * H:(tq + 1) * H]
            cb = (tq % 2) * 8
            w0 = [("ldx", ldx_now), ("dve", INIT_DVE)]
            if tq >= 2:
                w0.append(("act", xn_act_ev[tq - 2]))
            emit("vector", w0,
                 lambda e, xt=xt, cb=cb: e.reduce_sum(lnsm[:, cb:cb + 1], xt, axis=AXX),
                 inc="dve")
            sq_w = [("ldx", ldx_now), ("act", INIT_ACT)]
            if span:
                sq_w.append(("pe", S["ssm_pe"]))
                sq_w.append(("dve", S["arena_dve"]))
            emit("scalar", sq_w,
                 lambda e, xt=xt, cb=cb: e.activation(arf[:, 0:2048], xt, AF.Square,
                                                      accum_out=lnsm[:, cb + 1:cb + 2]),
                 inc="act")
            emit("vector", [("act", now("act"))],
                 lambda e, cb=cb: e.tensor_scalar(lnsm[:, cb + 2:cb + 3],
                                                  lnsm[:, cb:cb + 1], 1.0 / H, None,
                                                  ALU.mult), inc="dve")
            emit("vector", [],
                 lambda e, cb=cb: e.tensor_tensor(lnsm[:, cb + 3:cb + 4],
                                                  lnsm[:, cb + 2:cb + 3],
                                                  lnsm[:, cb + 2:cb + 3], ALU.mult),
                 inc="dve")
            emit("vector", [],
                 lambda e, cb=cb: e.scalar_tensor_tensor(
                     lnsm[:, cb + 4:cb + 5], lnsm[:, cb + 1:cb + 2], 1.0 / H,
                     lnsm[:, cb + 3:cb + 4], ALU.mult, ALU.subtract), inc="dve")
            emit("scalar", [("dve", now("dve"))],
                 lambda e, cb=cb: e.activation(lnsm[:, cb + 5:cb + 6],
                                               lnsm[:, cb + 4:cb + 5], AF.Ln,
                                               bias=C["cvals"][:, 0:1]), inc="act")
            emit("scalar", [],
                 lambda e, cb=cb: e.activation(lnsm[:, cb + 5:cb + 6],
                                               lnsm[:, cb + 5:cb + 6], AF.Exp,
                                               scale=-0.5), inc="act")
            emit("vector", [("act", now("act"))],
                 lambda e, cb=cb: e.tensor_tensor(lnsm[:, cb + 6:cb + 7],
                                                  lnsm[:, cb + 2:cb + 3],
                                                  lnsm[:, cb + 5:cb + 6], ALU.mult),
                 inc="dve")
            emit("vector", [],
                 lambda e, cb=cb: e.tensor_scalar(lnsm[:, cb + 6:cb + 7],
                                                  lnsm[:, cb + 6:cb + 7], -1.0, None,
                                                  ALU.mult), inc="dve")
            slot = tq % 2
            xnw = [("dve", now("dve"))]
            if tq >= 2:
                xnw.append(("pe", xn_fill_pe[tq - 2]))
            emit("scalar", xnw,
                 lambda e, xt=xt, slot=slot, cb=cb: e.activation(
                     arb[:, 4096 + slot * 2048: 4096 + (slot + 1) * 2048], xt,
                     AF.Identity, bias=lnsm[:, cb + 6:cb + 7],
                     scale=lnsm[:, cb + 5:cb + 6]), inc="act")
            xn_act_ev.append(now("act"))
            xa = now("act")
            fp_last = 0
            for half in range(2):
                entries = []
                for j in range(8):
                    kc = half * 8 + j
                    entries.append((
                        arb[:, 4096 + slot * 2048 + kc * 128:
                            4096 + slot * 2048 + (kc + 1) * 128],
                        C["identb"][:],
                        xn_sb[:, kc * SPAN + tq * 128: kc * SPAN + (tq + 1) * 128],
                        None))
                fp, _ = tbb.batch(entries, extra=[("act", xa)] + xn_guard)
                fp_last = fp
            xn_fill_pe.append(fp_last)
        ln_done_dve = now("dve")
        if span == 0:
            dbg_tap("xn", xn_sb[:])

        gate_guard = [("pe", S["outproj_pe"])] if span else []
        conv_dve_ev = [0] * 34
        silu_act_ev = [0] * 34
        xtr_fill_pe = [0] * 34
        xtr_pend = []
        bank_evac = {0: 0, 1: 0}
        pb01_guard = ([("dve", S["pb01_dve"])] if span else [])
        dtraw_act = 0
        for og in range(34):
            cw = []
            if span == 0:
                cw.append(("cvA%d" % (og % 2), (og // 2 + 1) * 16))
            if og < 2:
                cw.extend(r_wd.freed[-2:])
            k = r_win.load(lambda e, slot, og=og: e.dma_start(
                out=wsAb[:, slot * 4096: slot * 4096 + 4096],
                in_=d_w_in_b[og]), engine="sync", extra_waits=cw)
            for oc in range(2):
                o = og * 2 + oc
                if o == 67:
                    continue
                bank = o % 2
                pw = [r_win.use_wait(k), ("dve", ln_done_dve)]
                if bank_evac[bank]:
                    pw.append(("act", bank_evac[bank]))
                else:
                    pw += pb01_guard
                bufbase = (k % 2) * 4096
                def mms(e, bufbase=bufbase, oc=oc, bank=bank):
                    last = None
                    for kc in range(KC):
                        last = e.matmul(
                            pb[bank][:],
                            wsAb[:, bufbase + kc * 256 + oc * 128:
                                 bufbase + kc * 256 + oc * 128 + 128],
                            xn_sb[:, kc * SPAN: kc * SPAN + 512],
                            start=(kc == 0), stop=(kc == KC - 1))
                    return last
                emit("tensor", pw, mms, inc="pe")
                if oc == 1 or o == 66:
                    r_win.free(("pe", now("pe")))
                mmw = now("pe")
                if o < 32:
                    fc = o
                    emit("scalar", [("pe", mmw)] + gate_guard,
                         lambda e, fc=fc, bank=bank: e.activation(
                             gate_sb[:, fc * SPAN:(fc + 1) * SPAN], pb[bank][:],
                             AF.Silu, bias=C["b_in"][:, fc:fc + 1]), inc="act")
                elif o < 66:
                    fc = o - 32
                    ring = fc % 3
                    hbct = lcf[:, ring * 516:(ring + 1) * 516]
                    gw = [("pe", mmw)]
                    if fc >= 3:
                        gw.append(("dve", conv_dve_ev[fc - 3]))
                    emit("scalar", gw,
                         lambda e, hbct=hbct, bank=bank, o=o: e.activation(
                             hbct[:, 3:515], pb[bank][:], AF.Identity,
                             bias=C["b_in"][:, o:o + 1]), inc="act")
                    hev = now("act")
                    def halo_fill(e, hbct=hbct, fc=fc, seq_start=seq_start):
                        if seq_start:
                            return e.memset(hbct[:, 0:3], 0.0)
                        return e.tensor_copy(hbct[:, 0:3], halo_sb[:, fc * 3:fc * 3 + 3])
                    emit("vector", [("act", hev)], halo_fill, inc="dve")
                    acc = arf[:, 4096 + (fc % 2) * 512: 4096 + (fc % 2) * 512 + 512]
                    accw = [("act", silu_act_ev[fc - 2])] if fc >= 2 else []
                    emit("vector", accw,
                         lambda e, hbct=hbct, acc=acc, fc=fc: e.tensor_scalar(
                             acc, hbct[:, 0:512], C["convw"][:, fc * 4:fc * 4 + 1],
                             None, ALU.mult), inc="dve")
                    for j in range(1, 4):
                        emit("vector", [],
                             lambda e, hbct=hbct, acc=acc, fc=fc, j=j: e.scalar_tensor_tensor(
                                 acc, hbct[:, j:512 + j],
                                 C["convw"][:, fc * 4 + j:fc * 4 + j + 1],
                                 acc, ALU.mult, ALU.add), inc="dve")
                    emit("vector", [],
                         lambda e, hbct=hbct, fc=fc: e.tensor_copy(
                             halo_sb[:, fc * 3:fc * 3 + 3], hbct[:, 512:515]), inc="dve")
                    conv_dve_ev[fc] = now("dve")
                    if fc < 32:
                        dst = arb[:, 10240 + (fc % 4) * 512: 10240 + (fc % 4) * 512 + 512]
                    elif fc == 32:
                        dst = bc_sb[:, 0:512]
                    else:
                        dst = bc_sb[:, 512:1024]
                    sw = [("dve", now("dve"))]
                    if 4 <= fc and fc - 4 < 32:
                        sw.append(("pe", xtr_fill_pe[fc - 4]))
                    emit("scalar", sw,
                         lambda e, acc=acc, dst=dst: e.activation(dst, acc, AF.Silu),
                         inc="act")
                    silu_act_ev[fc] = now("act")
                    if fc < 32:
                        entries = []
                        for tt in range(4):
                            entries.append((
                                arb[:, 10240 + (fc % 4) * 512 + tt * 128:
                                    10240 + (fc % 4) * 512 + (tt + 1) * 128],
                                C["identb"][:],
                                xtm_sb[:, tt * INTER + fc * 128: tt * INTER + (fc + 1) * 128],
                                None))
                        xg = [("act", silu_act_ev[fc])]
                        if span:
                            xg.append(("pe", S["ssm_pe"]))
                        xtr_pend.append((fc, entries, xg))
                        if len(xtr_pend) > 2:
                            pfc, pent, pxg = xtr_pend.pop(0)
                            fp, _ = tbb.batch(pent, extra=pxg)
                            xtr_fill_pe[pfc] = fp
                    elif fc == 32:
                        while xtr_pend:
                            pfc, pent, pxg = xtr_pend.pop(0)
                            fp, _ = tbb.batch(pent, extra=pxg)
                            xtr_fill_pe[pfc] = fp
                        entries = []
                        for tt in range(4):
                            entries.append((
                                bc_sb[:, tt * 128:(tt + 1) * 128], C["identb"][:],
                                btm_sb[:, tt * 128:(tt + 1) * 128], None))
                        bg2 = [("act", silu_act_ev[32])]
                        if span:
                            bg2.append(("pe", S["ssm_pe"]))
                        tbb.batch(entries, extra=bg2)
                else:
                    emit("scalar", [("pe", mmw)],
                         lambda e, bank=bank: e.activation(
                             dtraw_sb[:, :], pb[bank][0:64, :], AF.Identity,
                             bias=C["bias66"][:]), inc="act")
                    dtraw_act = now("act")
                bank_evac[bank] = now("act")
        if span == 0:
            dbg_tap("gate", gate_sb[:])
            dbg_tap("xtm", xtm_sb[:])
            dbg_tap("bcfm", bc_sb[:])
            dbg_tap("dtraw", dtraw_sb[:])

        tbb.set_banks(["pb6", "pb0b", "pb1b"],
                      new_guards={"pb0b": [("act", bank_evac[0]),
                                           ("dve", S["yo0_dve"])] + pb01_guard,
                                  "pb1b": [("act", bank_evac[1]),
                                           ("dve", S["yo1_dve"])] + pb01_guard})
        gu_bank_guard = {2: [("dve", tbb.drain_of("pb2b")), ("act", S["hb2"][0]),
                             ("dve", S["hb2"][1])],
                         3: [("dve", tbb.drain_of("pb3b")), ("act", S["hb3"][0]),
                             ("dve", S["hb3"][1])],
                         4: [("dve", S["pb4_dve"]), ("act", S["pb4_act"])],
                         5: [("dve", S["pb5_dve"])]}
        gu_bank_act = {2: 0, 3: 0, 4: 0, 5: 0}
        gu_bank_dve = {2: 0, 3: 0, 4: 0, 5: 0}
        out_add_ev = [0, 0, 0, 0]
        outadd_guard_act = xn_act_ev[3]
        tsil_ev = [0, 0]
        dn_ev = {2: 0, 3: 0, 4: 0, 5: 0}
        ffn_tr_pend = []
        for ih in range(2):
            for ig in range(16):
                igg = ih * 16 + ig
                xgw = r_wout.freed[-2:] if (ih == 0 and ig == 0) else ()
                kg = r_wgu.load(lambda e, slot, igg=igg: e.dma_start(
                    out=wsBb[:, slot * 4096: slot * 4096 + 4096],
                    in_=dr["w_g"][igg].rearrange("kc p o -> p kc o")),
                    extra_waits=xgw)
                gbuf = (kg % 2) * 4096
                for isub in range(2):
                    bg_ = 2 + isub
                    gw = [r_wgu.use_wait(kg)] + gu_bank_guard[bg_]
                    gu_bank_guard[bg_] = []
                    if gu_bank_act[bg_]:
                        gw.append(("act", gu_bank_act[bg_]))
                    if dn_ev[bg_]:
                        gw.append(("dve", dn_ev[bg_]))
                    def gmms(e, gbuf=gbuf, isub=isub, bg_=bg_):
                        last = None
                        for kc in range(KC):
                            last = e.matmul(
                                pb[bg_][:],
                                wsBb[:, gbuf + kc * 256 + isub * 128:
                                     gbuf + kc * 256 + isub * 128 + 128],
                                xn_sb[:, kc * SPAN: kc * SPAN + 512],
                                start=(kc == 0), stop=(kc == KC - 1))
                        return last
                    emit("tensor", gw, gmms, inc="pe")
                r_wgu.free(("pe", now("pe")))
                g_pe = now("pe")
                ku = r_wgu.load(lambda e, slot, igg=igg: e.dma_start(
                    out=wsBb[:, slot * 4096: slot * 4096 + 4096],
                    in_=dr["w_u"][igg].rearrange("kc p o -> p kc o")))
                ubuf = (ku % 2) * 4096
                for isub in range(2):
                    bu_ = 4 + isub
                    uw = [r_wgu.use_wait(ku)] + gu_bank_guard[bu_]
                    gu_bank_guard[bu_] = []
                    if gu_bank_dve[bu_]:
                        uw.append(("dve", gu_bank_dve[bu_]))
                    if dn_ev[bu_]:
                        uw.append(("dve", dn_ev[bu_]))
                    def umms(e, ubuf=ubuf, isub=isub, bu_=bu_):
                        last = None
                        for kc in range(KC):
                            last = e.matmul(
                                pb[bu_][:],
                                wsBb[:, ubuf + kc * 256 + isub * 128:
                                     ubuf + kc * 256 + isub * 128 + 128],
                                xn_sb[:, kc * SPAN: kc * SPAN + 512],
                                start=(kc == 0), stop=(kc == KC - 1))
                        return last
                    emit("tensor", uw, umms, inc="pe")
                r_wgu.free(("pe", now("pe")))
                u_pe = now("pe")
                for isub in range(2):
                    i_g = igg * 2 + isub
                    il = ig * 2 + isub
                    tw = [("pe", g_pe)]
                    if tsil_ev[isub]:
                        tw.append(("dve", tsil_ev[isub]))
                    emit("scalar", tw,
                         lambda e, isub=isub, i_g=i_g: e.activation(
                             tsil_sb[:, isub * 512:(isub + 1) * 512],
                             pb[2 + isub][:], AF.Silu,
                             scale=C["sg"][:, i_g:i_g + 1]), inc="act")
                    emit("vector", [("act", now("act")), ("pe", u_pe)],
                         lambda e, isub=isub, i_g=i_g, il=il: e.scalar_tensor_tensor(
                             fuse_t(il), pb[4 + isub][:], C["su"][:, i_g:i_g + 1],
                             tsil_sb[:, isub * 512:(isub + 1) * 512],
                             ALU.mult, ALU.mult), inc="dve")
                    tsil_ev[isub] = now("dve")
                    gu_bank_act[2 + isub] = now("act")
                    gu_bank_dve[4 + isub] = now("dve")
            last_fuse_dve = now("dve")
            if span == 0 and ih == 0:
                dbg_tap("fuse", arb[:, 0:16384])
            for fop in range(8):
                bA, bB = (2, 3) if fop % 2 == 0 else (4, 5)
                fw = [("dve", last_fuse_dve)]
                for b_ in (bA, bB):
                    if dn_ev[b_]:
                        fw.append(("dve", dn_ev[b_]))
                    else:
                        if gu_bank_act[b_]:
                            fw.append(("act", gu_bank_act[b_]))
                        if gu_bank_dve[b_]:
                            fw.append(("dve", gu_bank_dve[b_]))
                first = True
                for ihq2 in range(2):
                    ihq = ih * 2 + ihq2
                    xdw = (r_win.freed[-2:]
                           if (ih == 0 and fop == 0 and ihq2 == 0) else ())
                    kd = r_wd.load(lambda e, slot, fop=fop, ihq=ihq: e.dma_start(
                        out=wsAb[:, slot * 4096: slot * 4096 + 4096],
                        in_=dr["w_d"][fop, ihq].rearrange("ic p o -> p ic o")),
                        extra_waits=xdw)
                    dbuf = (kd % 2) * 4096
                    w = (list(fw) if first else []) + [r_wd.use_wait(kd)]
                    def dmms(e, dbuf=dbuf, bA=bA, bB=bB, first=first, ihq2=ihq2):
                        last = None
                        for fh, bk in ((0, bA), (1, bB)):
                            for ic in range(16):
                                last = e.matmul(
                                    pb[bk][:],
                                    wsAb[:, dbuf + ic * 256 + fh * 128:
                                         dbuf + ic * 256 + fh * 128 + 128],
                                    fuse_t(ihq2 * 16 + ic),
                                    start=(first and ic == 0),
                                    stop=(ihq2 == 1 and ic == 15))
                        return last
                    emit("tensor", w, dmms, inc="pe")
                    r_wd.free(("pe", now("pe")))
                    first = False
                d_pe = now("pe")
                while ffn_tr_pend:
                    pent, pex = ffn_tr_pend.pop(0)
                    _, dd = tbb.batch(pent, extra=pex)
                    for tq in range(4):
                        out_add_ev[tq] = dd
                for fh, bk in ((0, bA), (1, bB)):
                    fo = fop * 2 + fh
                    pslot = fo % 2
                    emit("vector", [("pe", d_pe)],
                         lambda e, bk=bk, fo=fo, pslot=pslot: e.tensor_scalar(
                             ffp_sb[:, pslot * 512:(pslot + 1) * 512],
                             pb[bk][:], C["sd"][:, fo:fo + 1], None, ALU.mult),
                         inc="dve")
                    pev = now("dve")
                    dn_ev[bk] = pev
                    entries = []
                    for tq in range(4):
                        def evf(e, slot, tq=tq, fo=fo):
                            return e.tensor_tensor(
                                out_sb[:, tq * H + fo * 128: tq * H + (fo + 1) * 128],
                                out_sb[:, tq * H + fo * 128: tq * H + (fo + 1) * 128],
                                slot, ALU.add)
                        entries.append((
                            ffp_sb[:, pslot * 512 + tq * 128: pslot * 512 + (tq + 1) * 128],
                            C["identb"][:], None, evf))
                    ffn_tr_pend.append((entries,
                                        [("dve", pev), ("act", outadd_guard_act),
                                         ("ldx", ldx_now)]))
        while ffn_tr_pend:
            pent, pex = ffn_tr_pend.pop(0)
            _, dd = tbb.batch(pent, extra=pex)
            for tq in range(4):
                out_add_ev[tq] = dd
        S["ffn_pe"] = now("pe")
        ffn_dn_ev = dict(dn_ev)

        tbb.set_banks(["pb6"])
        for cl in range(2):
            DT = A_DT
            dt_fm = arf[0:64, DT:DT + 256]
            ca_fm = arf[0:64, DT + 256:DT + 512]
            lndt = arf[0:64, DT + 512:DT + 768]
            scr1 = arf[0:64, DT + 768:DT + 1024]
            scr2 = arf[0:64, DT + 1024:DT + 1280]
            dts = dtraw_sb[:, cl * 256:(cl + 1) * 256]
            dtg = [("act", dtraw_act), ("pe", now("pe"))]
            emit("scalar", dtg,
                 lambda e, dts=dts, scr1=scr1: e.activation(scr1, dts, AF.Abs),
                 inc="act")
            emit("scalar", [],
                 lambda e, scr1=scr1: e.activation(scr1, scr1, AF.Exp, scale=-1.0),
                 inc="act")
            emit("scalar", [],
                 lambda e, scr1=scr1: e.activation(scr1, scr1, AF.Ln,
                                                   bias=C["cvals"][0:64, 1:2]),
                 inc="act")
            emit("vector", [("act", now("act"))],
                 lambda e, dts=dts, scr2=scr2: e.tensor_scalar(
                     scr2, dts, 0.0, None, ALU.max), inc="dve")
            emit("vector", [],
                 lambda e, dt_fm=dt_fm, scr1=scr1, scr2=scr2: e.tensor_tensor(
                     dt_fm, scr1, scr2, ALU.add), inc="dve")
            emit("scalar", [("dve", now("dve"))],
                 lambda e, lndt=lndt, dt_fm=dt_fm: e.activation(lndt, dt_fm, AF.Ln),
                 inc="act")
            emit("vector", [],
                 lambda e, scr1=scr1, dt_fm=dt_fm: e.tensor_scalar(
                     scr1, dt_fm, C["acol"][:], None, ALU.mult), inc="dve")
            emit("vector", [],
                 lambda e, ca_fm=ca_fm, scr1=scr1: e.tensor_tensor_scan(
                     ca_fm, scr1, scr1, 0.0, ALU.add, ALU.bypass), inc="dve")
            ca_dve = now("dve")
            emit("vector", [("act", now("act"))],
                 lambda e, scr2=scr2, lndt=lndt, ca_fm=ca_fm: e.tensor_tensor(
                     scr2, lndt, ca_fm, ALU.subtract), inc="dve")
            emit("scalar", [("dve", now("dve"))],
                 lambda e, scr1=scr1, ca_fm=ca_fm: e.activation(
                     scr1, ca_fm, AF.Exp, scale=-1.0, bias=ca_fm[:, 255:256]),
                 inc="act")
            emit("vector", [("act", now("act"))],
                 lambda e, scr1=scr1, dt_fm=dt_fm: e.tensor_tensor(
                     scr1, scr1, dt_fm, ALU.mult), inc="dve")
            emit("scalar", [],
                 lambda e, ca_fm=ca_fm: e.activation(etcol_sb[:], ca_fm[:, 255:256],
                                                     AF.Exp), inc="act")
            emit("scalar", [("dve", now("dve"))],
                 lambda e, dt_fm=dt_fm, ca_fm=ca_fm: e.activation(dt_fm, ca_fm, AF.Exp),
                 inc="act")
            sm_dve = now("dve")
            sm_act = now("act")
            tw = [("dve", sm_dve), ("act", sm_act)]
            if pb7_last:
                tw.append(("dve", pb7_last))
            def dtfill(e, dt_fm=dt_fm, scr1=scr1, scr2=scr2, lndt=lndt):
                for q in range(2):
                    e.transpose(pb7[:, q * 64:(q + 1) * 64],
                                dt_fm[:, q * 128:(q + 1) * 128], C["identf"][0:64, 0:64])
                for q in range(2):
                    e.transpose(pb7[:, 128 + q * 64:128 + (q + 1) * 64],
                                scr2[:, q * 128:(q + 1) * 128], C["identf"][0:64, 0:64])
                for q in range(2):
                    e.transpose(pb7[:, 256 + q * 64:256 + (q + 1) * 64],
                                scr1[:, q * 128:(q + 1) * 128], C["identf"][0:64, 0:64])
                e.transpose(pb7[:, 384 + 0 * 64:384 + 64],
                            lndt[:, 0:128], C["identf"][0:64, 0:64])
                return e.transpose(pb7[:, 448:512],
                                   lndt[:, 128:256], C["identf"][0:64, 0:64])
            emit("tensor", tw, dtfill, inc="pe")
            dtf_pe = now("pe")
            def dtdrain(e, dtf_pe=dtf_pe):
                e.wait_ge(sems["pe"], dtf_pe)
                return e.tensor_copy(arf[:, A_CATM:A_CATM + 512], pb7[:, 0:512])
            emit("vector", [], dtdrain, inc="dve")
            catm_dve = now("dve")
            emit("vector", [],
                 lambda e: e.tensor_tensor(arf[:, A_ARG + 256:A_ARG + 384],
                                           arf[:, A_CATM + 128:A_CATM + 256],
                                           arf[:, A_CATM + 384:A_CATM + 512],
                                           ALU.subtract), inc="dve")
            nca_dve = now("dve")
            def etfill(e):
                return e.transpose(pb7[0:1, 0:64], etcol_sb[:], C["identf"][0:64, 0:64])
            emit("tensor", [("dve", catm_dve)], etfill, inc="pe")
            etf_pe = now("pe")
            def etdrain(e, etf_pe=etf_pe):
                e.wait_ge(sems["pe"], etf_pe)
                return e.tensor_copy(etrow_sb[0:1, :], pb7[0:1, 0:64])
            emit("vector", [], etdrain, inc="dve")
            emit("tensor", [("dve", now("dve"))],
                 lambda e: e.matmul(pb7[:, 64:128], C["onesr"][:], etrow_sb[0:1, :],
                                    start=True, stop=True, skip_group_check=True),
                 inc="pe")
            emit("vector", [("pe", now("pe"))],
                 lambda e: e.tensor_copy(arf[:, A_CATM + 512:A_CATM + 576],
                                         pb7[:, 64:128]), inc="dve")
            etb_dve = now("dve")
            pb7_last = etb_dve

            gmw = [("act", silu_act_ev[33]), ("act", now("act")), ("dve", now("dve"))]
            def gmm(e, cl=cl):
                e.matmul(pb[4][:, 0:256], bc_sb[:, cl * 256:cl * 256 + 128],
                         bc_sb[:, 512 + cl * 256:512 + (cl + 1) * 256],
                         start=True, stop=False, skip_group_check=True)
                return e.matmul(pb[4][:, 256:512],
                                bc_sb[:, cl * 256 + 128:cl * 256 + 256],
                                bc_sb[:, 512 + cl * 256:512 + (cl + 1) * 256],
                                start=False, stop=True, skip_group_check=True)
            emit("tensor", gmw, gmm, inc="pe")
            emit("vector", [("pe", now("pe"))],
                 lambda e: e.tensor_tensor(arf[:, A_GM0:A_GM0 + 256], pb[4][:, 0:256],
                                           C["mask0"][:], ALU.mult), inc="dve")
            emit("vector", [],
                 lambda e: e.tensor_tensor(arf[:, A_GM1:A_GM1 + 128], pb[4][:, 384:512],
                                           C["mask0"][:, 0:128], ALU.mult), inc="dve")
            gm_dve = now("dve")
            if span == 0 and cl == 0:
                dbg_tap("gm", arf[:, 0:384])

            if seq_start and cl == 0:
                emit("vector", [("pe", S["ssm_pe"])] if span else [],
                     lambda e: e.memset(rbf_sb[:], 0.0), inc="dve")
                for g_ in range(16):
                    rupd4[g_] = now("dve")

            def S_ap(slot):
                if slot < 2:
                    return ssb[:, slot * 384:(slot + 1) * 384]
                return arb[:, 11264 + (slot - 2) * 384: 11264 + (slot - 1) * 384]

            PD = 4
            E_ev = [0] * NH
            S_ev = [0] * NH
            ymm_ev = [0] * NH
            cab_pe_ev = [0] * 32
            t2_ev = [0] * 16
            y4_ev = [0] * 16
            st_pe_ev = [0] * 16
            gdrain = [0] * 32
            ytr_pend = []
            ytr_fill = [0] * 16
            car_base = car_k
            for g4 in range(2):
                cw = [("dve", ca_dve)]
                if car_k >= 2:
                    cw.append(("pe", car_free_pe[car_k - 2]))
                emit("gpsimd", cw,
                     lambda e, g4=g4, ca_fm=ca_fm: e.dma_start(
                         out=arf[0:1, A_CAR + (g4 % 2) * 1024:
                                 A_CAR + (g4 % 2) * 1024 + 1024],
                         in_=ca_fm[g4 * 4:(g4 + 1) * 4, :]),
                     inc="ldcr%d" % (car_k % 2), inc_by=16)
                car_k += 1
            for hh in range(NH + PD):
                if hh >= PD:
                    h = hh - PD
                    g = h // 4
                    hg = h % 4
                    gb = pb[2 + g % 2]
                    sap = S_ap(h % 4)
                    yw = [("pool", S_ev[h])]
                    if hg == 0:
                        if g >= 2:
                            yw.append(("dve", y4_ev[g - 2]))
                        else:
                            yw.append(("dve", ffn_dn_ev.get(2 + g % 2, 0)))
                            ga = gu_bank_act.get(2 + g % 2, 0)
                            if ga:
                                yw.append(("act", ga))
                            hb = S["hb2"] if g % 2 == 0 else S["hb3"]
                            if hb[1]:
                                yw.append(("dve", hb[1]))
                    tt0, tt1 = 2 * cl, 2 * cl + 1
                    def ydmms(e, gb=gb, sap=sap, hg=hg, h=h, tt0=tt0, tt1=tt1):
                        hc = h * 64
                        e.matmul(gb[:, hg * 128:hg * 128 + 64], sap[:, 0:128],
                                 xtm_sb[:, tt0 * INTER + hc:tt0 * INTER + hc + 64],
                                 start=(hg == 0), stop=False, skip_group_check=True)
                        e.matmul(gb[:, hg * 128 + 64:hg * 128 + 128],
                                 sap[:, 128:256],
                                 xtm_sb[:, tt0 * INTER + hc:tt0 * INTER + hc + 64],
                                 start=False, stop=False, skip_group_check=True)
                        return e.matmul(gb[:, hg * 128 + 64:hg * 128 + 128],
                                        sap[:, 256:384],
                                        xtm_sb[:, tt1 * INTER + hc:tt1 * INTER + hc + 64],
                                        start=False, stop=(hg == 3),
                                        skip_group_check=True)
                    emit("tensor", yw, ydmms, inc="pe")
                    ymm_ev[h] = now("pe")
                    if hg == 3:
                        for sh in range(2):
                            xw = [("dve", catm_dve)]
                            if g >= 2:
                                xw.append(("pe", st_pe_ev[g - 2]))
                            emit("vector", xw,
                                 lambda e, sh=sh, g=g, cl=cl: e.tensor_tensor(
                                     arb[:, 10240 + ((g % 2) * 2 + sh) * 256:
                                         10240 + ((g % 2) * 2 + sh) * 256 + 256]
                                     .rearrange("p (h w) -> p h w", h=4),
                                     xtm_sb[:, (2 * cl + sh) * INTER + g * 256:
                                            (2 * cl + sh) * INTER + g * 256 + 256]
                                     .rearrange("p (h w) -> p h w", h=4),
                                     arf[:, A_CATM + 256 + sh * 64 + 4 * g:
                                         A_CATM + 256 + sh * 64 + 4 * g + 4]
                                     .unsqueeze(2).broadcast_to([128, 4, 64]),
                                     ALU.mult), inc="dve")
                        xp_dve = now("dve")
                        yob = pb[g % 2]
                        ow = [("dve", rupd4[g])]
                        if g >= 2:
                            ow.append(("dve", t2_ev[g - 2]))
                        else:
                            ow.append(("dve", tbb.drain_of("pb0b" if g % 2 == 0
                                                           else "pb1b")))
                            yo0 = S.get("yo%d_dve" % (g % 2), 0)
                            if yo0:
                                ow.append(("dve", yo0))
                        def yomms(e, yob=yob, g=g, cl=cl):
                            e.matmul(yob[:, 0:256],
                                     bc_sb[:, 512 + cl * 256:512 + cl * 256 + 128],
                                     rbf_sb[:, g * 256:(g + 1) * 256],
                                     start=True, stop=True, skip_group_check=True)
                            return e.matmul(yob[:, 256:512],
                                            bc_sb[:, 512 + cl * 256 + 128:
                                                  512 + cl * 256 + 256],
                                            rbf_sb[:, g * 256:(g + 1) * 256],
                                            start=False, stop=True,
                                            skip_group_check=True)
                        emit("tensor", ow, yomms, inc="pe")
                        yo_pe = now("pe")
                        stw = [("dve", xp_dve)]
                        if g >= 1:
                            stw.append(("dve", rupd4[g - 1]))
                        else:
                            stw.append(("dve", max(S["pb5_dve"],
                                                   ffn_dn_ev.get(5, 0),
                                                   gu_bank_dve.get(5, 0))))
                        def stmms(e, g=g, cl=cl):
                            e.matmul(pb[5][:, 0:256],
                                     btm_sb[:, (2 * cl) * 128:(2 * cl) * 128 + 128],
                                     arb[:, 10240 + ((g % 2) * 2 + 0) * 256:
                                         10240 + ((g % 2) * 2 + 0) * 256 + 256],
                                     start=True, stop=False, skip_group_check=True)
                            return e.matmul(pb[5][:, 0:256],
                                            btm_sb[:, (2 * cl + 1) * 128:
                                                   (2 * cl + 1) * 128 + 128],
                                            arb[:, 10240 + ((g % 2) * 2 + 1) * 256:
                                                10240 + ((g % 2) * 2 + 1) * 256 + 256],
                                            start=False, stop=True,
                                            skip_group_check=True)
                        emit("tensor", stw, stmms, inc="pe")
                        st_pe_ev[g] = now("pe")
                        t2 = arf[:, A_T2:A_T2 + 256]
                        t2_3 = t2.rearrange("p (h w) -> p h w", h=4)
                        gslot = g % 2
                        for q in range(2):
                            ecarep = (arf[:, A_CATM + q * 64 + 4 * g:
                                          A_CATM + q * 64 + 4 * g + 4]
                                      .unsqueeze(2).broadcast_to([128, 4, 64]))
                            emit("vector", [("pe", yo_pe)] if q == 0 else [],
                                 lambda e, yob=yob, q=q, t2_3=t2_3, ecarep=ecarep:
                                 e.tensor_tensor(
                                     t2_3,
                                     yob[:, q * 256:(q + 1) * 256]
                                     .rearrange("p (h w) -> p h w", h=4),
                                     ecarep, ALU.mult), inc="dve")
                            emit("vector", [],
                                 lambda e, gb=gb, q=q, t2_3=t2_3: e.tensor_tensor(
                                     t2_3, t2_3,
                                     gb[:].rearrange("p (hg c) -> p hg c", hg=4)
                                     [:, :, q * 64:(q + 1) * 64], ALU.add), inc="dve")
                            ydst = (ybuf[:]
                                    .rearrange("p (s pair qq hip w) -> p s pair qq hip w",
                                               s=2, pair=2, qq=2, hip=2)
                                    [:, gslot, :, q, :, :])
                            drep = (C["DB"][:, 4 * g:4 * g + 4]
                                    .rearrange("p (pair hip) -> p pair hip", pair=2)
                                    .unsqueeze(3).broadcast_to([128, 2, 2, 64]))
                            t2_4 = t2.rearrange("p (pair hip w) -> p pair hip w",
                                                pair=2, hip=2)
                            emit("vector", [],
                                 lambda e, ydst=ydst, g=g, cl=cl, q=q, drep=drep:
                                 e.tensor_tensor(
                                     ydst,
                                     xtm_sb[:, (2 * cl + q) * INTER + g * 256:
                                            (2 * cl + q) * INTER + g * 256 + 256]
                                     .rearrange("p (pair hip w) -> p pair hip w",
                                                pair=2, hip=2),
                                     drep, ALU.mult), inc="dve")
                            emit("vector", [],
                                 lambda e, ydst=ydst, t2_4=t2_4: e.tensor_tensor(
                                     ydst, ydst, t2_4, ALU.add), inc="dve")
                        t2_ev[g] = now("dve")
                        y4_ev[g] = now("dve")
                        etrep = (arf[:, A_CATM + 512 + 4 * g:A_CATM + 512 + 4 * g + 4]
                                 .unsqueeze(2).broadcast_to([128, 4, 64]))
                        rb4 = rbf_sb[:, g * 256:(g + 1) * 256].rearrange(
                            "p (h w) -> p h w", h=4)
                        emit("vector", [("pe", st_pe_ev[g])],
                             lambda e, rb4=rb4, etrep=etrep: e.tensor_tensor(
                                 rb4, rb4, etrep, ALU.mult), inc="dve")
                        emit("vector", [],
                             lambda e, rb4=rb4: e.tensor_tensor(
                                 rb4, rb4,
                                 pb[5][:, 0:256].rearrange("p (h w) -> p h w", h=4),
                                 ALU.add), inc="dve")
                        rupd4[g] = now("dve")
                        for pr in range(2):
                            fc = 2 * g + pr
                            def evf(e, slot, fc=fc, cl=cl):
                                gsl = gate_sb[:, fc * SPAN + cl * 256:
                                              fc * SPAN + cl * 256 + 128]
                                return e.scalar_tensor_tensor(
                                    gsl, slot, C["nw"][:, fc:fc + 1], gsl,
                                    ALU.mult, ALU.mult)
                            def evf1(e, slot, fc=fc, cl=cl):
                                gsl = gate_sb[:, fc * SPAN + cl * 256 + 128:
                                              fc * SPAN + cl * 256 + 256]
                                return e.scalar_tensor_tensor(
                                    gsl, slot, C["nw"][:, fc:fc + 1], gsl,
                                    ALU.mult, ALU.mult)
                            ytr_pend.append((
                                ybuf[:, gslot * 512 + pr * 256:
                                     gslot * 512 + pr * 256 + 128],
                                C["identb"][:], None, evf))
                            ytr_pend.append((
                                ybuf[:, gslot * 512 + pr * 256 + 128:
                                     gslot * 512 + pr * 256 + 256],
                                C["identb"][:], None, evf1))
                        if g % 2 == 1:
                            _, pd_ = tbb.batch(ytr_pend,
                                               extra=[("dve", y4_ev[g])])
                            ytr_pend = []
                            for fcq in range(2 * g - 2, 2 * g + 2):
                                gdrain[fcq] = pd_
                if hh < NH:
                    h = hh
                    r = h % 2
                    g4 = h // 4
                    pr_ = h // 2
                    cabbank = pb[4] if pr_ % 2 == 0 else pb7
                    if r == 0:
                        cw = [("ldcr%d" % (g4 % 2),
                               ((car_base + g4) // 2 + 1) * 16)]
                        if pr_ >= 2:
                            cw.append(("act", E_ev[2 * (pr_ - 2) + 1]))
                        elif pr_ == 0:
                            cw.append(("dve", gm_dve))
                        else:
                            cw.append(("dve", etb_dve))
                        emit("tensor", cw,
                             lambda e, cabbank=cabbank, g4=g4, h=h: e.matmul(
                                 cabbank[:, 0:512], C["onesr"][:],
                                 arf[0:1, A_CAR + (g4 % 2) * 1024 + (h % 4) * 256:
                                     A_CAR + (g4 % 2) * 1024 + (h % 4) * 256 + 512],
                                 start=True, stop=True, skip_group_check=True),
                             inc="pe")
                        cab_pe_ev[pr_] = now("pe")
                        if h % 4 == 2:
                            car_free_pe.append(now("pe"))
                            if g4 + 2 < 16:
                                gn = g4 + 2
                                cw2 = [("dve", ca_dve)]
                                if car_k >= 2:
                                    cw2.append(("pe", car_free_pe[car_k - 2]))
                                emit("gpsimd", cw2,
                                     lambda e, gn=gn, ca_fm=ca_fm: e.dma_start(
                                         out=arf[0:1, A_CAR + (gn % 2) * 1024:
                                                 A_CAR + (gn % 2) * 1024 + 1024],
                                         in_=ca_fm[gn * 4:(gn + 1) * 4, :]),
                                     inc="ldcr%d" % (car_k % 2), inc_by=16)
                                car_k += 1
                    cab_pe = cab_pe_ev[pr_]
                    eset = A_E + r * 384
                    absl = arf[:, A_ARG + r * 128:A_ARG + r * 128 + 128]
                    nca0 = arf[:, A_ARG + 256 + h:A_ARG + 256 + h + 1]
                    nca1 = arf[:, A_ARG + 320 + h:A_ARG + 320 + h + 1]
                    bcol0 = arf[:, A_CATM + 128 + h:A_CATM + 128 + h + 1]
                    lnd0 = arf[:, A_CATM + 384 + h:A_CATM + 384 + h + 1]
                    lnd1 = arf[:, A_CATM + 448 + h:A_CATM + 448 + h + 1]
                    ew = [("pe", cab_pe)]
                    if h >= 2:
                        ew.append(("pool", S_ev[h - 2]))
                    else:
                        ew.append(("dve", nca_dve))
                    cbase = (h % 2) * 256
                    emit("scalar", ew,
                         lambda e, cabbank=cabbank, eset=eset, bcol0=bcol0, cbase=cbase:
                         e.activation(
                             arf[:, eset + 128:eset + 256],
                             cabbank[:, cbase + 128:cbase + 256],
                             AF.Exp, bias=bcol0), inc="act")
                    emit("scalar", [],
                         lambda e, cabbank=cabbank, absl=absl, nca0=nca0, cbase=cbase:
                         e.activation(
                             absl, cabbank[:, cbase:cbase + 128], AF.Abs, bias=nca0),
                         inc="act")
                    emit("scalar", [],
                         lambda e, absl=absl, eset=eset, lnd0=lnd0: e.activation(
                             arf[:, eset:eset + 128], absl, AF.Exp, scale=-1.0,
                             bias=lnd0), inc="act")
                    emit("scalar", [],
                         lambda e, cabbank=cabbank, absl=absl, nca1=nca1, cbase=cbase:
                         e.activation(
                             absl, cabbank[:, cbase + 128:cbase + 256], AF.Abs,
                             bias=nca1), inc="act")
                    emit("scalar", [],
                         lambda e, absl=absl, eset=eset, lnd1=lnd1: e.activation(
                             arf[:, eset + 256:eset + 384], absl, AF.Exp, scale=-1.0,
                             bias=lnd1), inc="act")
                    E_ev[h] = now("act")
                    sw = [("act", E_ev[h])]
                    if h >= PD:
                        sw.append(("pe", ymm_ev[h - PD]))
                    if h == 0:
                        sw.append(("dve", gm_dve))
                    sap = S_ap(h % 4)
                    emit("gpsimd", sw,
                         lambda e, sap=sap, eset=eset: e.tensor_tensor(
                             sap[:, 0:128], arf[:, A_GM0:A_GM0 + 128],
                             arf[:, eset:eset + 128], ALU.mult), inc="pool")
                    emit("gpsimd", [],
                         lambda e, sap=sap, eset=eset: e.tensor_tensor(
                             sap[:, 128:256], arf[:, A_GM0 + 128:A_GM0 + 256],
                             arf[:, eset + 128:eset + 256], ALU.mult), inc="pool")
                    emit("gpsimd", [],
                         lambda e, sap=sap, eset=eset: e.tensor_tensor(
                             sap[:, 256:384], arf[:, A_GM1:A_GM1 + 128],
                             arf[:, eset + 256:eset + 384], ALU.mult), inc="pool")
                    S_ev[h] = now("pool")
            rms_pl = [0] * 32
            racc = arf[:, A_SSX:A_SSX + 256]
            for fc in range(32):
                sqw = [("dve", gdrain[fc])]
                if fc == 2:
                    sqw.append(("dve", racc_init_dve))
                elif fc > 2:
                    sqw.append(("pool", rms_pl[fc - 2]))
                emit("scalar", sqw,
                     lambda e, fc=fc, cl=cl: e.activation(
                         arf[:, A_G2 + (fc % 2) * 256:A_G2 + (fc % 2) * 256 + 256],
                         gate_sb[:, fc * SPAN + cl * 256: fc * SPAN + (cl + 1) * 256],
                         AF.Square), inc="act")
                sqe = now("act")
                if fc == 0:
                    emit("vector", [("act", sqe), ("pe", ymm_ev[NH - 1])],
                         lambda e: e.tensor_copy(
                             racc, arf[:, A_G2:A_G2 + 256]), inc="dve")
                    racc_init_dve = now("dve")
                    rms_pl[fc] = 0
                else:
                    aw = [("act", sqe)]
                    if fc == 1:
                        aw.append(("dve", racc_init_dve))
                    emit("gpsimd", aw,
                         lambda e, fc=fc: e.tensor_tensor(
                             racc, racc,
                             arf[:, A_G2 + (fc % 2) * 256:A_G2 + (fc % 2) * 256 + 256],
                             ALU.add), inc="pool")
                    rms_pl[fc] = now("pool")
            emit("tensor", [("pool", now("pool")), ("dve", rupd4[15])],
                 lambda e: e.matmul(
                     pb[5][0:1, 256:512], C["onescol"][:],
                     arf[:, A_SSX:A_SSX + 256],
                     start=True, stop=True, skip_group_check=True), inc="pe")
            emit("vector", [("pe", now("pe"))],
                 lambda e: e.tensor_scalar(rstdrow_sb[0:1, :],
                                           pb[5][0:1, 256:512],
                                           1.0 / INTER, None, ALU.mult), inc="dve")
            S["pb5_dve"] = now("dve")
            emit("scalar", [("dve", now("dve"))],
                 lambda e: e.activation(rstdrow_sb[0:1, :], rstdrow_sb[0:1, :],
                                        AF.Ln, bias=C["cvals"][0:1, 0:1]), inc="act")
            emit("scalar", [],
                 lambda e: e.activation(rstdrow_sb[0:1, :], rstdrow_sb[0:1, :],
                                        AF.Exp, scale=-0.5), inc="act")
            emit("tensor", [("act", now("act")), ("act", E_ev[NH - 1]),
                            ("dve", pb7_last)],
                 lambda e: e.matmul(pb7[:, 0:256], C["onesr"][:], rstdrow_sb[0:1, :],
                                    start=True, stop=True, skip_group_check=True),
                 inc="pe")
            emit("vector", [("pe", now("pe"))],
                 lambda e, cl=cl: e.tensor_copy(
                     arf[:, A_RSTD + cl * 256:A_RSTD + (cl + 1) * 256], pb7[:, 0:256]),
                 inc="dve")
            pb7_last = now("dve")
            S["hb2"] = (0, y4_ev[14])
            S["hb3"] = (0, y4_ev[15])
            S["pb4_act"] = E_ev[NH - 2]
            S["yo0_dve"] = t2_ev[14]
            S["yo1_dve"] = t2_ev[15]
        S["ssm_pe"] = now("pe")
        rstd_dve = now("dve")
        S["arena_dve"] = rstd_dve
        if span == 0:
            dbg_tap("g", gate_sb[:])
            dbg_tap("rstd", arf[:, A_RSTD:A_RSTD + 512])

        tbb.set_banks(["pb6", "pb2b", "pb3b"],
                      new_guards={"pb2b": [("act", S["hb2"][0]), ("dve", S["hb2"][1])],
                                  "pb3b": [("act", S["hb3"][0]), ("dve", S["hb3"][1])]})
        op_bank_dve = {0: 0, 1: 0}
        otr_pend = []
        otr_fill = {}
        ytr_pb01 = [("dve", S["yo0_dve"]), ("dve", S["yo1_dve"])]
        for og in range(16):
            cw = []
            if span == 0:
                cw.append(("cvB%d" % (og % 2), (og // 2 + 1) * 16))
            if og < 2:
                cw.extend(r_wgu.freed[-2:])
            k = r_wout.load(lambda e, slot, og=og: e.dma_start(
                out=wsBb[:, slot * 4096: slot * 4096 + 4096],
                in_=d_w_out_b[og]), engine="sync", extra_waits=cw)
            bank = og % 2
            pw = [r_wout.use_wait(k), ("dve", rstd_dve), ("act", bank_evac[bank])]
            pw += [w for w in ytr_pb01 if w[1] > 0]
            if op_bank_dve[bank]:
                pw.append(("dve", op_bank_dve[bank]))
            obuf = (k % 2) * 4096
            def omms(e, obuf=obuf, bank=bank):
                last = None
                for kc in range(32):
                    last = e.matmul(
                        pb[bank][:],
                        wsBb[:, obuf + kc * 128: obuf + (kc + 1) * 128],
                        gate_sb[:, kc * SPAN: kc * SPAN + 512],
                        start=(kc == 0), stop=(kc == 31))
                return last
            emit("tensor", pw, omms, inc="pe")
            r_wout.free(("pe", now("pe")))
            o_pe = now("pe")
            while otr_pend:
                pcl, pog, pev2 = otr_pend.pop(0)
                fslp = arf[:, A_FFOUT + ((pog % 2) * 2 + pcl) * 256:
                           A_FFOUT + ((pog % 2) * 2 + pcl) * 256 + 256]
                tw2 = [("dve", pev2), ("dve", pb7_last)]
                def ofill(e, fslp=fslp):
                    e.transpose(pb7[:, 0:128], fslp[:, 0:128], C["identf"][:])
                    return e.transpose(pb7[:, 128:256], fslp[:, 128:256],
                                       C["identf"][:])
                emit("tensor", tw2, ofill, inc="pe")
                ofp = now("pe")
                def odrain(e, ofp=ofp, pcl=pcl, pog=pog):
                    e.wait_ge(sems["pe"], ofp)
                    last = None
                    for q in range(2):
                        tq = pcl * 2 + q
                        last = e.tensor_tensor(
                            out_sb[:, tq * H + pog * 128: tq * H + (pog + 1) * 128],
                            out_sb[:, tq * H + pog * 128: tq * H + (pog + 1) * 128],
                            pb7[:, q * 128:(q + 1) * 128], ALU.add)
                    return last
                emit("vector", [], odrain, inc="dve")
                pb7_last = now("dve")
                out_add_ev[pcl * 2] = pb7_last
                out_add_ev[pcl * 2 + 1] = pb7_last
            for cl in range(2):
                fsl = arf[:, A_FFOUT + ((og % 2) * 2 + cl) * 256:
                          A_FFOUT + ((og % 2) * 2 + cl) * 256 + 256]
                fw2 = [("pe", o_pe)]
                if og >= 2:
                    fw2.append(("pe", otr_fill.get((og - 2, cl), 0)))
                emit("vector", [w for w in fw2 if w[1] > 0],
                     lambda e, bank=bank, cl=cl, fsl=fsl: e.tensor_tensor(
                         fsl, pb[bank][:, cl * 256:(cl + 1) * 256],
                         arf[:, A_RSTD + cl * 256:A_RSTD + (cl + 1) * 256], ALU.mult),
                     inc="dve")
                emit("vector", [],
                     lambda e, og=og, fsl=fsl: e.tensor_scalar(
                         fsl, fsl, C["bout"][:, og:og + 1], None, ALU.add), inc="dve")
                otr_pend.append((cl, og, now("dve")))
            op_bank_dve[bank] = now("dve")
        while otr_pend:
            pcl, pog, pev2 = otr_pend.pop(0)
            fslp = arf[:, A_FFOUT + ((pog % 2) * 2 + pcl) * 256:
                       A_FFOUT + ((pog % 2) * 2 + pcl) * 256 + 256]
            tw2 = [("dve", pev2), ("dve", pb7_last)]
            def ofill(e, fslp=fslp):
                e.transpose(pb7[:, 0:128], fslp[:, 0:128], C["identf"][:])
                return e.transpose(pb7[:, 128:256], fslp[:, 128:256], C["identf"][:])
            emit("tensor", tw2, ofill, inc="pe")
            ofp = now("pe")
            def odrain(e, ofp=ofp, pcl=pcl, pog=pog):
                e.wait_ge(sems["pe"], ofp)
                last = None
                for q in range(2):
                    tq = pcl * 2 + q
                    last = e.tensor_tensor(
                        out_sb[:, tq * H + pog * 128: tq * H + (pog + 1) * 128],
                        out_sb[:, tq * H + pog * 128: tq * H + (pog + 1) * 128],
                        pb7[:, q * 128:(q + 1) * 128], ALU.add)
                return last
            emit("vector", [], odrain, inc="dve")
            pb7_last = now("dve")
            out_add_ev[pcl * 2] = pb7_last
            out_add_ev[pcl * 2 + 1] = pb7_last
        S["outproj_pe"] = now("pe")
        S["pb01_dve"] = max(op_bank_dve[0], op_bank_dve[1])
        if span == 0:
            dbg_tap("outsb", out_sb[:])

        for tq in range(4):
            emit("sync", [("dve", out_add_ev[tq])],
                 lambda e, tq=tq, row0=row0: e.dma_start(
                     out=dout[row0 + tq * 128: row0 + (tq + 1) * 128, :],
                     in_=out_sb[:, tq * H:(tq + 1) * H]), inc="st", inc_by=16)

    total_st = now("st")
    total_dbg = now("dbg")
    def fin(e):
        if total_dbg:
            e.wait_ge(sems["dbg"], total_dbg)
        e.wait_ge(sems["st"], total_st)
        return e.nop()
    emit("sync", [], fin)

    nops = {k: sum(1 for en, _ in ops if en == k)
            for k in ["tensor", "vector", "scalar", "gpsimd", "sync"]}
    print("op groups per engine:", nops)

    with nc.Block() as block:
        @block.tensor
        def _(e):
            for en, f in ops:
                if en == "tensor":
                    f(e)

        @block.vector
        def _(e):
            for en, f in ops:
                if en == "vector":
                    f(e)

        @block.scalar
        def _(e):
            for en, f in ops:
                if en == "scalar":
                    f(e)

        @block.gpsimd
        def _(e):
            for en, f in ops:
                if en == "gpsimd":
                    f(e)

        @block.sync
        def _(e):
            for en, f in ops:
                if en == "sync":
                    f(e)

    ctx.close()
    return nc


def kernel(**inputs):
    g = prep_host(inputs)
    x = _f32(inputs["L0_attn_hidden_states"])
    nc = build_program()
    in_maps = []
    for core in range(NCORES):
        m = dict(g)
        m["x"] = np.ascontiguousarray(
            x[core * SEQ_PER_CORE:(core + 1) * SEQ_PER_CORE].reshape(SEQ_PER_CORE * L, H))
        in_maps.append(m)
    res = run_bass_kernel_spmd(nc, in_maps, core_ids=list(range(NCORES)))
    out = np.stack([np.asarray(res.results[c]["out"]).reshape(SEQ_PER_CORE, L, H)
                    for c in range(NCORES)])
    return out.reshape(16, L, H)


if __name__ == "__main__":
    import reference as refmod
    ins = refmod.setup_inputs(0)
    o = kernel(**{k: np.asarray(v) for k, v in ins.items()})
    print(o.shape, o.dtype)
